# version 03
# baseline (speedup 1.0000x reference)
import numpy as np
from contextlib import ExitStack

import concourse.bass as bass
import concourse.mybir as mybir
from concourse.bass_utils import run_bass_kernel_spmd

F32 = mybir.dt.float32
F32R = mybir.dt.float32r
BF16 = mybir.dt.bfloat16
FP8 = mybir.dt.float8e4
AX = mybir.AxisListType.X
OP = mybir.AluOpType
AF = mybir.ActivationFunctionType

T = 1024
NTT = 8
D = 2048
ND = 16
INNER = 4096
NCT = 32
GRP = 4
NG = NCT // GRP
IF = 5632
NIT = 44
EPS = 1e-5

_CACHE = {}


def _stage_shared(inputs):
    g = lambda k: np.asarray(inputs[k], dtype=np.float32)
    s = {}
    s["x_full"] = g("L0_attn_hidden_states")
    s["w_in_T"] = np.ascontiguousarray(g("L0_attn_in_proj_w").T)
    s["w_gate_T"] = np.ascontiguousarray(g("L0_attn_gate_proj_w").T)
    s["w_out_T"] = np.ascontiguousarray(g("L0_attn_out_proj_w").T)
    s["w_g_nat"] = np.ascontiguousarray(g("L0_ffn_w_gate"))
    s["w_u_nat"] = np.ascontiguousarray(g("L0_ffn_w_up"))
    dn_idx = np.asarray(inputs["L0_ffn_down_reorder_index"]).astype(np.int64)
    s["w_d_eff"] = np.ascontiguousarray(g("L0_ffn_w_down")[:, np.argsort(dn_idx)])

    def chanmajor(a):
        return np.ascontiguousarray(a.reshape(NCT, 128).T)

    s["a_logt"] = chanmajor(g("L0_attn_A_log").reshape(-1))
    s["bp_t"] = chanmajor(g("L0_attn_Bp").reshape(-1))
    s["cp_t"] = chanmajor(g("L0_attn_Cp").reshape(-1))
    s["dp_t"] = chanmajor(g("L0_attn_Dp").reshape(-1))
    s["conv_bt"] = chanmajor(g("L0_attn_conv_b"))
    cw = g("L0_attn_conv_w")
    s["conv_wt"] = np.ascontiguousarray(
        cw.reshape(NCT, 128, 4).transpose(1, 0, 2).reshape(128, NCT * 4))
    dtw = g("L0_attn_dt_w")
    dtw_stat = np.zeros((128, NCT * 8), np.float32)
    sel_stat = np.zeros((8, GRP * 128), np.float32)
    for ct in range(NCT):
        j0 = 2 * (ct % GRP)
        dtw_stat[0:64, ct * 8 + j0] = dtw[2 * ct]
        dtw_stat[64:128, ct * 8 + j0 + 1] = dtw[2 * ct + 1]
    for kk in range(GRP):
        sel_stat[2 * kk, kk * 128:kk * 128 + 64] = 1.0
        sel_stat[2 * kk + 1, kk * 128 + 64:kk * 128 + 128] = 1.0
    s["dtw_stat"] = dtw_stat
    s["sel_stat"] = sel_stat
    s["dt_bt"] = np.ascontiguousarray(g("L0_attn_dt_b").reshape(NG, 8).T)
    s["ident_f"] = np.eye(128, dtype=np.float32)
    return s


def _stage_core(s, core):
    q = core % 4
    xb = s["x_full"][core // 4]
    if q == 0:
        xs = np.concatenate([np.zeros((3, D), np.float32), xb[0:1024]], 0)
    else:
        xs = xb[1024 * q - 3:1024 * (q + 1)]
    m = np.zeros((128, 4), np.float32)
    m[:, :q] = 1.0
    return dict(
        x_sl=np.ascontiguousarray(xs),
        w_in_T=s["w_in_T"], w_gate_T=s["w_gate_T"], w_out_T=s["w_out_T"],
        w_g_nat=s["w_g_nat"], w_u_nat=s["w_u_nat"], w_d_eff=s["w_d_eff"],
        a_logt=s["a_logt"], bp_t=s["bp_t"], cp_t=s["cp_t"], dp_t=s["dp_t"],
        conv_bt=s["conv_bt"], conv_wt=s["conv_wt"],
        dtw_stat=s["dtw_stat"], sel_stat=s["sel_stat"], dt_bt=s["dt_bt"],
        mask_m=np.ascontiguousarray(m),
        mask_1m=np.ascontiguousarray(1.0 - m),
        ident_f=s["ident_f"],
    )


def build(debug=(), upto="full", ng_lim=NG):
    nc = bass.Bass("TRN2", target_bir_lowering=False, debug=False,
                   detect_race_conditions=False)

    d_x = nc.dram_tensor("x_sl", [1027, D], F32, kind="ExternalInput").ap()
    d_win = nc.dram_tensor("w_in_T", [D, INNER], F32, kind="ExternalInput").ap()
    d_wgt = nc.dram_tensor("w_gate_T", [D, INNER], F32, kind="ExternalInput").ap()
    d_wout = nc.dram_tensor("w_out_T", [INNER, D], F32, kind="ExternalInput").ap()
    d_wg = nc.dram_tensor("w_g_nat", [IF, D], F32, kind="ExternalInput").ap()
    d_wu = nc.dram_tensor("w_u_nat", [IF, D], F32, kind="ExternalInput").ap()
    d_wd = nc.dram_tensor("w_d_eff", [D, IF], F32, kind="ExternalInput").ap()
    d_alog = nc.dram_tensor("a_logt", [128, NCT], F32, kind="ExternalInput").ap()
    d_bp = nc.dram_tensor("bp_t", [128, NCT], F32, kind="ExternalInput").ap()
    d_cp = nc.dram_tensor("cp_t", [128, NCT], F32, kind="ExternalInput").ap()
    d_dp = nc.dram_tensor("dp_t", [128, NCT], F32, kind="ExternalInput").ap()
    d_cb = nc.dram_tensor("conv_bt", [128, NCT], F32, kind="ExternalInput").ap()
    d_cw = nc.dram_tensor("conv_wt", [128, NCT * 4], F32, kind="ExternalInput").ap()
    d_dtw = nc.dram_tensor("dtw_stat", [128, NCT * 8], F32, kind="ExternalInput").ap()
    d_sel = nc.dram_tensor("sel_stat", [8, GRP * 128], F32, kind="ExternalInput").ap()
    d_dtb = nc.dram_tensor("dt_bt", [8, NG], F32, kind="ExternalInput").ap()
    d_mm_ = nc.dram_tensor("mask_m", [128, 4], F32, kind="ExternalInput").ap()
    d_m1 = nc.dram_tensor("mask_1m", [128, 4], F32, kind="ExternalInput").ap()
    d_idf = nc.dram_tensor("ident_f", [128, 128], F32, kind="ExternalInput").ap()

    d_out = nc.dram_tensor("out", [T, D], F32, kind="ExternalOutput").ap()

    d_ag_in = nc.dram_tensor("ag_in", [NG, 8 * 128], F32).ap()
    d_ag_out = nc.dram_tensor("ag_out", [NG, 4 * 8 * 128], F32).ap()
    d_r1 = nc.dram_tensor("r1_sp", [T, D], F32).ap()
    d_x2 = nc.dram_tensor("x2_sp", [T, D], F32).ap()
    d_r2 = nc.dram_tensor("r2_sp", [T, D], F32).ap()
    d_t = nc.dram_tensor("t_sp", [T, IF], F32).ap()
    d_qdq = nc.dram_tensor("qdq_sp", [T, IF], BF16).ap()
    d_wgq = nc.dram_tensor("wgq_sp", [IF, D], BF16).ap()
    d_wuq = nc.dram_tensor("wuq_sp", [IF, D], BF16).ap()
    d_wdq = nc.dram_tensor("wdq_sp", [D, IF], BF16).ap()

    def mkdbg(name, shape, dtype=F32):
        if name in debug:
            return nc.dram_tensor("dbg_" + name, shape, dtype,
                                  kind="ExternalOutput").ap()
        return None

    db_decay = mkdbg("decay", [NCT * 128, T])
    db_states = mkdbg("states", [NCT * 128, T])
    db_dt = mkdbg("dt", [8, NG * T])
    db_comb = mkdbg("combined", [NCT * 128, T])
    db_r1 = mkdbg("r1", [T, D])
    db_x2 = mkdbg("x2", [T, D])
    db_qxdq = mkdbg("qxdq", [T, D])
    db_cols = mkdbg("lncols", [NTT * 128, 16])
    db_t = mkdbg("ffn_t", [T, IF])
    db_wgq = mkdbg("wgq", [IF, D], BF16)
    db_qtdq = mkdbg("qtdq", [T, IF])

    C, S = {}, {}

    def inc(instr, name, n=1):
        C[name] += n
        instr.then_inc(S[name], n)

    def wall(eng, names):
        for n in names:
            if C[n] > 0:
                eng.wait_ge(S[n], C[n])

    ALLSEM = (["mm", "v", "sc", "par", "sdbg",
               "sx0", "sx1", "sw0", "sw1", "agi", "ago",
               "swo0", "swo1", "sxr0", "sxr1", "r1w", "r1r0", "r1r1", "x2w",
               "sx2c0", "sx2c1", "wq0", "wq1", "wgqw", "wuqw", "wdqw",
               "swm0", "swm1", "tw", "tr0", "tr1", "qdqw",
               "qdT", "wdm0", "wdm1", "r2w", "r2r0", "r2r1", "outw", "gcq", "ad0", "ad1"]
              + ["cc%d" % i for i in range(NG)])

    ctx = ExitStack()
    with ctx:
        e = ctx.enter_context
        for n in ALLSEM:
            S[n] = e(nc.semaphore(n))
            C[n] = 0

        blk = e(nc.Block())

        class W:
            def __init__(self, eng):
                object.__setattr__(self, "_e", eng)
                object.__setattr__(self, "last", None)

            def __getattr__(self, n):
                f = getattr(self._e, n)
                if callable(f):
                    def g(*a, **k):
                        r = f(*a, **k)
                        object.__setattr__(self, "last", r)
                        return r
                    return g
                return f

        def vlink(eng, sem):
            inc(eng.last, sem, 1)
            eng.wait_ge(S[sem], C[sem])

        def SEC(name):
            def deco(f):
                def wrapper(eng):
                    return f(W(eng))
                getattr(blk, name)(wrapper)
                return None
            return deco

        scope_x = ExitStack()
        scope_c = ExitStack()
        scope_q = ExitStack()
        comb = scope_c.enter_context(
            nc.sbuf_tensor("comb", [128, NCT, T], BF16, side="right"))
        xbf = scope_x.enter_context(
            nc.sbuf_tensor("xbf", [128, ND, 1028], BF16, side="right"))
        p_alog = e(nc.sbuf_tensor("p_alog", [128, NCT], F32))
        p_negA = e(nc.sbuf_tensor("p_negA", [128, NCT], F32))
        p_bp = e(nc.sbuf_tensor("p_bp", [128, NCT], F32))
        p_cp = e(nc.sbuf_tensor("p_cp", [128, NCT], F32))
        p_dp = e(nc.sbuf_tensor("p_dp", [128, NCT], F32))
        p_cb = e(nc.sbuf_tensor("p_cb", [128, NCT], F32))
        p_cw = e(nc.sbuf_tensor("p_cw", [128, NCT * 4], F32))
        p_dtw = e(nc.sbuf_tensor("p_dtw", [128, NCT * 8], BF16))
        p_sel = e(nc.sbuf_tensor("p_sel", [8, GRP * 128], F32R))
        p_dtb = e(nc.sbuf_tensor("p_dtb", [8, NG], F32))
        p_mm = e(nc.sbuf_tensor("p_mm", [128, 4], F32))
        p_m1 = e(nc.sbuf_tensor("p_m1", [128, 4], F32))
        p_idf = e(nc.sbuf_tensor("p_idf", [128, 128], F32))
        p_idb = e(nc.sbuf_tensor("p_idb", [128, 128], BF16))
        cols = e(nc.sbuf_tensor("cols", [128, 64], F32))
        rmac = e(nc.sbuf_tensor("rmac", [128, NTT], F32))

        @SEC('gpsimd')
        def _(g):
            for dst, src in [(p_alog, d_alog), (p_bp, d_bp), (p_cp, d_cp),
                             (p_dp, d_dp), (p_cb, d_cb), (p_cw, d_cw),
                             (p_dtw, d_dtw), (p_sel, d_sel), (p_dtb, d_dtb),
                             (p_mm, d_mm_), (p_m1, d_m1), (p_idf, d_idf),
                             (p_idb, d_idf)]:
                inc(g.dma_start(out=dst[:, :], in_=src), "par", 16)

        @SEC('scalar')
        def _(s):
            wall(s, ["par"])
            s.activation(p_negA[:, :], p_alog[:, :], AF.Exp)
            inc(s.last, "sc", 1)

        @SEC('vector')
        def _(v):
            wall(v, ["sc"])
            v.tensor_scalar(p_negA[:, :], p_negA[:, :], -1.0, None, OP.mult)
            inc(v.last, "v", 1)

        with ExitStack() as c0:
            xtok = c0.enter_context(nc.sbuf_tensor("xtok", [128, 2, D], F32))
            ps_tr = c0.enter_context(nc.psum_tensor("ps_tr", [128, 512], F32))

            xcons_mark = {0: 0, 1: 0}
            for tt in range(9):
                rows = 128 if tt < 8 else 3
                buf = tt % 2
                col0 = tt * 128
                sxn = "sx%d" % buf

                @SEC('sync')
                def _(sy, tt=tt, rows=rows, buf=buf, sxn=sxn):
                    sy.wait_ge(S["mm"], xcons_mark[buf])
                    inc(sy.dma_start(out=xtok[0:rows, buf, :],
                                     in_=d_x[tt * 128:tt * 128 + rows, :]),
                        sxn, 16)
                sd_mark = C[sxn]

                for d4 in range(4):
                    @SEC('tensor')
                    def _(t, rows=rows, buf=buf, d4=d4, sd_mark=sd_mark,
                          sxn=sxn):
                        t.wait_ge(S[sxn], sd_mark)
                        t.wait_ge(S["v"], C["v"])
                        last = None
                        for k in range(4):
                            di = d4 * 4 + k
                            last = t.transpose(
                                ps_tr[:, k * 128:k * 128 + rows],
                                xtok[0:rows, buf, di * 128:(di + 1) * 128],
                                p_idf[0:rows, 0:rows])
                        inc(last, "mm", 1)
                    mm_mark = C["mm"]

                    @SEC('vector')
                    def _(v, rows=rows, d4=d4, col0=col0, mm_mark=mm_mark):
                        v.wait_ge(S["mm"], mm_mark)
                        src = ps_tr[:, 0:512].rearrange(
                            "p (a b) -> p a b", a=4)[:, :, 0:rows]
                        v.tensor_copy(xbf[:, d4 * 4:(d4 + 1) * 4,
                                          col0:col0 + rows], src)
                        inc(v.last, "v", 1)
                xcons_mark[buf] = C["mm"]

        pa = ExitStack()
        with pa:
            ea = pa.enter_context
            projX = ea(nc.sbuf_tensor("projX", [128, 2, GRP, 1028], BF16))
            gateb = ea(nc.sbuf_tensor("gateb", [128, 2, GRP, T], BF16))
            win = ea(nc.sbuf_tensor("win", [128, 2, ND * 128], BF16))
            wgt = ea(nc.sbuf_tensor("wgt", [128, 2, ND * 128], BF16))
            scr = ea(nc.sbuf_tensor("scr", [128, 6, T], F32))
            tmpv = ea(nc.sbuf_tensor("tmpv", [128, T], F32))
            dtf = ea(nc.sbuf_tensor("dtf", [8, T], F32R))
            cgb = ea(nc.sbuf_tensor("cgb", [128, 2, GRP, T], BF16))
            agb = ea(nc.sbuf_tensor("agb", [128, 8], F32))
            agot = ea(nc.sbuf_tensor("agot", [128, 32], F32))
            sall = ea(nc.sbuf_tensor("sall", [128, GRP], F32))
            zer = ea(nc.sbuf_tensor("zer", [128, T], F32))
            ps_p = ea(nc.psum_tensor("ps_p", [128, 2048], F32))
            ps_dt = ea(nc.psum_tensor("ps_dt", [8, T], F32))
            ps_bc = ea(nc.psum_tensor("ps_bc", [128, T], F32))

            @SEC('vector')
            def _(v):
                v.memset(zer[:, :], 0.0)
                if ng_lim < NG:
                    for ct in range(NCT):
                        v.memset(comb[:, ct, :], 0.0)
                inc(v.last, "v", 1)

            wload_mark = {}

            def emit_w_load(g, ct):
                buf = ct % 2
                sw = "sw%d" % buf
                for wt, dd in ((win, d_win), (wgt, d_wgt)):
                    inc(g.dma_start(
                        out=wt[:, buf, :].rearrange("p (a b) -> p a b", a=ND),
                        in_=dd[:, ct * 128:(ct + 1) * 128].rearrange(
                            "(a p) c -> p a c", p=128)), sw, 16)
                wload_mark[ct] = C[sw]

            @SEC('gpsimd')
            def _(g):
                wall(g, ["v"])
                emit_w_load(g, 0)
                emit_w_load(g, 1)

            evA = {0: 0, 1: 0, 2: 0}
            evac_mark = {"dt": 0, "dtv": 0}
            pend_ag = []
            corr_marks = {}
            prev_scan_mark = 0
            prev_drv_mark = 0
            prev_dcy_mark = 0

            for it in range(ng_lim + 1):
                if it < ng_lim:
                    for k in range(GRP):
                        ct = it * GRP + k
                        wbuf = ct % 2
                        swn = "sw%d" % wbuf

                        @SEC('tensor')
                        def _(t, ct=ct, wbuf=wbuf, swn=swn):
                            t.wait_ge(S[swn], wload_mark[ct])
                            if ct == 0:
                                wall(t, ["v"])
                            t.wait_ge(S["v"], evA[0])
                            for di in range(ND):
                                w = win[:, wbuf, di * 128:(di + 1) * 128]
                                t.matmul(ps_p[:, 0:512], w, xbf[:, di, 0:512],
                                         start=di == 0, stop=di == ND - 1)
                            inc(t.last, "mm", 1)
                        m_p0 = C["mm"]

                        @SEC('tensor')
                        def _(t, ct=ct, wbuf=wbuf):
                            t.wait_ge(S["v"], evA[1])
                            for di in range(ND):
                                w = win[:, wbuf, di * 128:(di + 1) * 128]
                                t.matmul(ps_p[:, 512:1024], w,
                                         xbf[:, di, 512:1024],
                                         start=di == 0, stop=di == ND - 1)
                            inc(t.last, "mm", 1)
                        m_p1 = C["mm"]

                        @SEC('tensor')
                        def _(t, ct=ct, wbuf=wbuf):
                            t.wait_ge(S["sc"], evA[2])
                            for di in range(ND):
                                w = wgt[:, wbuf, di * 128:(di + 1) * 128]
                                t.matmul(ps_p[:, 1024:1536], w,
                                         xbf[:, di, 3:515],
                                         start=di == 0, stop=di == ND - 1)
                            for di in range(ND):
                                w = wgt[:, wbuf, di * 128:(di + 1) * 128]
                                t.matmul(ps_p[:, 1536:2048], w,
                                         xbf[:, di, 515:1027],
                                         start=di == 0, stop=di == ND - 1)
                            inc(t.last, "mm", 1)
                        m_g = C["mm"]

                        @SEC('vector')
                        def _(v, it=it, k=k, m_p0=m_p0, m_p1=m_p1):
                            v.wait_ge(S["mm"], m_p1)
                            v.tensor_copy(projX[:, it % 2, k, 0:512],
                                          ps_p[:, 0:512])
                            inc(v.last, "v", 1)
                            v.tensor_copy(projX[:, it % 2, k, 512:1024],
                                          ps_p[:, 512:1024])
                            inc(v.last, "v", 1)
                        evA[0] = C["v"] - 1
                        evA[1] = C["v"]
                        ev_halo_pre = C["v"] - 1

                        @SEC('tensor')
                        def _(t, ct=ct, wbuf=wbuf, ev=ev_halo_pre):
                            t.wait_ge(S["v"], ev)
                            for di in range(ND):
                                w = win[:, wbuf, di * 128:(di + 1) * 128]
                                t.matmul(ps_p[:, 0:3], w,
                                         xbf[:, di, 1024:1027],
                                         start=di == 0, stop=di == ND - 1)
                            inc(t.last, "mm", 1)
                        m_ph = C["mm"]

                        @SEC('vector')
                        def _(v, it=it, k=k, m_ph=m_ph):
                            v.wait_ge(S["mm"], m_ph)
                            v.tensor_copy(projX[:, it % 2, k, 1024:1027],
                                          ps_p[:, 0:3])
                            inc(v.last, "v", 1)
                        evA[0] = C["v"]
                        proj_done = C["v"]

                        @SEC('scalar')
                        def _(s, it=it, k=k, m_g=m_g):
                            s.wait_ge(S["mm"], m_g)
                            s.activation(gateb[:, it % 2, k, 0:1024],
                                         ps_p[:, 1024:2048], AF.Sigmoid)
                            inc(s.last, "sc", 1)
                        evA[2] = C["sc"]

                        if ct + 2 < ng_lim * GRP:
                            @SEC('gpsimd')
                            def _(g, ct=ct):
                                g.wait_ge(S["mm"], C["mm"])
                                emit_w_load(g, ct + 2)

                        @SEC('vector')
                        def _(v, it=it, ct=ct, k=k, proj_done=proj_done):
                            v.wait_ge(S["v"], proj_done)
                            pj = projX[:, it % 2, k, :]
                            cwc = lambda j: p_cw[:, ct * 4 + j:ct * 4 + j + 1]
                            a0 = scr[:, k % 2, :]
                            a1 = scr[:, 2 + k % 2, :]
                            v.tensor_scalar(a0, pj[:, 0:1024], cwc(0),
                                            p_cb[:, ct:ct + 1], OP.mult,
                                            OP.add)
                            v.scalar_tensor_tensor(a1, pj[:, 1:1025], cwc(1),
                                                   a0, OP.mult, OP.add)
                            v.scalar_tensor_tensor(a0, pj[:, 2:1026], cwc(2),
                                                   a1, OP.mult, OP.add)
                            v.scalar_tensor_tensor(a1, pj[:, 3:1027], cwc(3),
                                                   a0, OP.mult, OP.add)
                            inc(v.last, "v", 1)
                        conv_mark = C["v"]

                        @SEC('scalar')
                        def _(s, it=it, k=k, conv_mark=conv_mark):
                            s.wait_ge(S["v"], conv_mark)
                            s.activation(projX[:, it % 2, k, 0:1024],
                                         scr[:, 2 + k % 2, :], AF.Silu)
                            inc(s.last, "sc", 1)
                        x_mark = C["sc"]

                        @SEC('tensor')
                        def _(t, it=it, ct=ct, k=k, x_mark=x_mark):
                            t.wait_ge(S["sc"], x_mark)
                            if k == 0:
                                t.wait_ge(S["sc"], evac_mark["dt"])
                                t.wait_ge(S["v"], evac_mark["dtv"])
                            Xap = projX[:, it % 2, k, 0:1024]
                            t.matmul(ps_dt[:, 0:512],
                                     p_dtw[:, ct * 8:(ct + 1) * 8],
                                     Xap[:, 0:512],
                                     start=k == 0, stop=k == GRP - 1)
                            t.matmul(ps_dt[:, 512:1024],
                                     p_dtw[:, ct * 8:(ct + 1) * 8],
                                     Xap[:, 512:1024],
                                     start=k == 0, stop=k == GRP - 1)
                            inc(t.last, "mm", 1)
                    dtmm_mark = C["mm"]

                    @SEC('scalar')
                    def _(s, it=it, dtmm_mark=dtmm_mark):
                        s.wait_ge(S["mm"], dtmm_mark)
                        s.wait_ge(S["v"], C["v"])
                        az = scr[0:8, 5, :]
                        s.activation(az, ps_dt[:, :], AF.Abs,
                                     bias=p_dtb[:, it:it + 1])
                        s.activation(az, az, AF.Exp, scale=-1.0)
                        s.activation(az, az, AF.Ln, bias=1.0)
                        inc(s.last, "sc", 1)
                    sp_mark = C["sc"]
                    evac_mark["dt"] = C["sc"]

                    @SEC('vector')
                    def _(v, it=it, dtmm_mark=dtmm_mark, sp_mark=sp_mark):
                        v.wait_ge(S["mm"], dtmm_mark)
                        v.tensor_scalar(tmpv[0:8, :], ps_dt[:, :],
                                        p_dtb[:, it:it + 1], 0.0,
                                        OP.add, OP.max)
                        v.wait_ge(S["sc"], sp_mark)
                        v.tensor_add(dtf[:, :], tmpv[0:8, :], scr[0:8, 5, :])
                        inc(v.last, "v", 1)
                    dtv_mark = C["v"]
                    evac_mark["dtv"] = C["v"]

                    if db_dt is not None:
                        @SEC('gpsimd')
                        def _(g, it=it, dtv_mark=dtv_mark):
                            g.wait_ge(S["v"], dtv_mark)
                            inc(g.dma_start(out=db_dt[:, it * T:(it + 1) * T],
                                            in_=dtf[:, :]), "sdbg", 16)
                            wall(g, ["sdbg"])

                    for k in range(GRP):
                        ct = it * GRP + k

                        @SEC('tensor')
                        def _(t, k=k, dtv_mark=dtv_mark,
                              psm=prev_drv_mark, pdm=prev_dcy_mark):
                            t.wait_ge(S["v"], dtv_mark)
                            t.wait_ge(S["sc"], pdm)
                            t.wait_ge(S["v"], psm)
                            t.matmul(ps_bc[:, 0:512],
                                     p_sel[:, k * 128:(k + 1) * 128],
                                     dtf[:, 0:512], start=True, stop=True)
                            t.matmul(ps_bc[:, 512:1024],
                                     p_sel[:, k * 128:(k + 1) * 128],
                                     dtf[:, 512:1024], start=True, stop=True)
                            inc(t.last, "mm", 1)
                        bc_mark = C["mm"]

                        @SEC('scalar')
                        def _(s, ct=ct, bc_mark=bc_mark, psm=prev_scan_mark):
                            s.wait_ge(S["mm"], bc_mark)
                            s.wait_ge(S["v"], psm)
                            s.activation(scr[:, 0, :], ps_bc[:, :], AF.Exp,
                                         scale=p_negA[:, ct:ct + 1])
                            inc(s.last, "sc", 1)
                        dcy_mark = C["sc"]
                        prev_dcy_mark = dcy_mark

                        @SEC('vector')
                        def _(v, it=it, ct=ct, k=k, dcy_mark=dcy_mark):
                            v.wait_ge(S["sc"], dcy_mark)
                            if k == 0:
                                v.wait_ge(S["agi"], C["agi"])
                            X = projX[:, it % 2, k, 0:1024]
                            gt = gateb[:, it % 2, k, :]
                            dcy, drv = scr[:, 0, :], scr[:, 1, :]
                            stt, cdc = scr[:, 2, :], scr[:, 3, :]
                            v.scalar_tensor_tensor(drv, ps_bc[:, :],
                                                   p_bp[:, ct:ct + 1], X,
                                                   OP.mult, OP.mult)
                            inc(v.last, "v", 1)
                            v.tensor_tensor_scan(stt, dcy, drv, 0.0,
                                                 OP.mult, OP.add)
                            v.tensor_tensor_scan(cdc, dcy, zer[:, :], 1.0,
                                                 OP.mult, OP.add)
                            v.tensor_scalar(tmpv[:, :], X, p_dp[:, ct:ct + 1],
                                            None, OP.mult)
                            v.scalar_tensor_tensor(tmpv[:, :], stt,
                                                   p_cp[:, ct:ct + 1],
                                                   tmpv[:, :], OP.mult,
                                                   OP.add)
                            v.tensor_mul(comb[:, ct, :], tmpv[:, :], gt)
                            v.scalar_tensor_tensor(cgb[:, it % 2, k, :], cdc,
                                                   p_cp[:, ct:ct + 1], gt,
                                                   OP.mult, OP.mult)
                            v.tensor_copy(agb[:, k:k + 1], cdc[:, T - 1:T])
                            v.tensor_copy(agb[:, GRP + k:GRP + k + 1],
                                          stt[:, T - 1:T])
                            inc(v.last, "v", 1)
                        prev_scan_mark = C["v"]
                        prev_drv_mark = C["v"] - 1

                        if db_decay is not None:
                            @SEC('gpsimd')
                            def _(g, ct=ct, sm=prev_scan_mark):
                                g.wait_ge(S["v"], sm)
                                inc(g.dma_start(
                                    out=db_decay[ct * 128:(ct + 1) * 128, :],
                                    in_=scr[:, 0, :]), "sdbg", 16)
                                inc(g.dma_start(
                                    out=db_states[ct * 128:(ct + 1) * 128, :],
                                    in_=scr[:, 2, :]), "sdbg", 16)
                                wall(g, ["sdbg"])

                    @SEC('gpsimd')
                    def _(g, it=it, sm=prev_scan_mark):
                        g.wait_ge(S["v"], sm)
                        inc(g.dma_start(out=d_ag_in[it, :].rearrange(
                            "(p j) -> p j", p=128), in_=agb[:, :]), "agi", 16)
                        g.wait_ge(S["agi"], C["agi"])
                        cc = g.collective_compute(
                            "AllGather", OP.bypass,
                            replica_groups=[[0, 1, 2, 3], [4, 5, 6, 7]],
                            ins=[d_ag_in[it, :]],
                            outs=[d_ag_out[it, :]])
                        inc(cc, "cc%d" % it, 1)

                    @SEC('sync')
                    def _(sy, it=it):
                        sy.wait_ge(S["cc%d" % it], 1)
                        sy.wait_ge(S["v"], corr_marks.get(it - 1, 0))
                        inc(sy.dma_start(out=agot[:, :].rearrange(
                            "p (r j) -> p r j", r=4),
                            in_=d_ag_out[it, :].rearrange(
                                "(r p j) -> p r j", r=4, p=128)), "ago", 16)
                    pend_ag.append((it, C["ago"]))

                while pend_ag and (len(pend_ag) > 1 or it == ng_lim):
                    git, mark = pend_ag.pop(0)

                    @SEC('vector')
                    def _(v, git=git, mark=mark):
                        v.wait_ge(S["ago"], mark)
                        v.memset(sall[:, :], 0.0)
                        vlink(v, "v")
                        for r in range(4):
                            dcol = agot[:, r * 8:r * 8 + GRP]
                            fcol = agot[:, r * 8 + GRP:r * 8 + 8]
                            v.tensor_scalar(tmpv[:, 0:GRP], dcol,
                                            p_mm[:, r:r + 1],
                                            p_m1[:, r:r + 1],
                                            OP.mult, OP.add)
                            v.tensor_scalar(tmpv[:, GRP:2 * GRP], fcol,
                                            p_mm[:, r:r + 1], None, OP.mult)
                            vlink(v, "v")
                            v.tensor_mul(sall[:, :], sall[:, :],
                                         tmpv[:, 0:GRP])
                            vlink(v, "v")
                            v.tensor_add(sall[:, :], sall[:, :],
                                         tmpv[:, GRP:2 * GRP])
                            vlink(v, "v")
                        for k in range(GRP):
                            ct = git * GRP + k
                            v.scalar_tensor_tensor(
                                comb[:, ct, :], cgb[:, git % 2, k, :],
                                sall[:, k:k + 1], comb[:, ct, :],
                                OP.mult, OP.add)
                        inc(v.last, "v", 1)
                    corr_marks[git] = C["v"]

                    if db_comb is not None:
                        @SEC('gpsimd')
                        def _(g, git=git, cm=corr_marks[git]):
                            g.wait_ge(S["v"], cm)
                            for k in range(GRP):
                                ct = git * GRP + k
                                inc(g.dma_start(
                                    out=db_comb[ct * 128:(ct + 1) * 128, :],
                                    in_=comb[:, ct, :]), "sdbg", 16)
                            wall(g, ["sdbg"])

        @SEC('tensor')
        def _(t):
            wall(t, ["v", "sc"])
            inc(t.last, "mm", 1)
        barA = C["mm"]
        scope_x.close()
        scope_w = ExitStack()
        wslg = scope_w.enter_context(
            nc.sbuf_tensor("wslg", [128, 2, 2048], F32, side="right"))
        wq8g = scope_w.enter_context(
            nc.sbuf_tensor("wq8g", [128, 2048], FP8, side="right"))
        wdqg = scope_w.enter_context(
            nc.sbuf_tensor("wdqg", [128, 2048], BF16, side="right"))

        if upto == "A":
            with ExitStack() as pt:
                ot = pt.enter_context(nc.sbuf_tensor("ot", [128, T], F32))

                @SEC('gpsimd')
                def _(g):
                    g.wait_ge(S["mm"], barA)
                    for ct in range(min(ng_lim * GRP, 8)):
                        inc(g.dma_start(out=ot[:, :], in_=comb[:, ct, :]),
                            "outw", 16)
                        g.wait_ge(S["outw"], C["outw"])
                        inc(g.dma_start(
                            out=d_out[(ct % 8) * 128:(ct % 8) * 128 + 128,
                                      0:T], in_=ot[:, :]), "outw", 16)
                    g.wait_ge(S["outw"], C["outw"])
            return nc

        qxT = scope_q.enter_context(
            nc.sbuf_tensor("qxT", [128, ND, T], BF16))
        pb = ExitStack()
        with pb:
            eb = pb.enter_context
            woutb = eb(nc.sbuf_tensor("woutb", [128, 2, NCT * 256], BF16))
            xres = eb(nc.sbuf_tensor("xres", [128, 2, 256], F32))
            r1c = eb(nc.sbuf_tensor("r1c", [128, 2, 256], F32))
            r1t = eb(nc.sbuf_tensor("r1t", [128, 2, D], F32))
            x2t = eb(nc.sbuf_tensor("x2t", [128, 2, D], F32))
            qf8x = eb(nc.sbuf_tensor("qf8x", [128, D], FP8))
            qxdq = eb(nc.sbuf_tensor("qxdq", [128, D], BF16))
            sqs = eb(nc.sbuf_tensor("sqs", [128, D], F32))
            ps_o = eb(nc.psum_tensor("ps_o", [128, 1024], F32))
            ps_qt = eb(nc.psum_tensor("ps_qt", [128, 512], BF16))

            wgu_slabs = [(d_wg, d_wgq, "wgqw", sl) for sl in range(NIT)] + \
                        [(d_wu, d_wuq, "wuqw", sl) for sl in range(NIT)]
            wgu_state = {"i": 0, "cons": {0: 0, 1: 0}}

            def emit_wgu(n):
                for _ in range(n):
                    if wgu_state["i"] >= len(wgu_slabs):
                        return
                    idx = wgu_state["i"]
                    wgu_state["i"] += 1
                    srcw, dstw, dsem, sl = wgu_slabs[idx]
                    pk = idx % 2
                    wqs = "wq%d" % pk
                    cb = 20 + 4 * pk

                    @SEC('sync')
                    def _(sy, srcw=srcw, sl=sl, pk=pk, wqs=wqs, idx=idx):
                        sy.wait_ge(S["v"], wgu_state["cons"][pk])
                        inc(sy.dma_start(out=wslg[:, pk, :],
                                         in_=srcw[sl * 128:(sl + 1) * 128, :]),
                            wqs, 16)
                    slm = C[wqs]

                    @SEC('vector')
                    def _(v, pk=pk, slm=slm, wqs=wqs, cb=cb):
                        v.wait_ge(S[wqs], slm)
                        v.reduce_max(cols[:, cb:cb + 1], wslg[:, pk, :], AX,
                                     apply_absolute_value=True)
                        vlink(v, "v")
                        v.reciprocal(cols[:, cb + 1:cb + 2],
                                     cols[:, cb:cb + 1])
                        vlink(v, "v")
                        v.tensor_scalar(cols[:, cb + 2:cb + 3],
                                        cols[:, cb + 1:cb + 2], 224.0, None,
                                        OP.mult)
                        v.tensor_scalar(cols[:, cb + 3:cb + 4],
                                        cols[:, cb:cb + 1], 1.0 / 224.0,
                                        None, OP.mult)
                        vlink(v, "v")
                    qsm = C["v"]

                    @SEC('vector')
                    def _(v, pk=pk, qsm=qsm, cb=cb):
                        v.wait_ge(S["gcq"], wgu_state.get("dqA", 0))
                        v.tensor_scalar(wq8g[:, :], wslg[:, pk, :],
                                        cols[:, cb + 2:cb + 3], None, OP.mult)
                        inc(v.last, "v", 1)
                    q8m = C["v"]
                    wgu_state["cons"][pk] = C["v"]

                    @SEC('gpsimd')
                    def _(g, pk=pk, q8m=q8m, cb=cb, dsem=dsem, dstw=dstw,
                          sl=sl):
                        g.wait_ge(S["v"], q8m)
                        g.wait_ge(S["wgqw"], C["wgqw"])
                        g.wait_ge(S["wuqw"], C["wuqw"])
                        g.tensor_scalar(wdqg[:, :], wq8g[:, :],
                                        cols[:, cb + 3:cb + 4], None, OP.mult)
                        inc(g.last, "gcq", 1)
                        inc(g.dma_start(out=dstw[sl * 128:(sl + 1) * 128, :],
                                        in_=wdqg[:, :]), dsem, 16)
                    wgu_state["dqA"] = C["gcq"]

            def wout_load(g, dch):
                swo = "swo%d" % (dch % 2)
                inc(g.dma_start(
                    out=woutb[:, dch % 2, :].rearrange(
                        "p (a b) -> p a b", a=NCT),
                    in_=d_wout[:, dch * 256:(dch + 1) * 256].rearrange(
                        "(a p) c -> p a c", p=128)), swo, 16)

            @SEC('gpsimd')
            def _(g):
                g.wait_ge(S["mm"], barA)
                wout_load(g, 0)
            wo_mark = {0: C["swo0"]}

            evo_mark = {0: 0, 1: 0}
            r1spill = {0: 0, 1: 0}
            for dch in range(8):
                emit_wgu(11)
                if dch + 1 < 8:
                    @SEC('gpsimd')
                    def _(g, dch=dch):
                        g.wait_ge(S["mm"], C["mm"])
                        wout_load(g, dch + 1)
                    wo_mark[dch + 1] = C["swo%d" % ((dch + 1) % 2)]

                for tt in range(NTT):
                    bk = tt % 2
                    sxr = "sxr%d" % bk

                    @SEC('sync')
                    def _(sy, dch=dch, tt=tt, bk=bk, sxr=sxr):
                        sy.wait_ge(S["mm"], barA)
                        sy.wait_ge(S["v"], evo_mark[bk])
                        inc(sy.dma_start(
                            out=xres[:, bk, :],
                            in_=d_x[3 + tt * 128:3 + (tt + 1) * 128,
                                    dch * 256:(dch + 1) * 256]), sxr, 16)
                    xr_mark = C[sxr]

                    @SEC('tensor')
                    def _(t, dch=dch, tt=tt, bk=bk):
                        t.wait_ge(S["swo%d" % (dch % 2)], wo_mark[dch])
                        if evo_mark[bk]:
                            t.wait_ge(S["v"], evo_mark[bk])
                        for ct in range(NCT):
                            t.matmul(ps_o[:, bk * 512:bk * 512 + 256],
                                     comb[:, ct, tt * 128:(tt + 1) * 128],
                                     woutb[:, dch % 2,
                                           ct * 256:(ct + 1) * 256],
                                     start=ct == 0, stop=ct == NCT - 1)
                        inc(t.last, "mm", 1)
                    o_mark = C["mm"]

                    @SEC('vector')
                    def _(v, bk=bk, o_mark=o_mark, xr_mark=xr_mark, sxr=sxr,
                          spm=None):
                        v.wait_ge(S["mm"], o_mark)
                        v.wait_ge(S[sxr], xr_mark)
                        v.wait_ge(S["r1w"], r1spill.get(bk, 0))
                        v.tensor_add(r1c[:, bk, :],
                                     ps_o[:, bk * 512:bk * 512 + 256],
                                     xres[:, bk, :])
                        inc(v.last, "v", 1)
                    evo_mark[bk] = C["v"]
                    r1c_mark = C["v"]

                    @SEC('gpsimd')
                    def _(g, dch=dch, tt=tt, bk=bk, r1c_mark=r1c_mark):
                        g.wait_ge(S["v"], r1c_mark)
                        inc(g.dma_start(
                            out=d_r1[tt * 128:(tt + 1) * 128,
                                     dch * 256:(dch + 1) * 256],
                            in_=r1c[:, bk, :]), "r1w", 16)
                    r1spill[bk] = C["r1w"]
            r1w_done = C["r1w"]

            if db_r1 is not None:
                @SEC('gpsimd')
                def _(g):
                    g.wait_ge(S["r1w"], r1w_done)
                    inc(g.dma_start(out=db_r1, in_=d_r1), "sdbg", 16)
                    wall(g, ["sdbg"])

            x2cons = {0: 0, 1: 0}
            x2spill = {0: 0, 1: 0}
            for tt in range(NTT):
                bk = tt % 2
                r1r = "r1r%d" % bk

                @SEC('sync')
                def _(sy, tt=tt, bk=bk, r1r=r1r):
                    sy.wait_ge(S["r1w"], r1w_done)
                    sy.wait_ge(S["v"], x2cons[bk])
                    inc(sy.dma_start(out=r1t[:, bk, :],
                                     in_=d_r1[tt * 128:(tt + 1) * 128, :]),
                        r1r, 16)
                r1t_mark = C[r1r]

                @SEC('scalar')
                def _(s, bk=bk, r1t_mark=r1t_mark, r1r=r1r):
                    s.wait_ge(S[r1r], r1t_mark)
                    s.wait_ge(S["v"], C["v"])
                    s.activation(sqs[:, :], r1t[:, bk, :], AF.Square,
                                 accum_out=cols[:, 0:1])
                    inc(s.last, "sc", 1)
                sq_mark = C["sc"]

                @SEC('vector')
                def _(v, bk=bk, sq_mark=sq_mark):
                    v.wait_ge(S["sc"], sq_mark)
                    v.reduce_sum(cols[:, 1:2], r1t[:, bk, :], AX)
                    vlink(v, "v")
                    v.tensor_scalar(cols[:, 2:3], cols[:, 1:2], 1.0 / D,
                                    None, OP.mult)
                    v.tensor_scalar(cols[:, 3:4], cols[:, 0:1], 1.0 / D,
                                    None, OP.mult)
                    vlink(v, "v")
                    v.tensor_mul(cols[:, 4:5], cols[:, 2:3], cols[:, 2:3])
                    vlink(v, "v")
                    v.tensor_sub(cols[:, 5:6], cols[:, 3:4], cols[:, 4:5])
                    vlink(v, "v")
                    v.tensor_scalar(cols[:, 5:6], cols[:, 5:6], 1.0, EPS,
                                    OP.mult, OP.add)
                    inc(v.last, "v", 1)
                var_mark = C["v"]

                @SEC('scalar')
                def _(s, var_mark=var_mark):
                    s.wait_ge(S["v"], var_mark)
                    s.activation(cols[:, 6:7], cols[:, 5:6], AF.Ln)
                    inc(s.last, "sc", 1)
                ln_mark = C["sc"]

                @SEC('vector')
                def _(v, ln_mark=ln_mark):
                    v.wait_ge(S["sc"], ln_mark)
                    v.tensor_scalar(cols[:, 6:7], cols[:, 6:7], -0.5, None,
                                    OP.mult)
                    inc(v.last, "v", 1)
                lh_mark = C["v"]

                @SEC('scalar')
                def _(s, lh_mark=lh_mark):
                    s.wait_ge(S["v"], lh_mark)
                    s.activation(cols[:, 6:7], cols[:, 6:7], AF.Exp)
                    inc(s.last, "sc", 1)
                rstd_mark = C["sc"]

                @SEC('vector')
                def _(v, rstd_mark=rstd_mark):
                    v.wait_ge(S["sc"], rstd_mark)
                    v.tensor_mul(cols[:, 7:8], cols[:, 2:3], cols[:, 6:7])
                    vlink(v, "v")
                    v.tensor_scalar(cols[:, 7:8], cols[:, 7:8], -1.0, None,
                                    OP.mult)
                    inc(v.last, "v", 1)
                nb_mark = C["v"]

                @SEC('vector')
                def _(v, bk=bk, nb_mark=nb_mark):
                    v.wait_ge(S["v"], nb_mark)
                    v.wait_ge(S["x2w"], x2spill[bk])
                    v.tensor_scalar(x2t[:, bk, :], r1t[:, bk, :],
                                    cols[:, 6:7], cols[:, 7:8],
                                    OP.mult, OP.add)
                    inc(v.last, "v", 1)
                x2_mark = C["v"]

                @SEC('gpsimd')
                def _(g, tt=tt, bk=bk, x2_mark=x2_mark):
                    g.wait_ge(S["v"], x2_mark)
                    inc(g.dma_start(out=d_x2[tt * 128:(tt + 1) * 128, :],
                                    in_=x2t[:, bk, :]), "x2w", 16)
                    if db_cols is not None:
                        inc(g.dma_start(
                            out=db_cols[tt * 128:(tt + 1) * 128, :],
                            in_=cols[:, 0:16]), "sdbg", 16)
                        wall(g, ["sdbg"])
                x2spill[bk] = C["x2w"]

                @SEC('vector')
                def _(v, bk=bk, x2_mark=x2_mark):
                    v.reduce_max(cols[:, 8:9], x2t[:, bk, :], AX,
                                 apply_absolute_value=True)
                    vlink(v, "v")
                    v.reciprocal(cols[:, 9:10], cols[:, 8:9])
                    vlink(v, "v")
                    v.tensor_scalar(cols[:, 10:11], cols[:, 9:10], 224.0,
                                    None, OP.mult)
                    v.tensor_scalar(cols[:, 11:12], cols[:, 8:9], 1.0 / 224.0,
                                    None, OP.mult)
                    vlink(v, "v")
                qs_mark = C["v"]

                @SEC('vector')
                def _(v, bk=bk, qs_mark=qs_mark):
                    v.tensor_scalar(qf8x[:, :], x2t[:, bk, :],
                                    cols[:, 10:11], None, OP.mult)
                    inc(v.last, "v", 1)
                q8_mark = C["v"]
                x2cons[bk] = C["v"]

                @SEC('vector')
                def _(v, q8_mark=q8_mark):
                    v.tensor_scalar(qxdq[:, :], qf8x[:, :], cols[:, 11:12],
                                    None, OP.mult)
                    inc(v.last, "v", 1)
                qdq_mark = C["v"]

                if db_qxdq is not None:
                    @SEC('gpsimd')
                    def _(g, tt=tt, qdq_mark=qdq_mark):
                        g.wait_ge(S["v"], qdq_mark)
                        inc(g.dma_start(
                            out=db_qxdq[tt * 128:(tt + 1) * 128, :],
                            in_=qxdq[:, :]), "sdbg", 16)
                        wall(g, ["sdbg"])

                for d4 in range(4):
                    @SEC('tensor')
                    def _(t, d4=d4, qdq_mark=qdq_mark):
                        t.wait_ge(S["v"], qdq_mark)
                        t.wait_ge(S["v"], C["v"])
                        last = None
                        for kk in range(4):
                            di = d4 * 4 + kk
                            last = t.transpose(
                                ps_qt[:, kk * 128:(kk + 1) * 128],
                                qxdq[:, di * 128:(di + 1) * 128],
                                p_idb[:, :])
                        inc(last, "mm", 1)
                    tr_mark = C["mm"]

                    @SEC('vector')
                    def _(v, tt=tt, d4=d4, tr_mark=tr_mark):
                        v.wait_ge(S["mm"], tr_mark)
                        v.tensor_copy(
                            qxT[:, d4 * 4:(d4 + 1) * 4,
                                tt * 128:(tt + 1) * 128],
                            ps_qt[:, 0:512].rearrange("p (a b) -> p a b",
                                                      a=4))
                        inc(v.last, "v", 1)

            emit_wgu(10 ** 9)

            if db_x2 is not None:
                @SEC('gpsimd')
                def _(g):
                    wall(g, ["x2w"])
                    inc(g.dma_start(out=db_x2, in_=d_x2), "sdbg", 16)
                    wall(g, ["sdbg"])

            @SEC('tensor')
            def _(t):
                wall(t, ["v", "sc", "gcq"])
                inc(t.last, "mm", 1)
            barB = C["mm"]

            if upto == "mixer":
                with ExitStack() as pt:
                    ot = pt.enter_context(nc.sbuf_tensor("ot", [128, D], F32))

                    @SEC('gpsimd')
                    def _(g):
                        g.wait_ge(S["mm"], barB)
                        wall(g, ["x2w"])
                        for tt in range(NTT):
                            inc(g.dma_start(
                                out=ot[:, :],
                                in_=d_x2[tt * 128:(tt + 1) * 128, :]),
                                "outw", 16)
                            g.wait_ge(S["outw"], C["outw"])
                            inc(g.dma_start(
                                out=d_out[tt * 128:(tt + 1) * 128, :],
                                in_=ot[:, :]), "outw", 16)
                        g.wait_ge(S["outw"], C["outw"])
                return nc
        scope_w.close()
        scope_c.close()

        ICH = [(0, 1024), (1024, 1024), (2048, 1024), (3072, 1024),
               (4096, 1024), (5120, 512)]
        pc = ExitStack()
        with pc:
            ec = pc.enter_context
            wmv = ec(nc.sbuf_tensor("wmv", [128, 2, 2, ND * 1024], BF16))
            gs_t = ec(nc.sbuf_tensor("gs_t", [128, 1024], F32))
            tpc = ec(nc.sbuf_tensor("tpc", [128, 1024], F32))
            wsld = ec(nc.sbuf_tensor("wsld", [128, 2, 2816], F32))
            wq8d = ec(nc.sbuf_tensor("wq8d", [128, 2816], FP8))
            wdqd = ec(nc.sbuf_tensor("wdqd", [128, 2816], BF16))
            ps_g = ec(nc.psum_tensor("ps_g", [128, 2, 1024], F32))
            ps_u = ec(nc.psum_tensor("ps_u", [128, 2, 1024], F32))

            wd_state = {"sl": 0, "cons": 0, "dq": 0}

            def emit_wd(n):
                for _ in range(n):
                    if wd_state["sl"] >= ND:
                        return
                    sl = wd_state["sl"]
                    wd_state["sl"] += 1
                    adn = "ad%d" % (sl % 2)

                    @SEC('scalar')
                    def _(s, sl=sl, adn=adn):
                        s.wait_ge(S["mm"], barB)
                        s.wait_ge(S["v"], wd_state["cons"])
                        for h in range(2):
                            inc(s.dma_start(
                                out=wsld[:, h, :],
                                in_=d_wd[sl * 128:(sl + 1) * 128,
                                         h * 2816:(h + 1) * 2816]), adn, 16)
                    slm = C[adn]

                    @SEC('vector')
                    def _(v, slm=slm, adn=adn):
                        v.wait_ge(S[adn], slm)
                        v.reduce_max(cols[:, 28:29], wsld[:, 0, :], AX,
                                     apply_absolute_value=True)
                        v.reduce_max(cols[:, 29:30], wsld[:, 1, :], AX,
                                     apply_absolute_value=True)
                        vlink(v, "v")
                        v.tensor_max(cols[:, 28:29], cols[:, 28:29],
                                     cols[:, 29:30])
                        vlink(v, "v")
                        v.reciprocal(cols[:, 30:31], cols[:, 28:29])
                        vlink(v, "v")
                        v.tensor_scalar(cols[:, 31:32], cols[:, 30:31],
                                        224.0, None, OP.mult)
                        v.tensor_scalar(cols[:, 29:30], cols[:, 28:29],
                                        1.0 / 224.0, None, OP.mult)
                        vlink(v, "v")
                    qsm = C["v"]

                    for h in range(2):
                        @SEC('vector')
                        def _(v, h=h, qsm=qsm):
                            v.wait_ge(S["gcq"], wd_state["dq"])
                            v.tensor_scalar(wq8d[:, :], wsld[:, h, :],
                                            cols[:, 31:32], None, OP.mult)
                            inc(v.last, "v", 1)
                        q8m = C["v"]
                        if h == 1:
                            wd_state["cons"] = C["v"]

                        @SEC('gpsimd')
                        def _(g, sl=sl, h=h, q8m=q8m):
                            g.wait_ge(S["v"], q8m)
                            g.wait_ge(S["wdqw"], C["wdqw"])
                            g.tensor_scalar(wdqd[:, :], wq8d[:, :],
                                            cols[:, 29:30], None, OP.mult)
                            inc(g.last, "gcq", 1)
                            inc(g.dma_start(
                                out=d_wdq[sl * 128:(sl + 1) * 128,
                                          h * 2816:(h + 1) * 2816],
                                in_=wdqd[:, :]), "wdqw", 16)
                        wd_state["dq"] = C["gcq"]

            @SEC('vector')
            def _(v):
                v.wait_ge(S["mm"], barB)
                v.memset(rmac[:, :], 0.0)
                inc(v.last, "v", 1)

            def wmv_load(sy, ic, i0, iw):
                swm = "swm%d" % (ic % 2)
                for wi, dd in ((0, d_wgq), (1, d_wuq)):
                    for di in range(ND):
                        inc(sy.dma_start(
                            out=wmv[:, ic % 2, wi,
                                    di * 1024:di * 1024 + iw],
                            in_=dd[i0:i0 + iw, di * 128:(di + 1) * 128],
                            transpose=True), swm, 16)

            @SEC('sync')
            def _(sy):
                sy.wait_ge(S["mm"], barB)
                wall(sy, ["wgqw", "wuqw"])
                wmv_load(sy, 0, ICH[0][0], ICH[0][1])
            wm_mark = {0: C["swm0"]}

            ev_mark2 = {0: [0, 0], 1: [0, 0]}
            gscons = [0]
            for ic, (i0, iw) in enumerate(ICH):
                emit_wd(3)
                if ic + 1 < len(ICH):
                    @SEC('sync')
                    def _(sy, ic=ic):
                        sy.wait_ge(S["mm"], C["mm"])
                        wmv_load(sy, ic + 1, ICH[ic + 1][0], ICH[ic + 1][1])
                    wm_mark[ic + 1] = C["swm%d" % ((ic + 1) % 2)]

                for tt in range(NTT):
                    pp = (ic * NTT + tt) % 2

                    @SEC('tensor')
                    def _(t, ic=ic, tt=tt, iw=iw, pp=pp):
                        t.wait_ge(S["swm%d" % (ic % 2)], wm_mark[ic])
                        t.wait_ge(S["v"], ev_mark2[pp][0])
                        t.wait_ge(S["sc"], ev_mark2[pp][1])
                        for half in range(2 if iw == 1024 else 1):
                            for di in range(ND):
                                qx = qxT[:, di, tt * 128:(tt + 1) * 128]
                                base = di * 1024 + half * 512
                                t.matmul(ps_g[:, pp,
                                              half * 512:half * 512 + 512],
                                         qx, wmv[:, ic % 2, 0,
                                                 base:base + 512],
                                         start=di == 0, stop=di == ND - 1)
                                t.matmul(ps_u[:, pp,
                                              half * 512:half * 512 + 512],
                                         qx, wmv[:, ic % 2, 1,
                                                 base:base + 512],
                                         start=di == 0, stop=di == ND - 1)
                        inc(t.last, "mm", 1)
                    gu_mark = C["mm"]

                    @SEC('scalar')
                    def _(s, iw=iw, gu_mark=gu_mark, pp=pp):
                        s.wait_ge(S["mm"], gu_mark)
                        s.wait_ge(S["v"], gscons[0])
                        s.activation(gs_t[:, 0:iw], ps_g[:, pp, 0:iw],
                                     AF.Silu)
                        inc(s.last, "sc", 1)
                    ev_mark2[pp][1] = C["sc"]
                    gs_mark = C["sc"]

                    @SEC('vector')
                    def _(v, tt=tt, iw=iw, gs_mark=gs_mark, pp=pp):
                        v.wait_ge(S["sc"], gs_mark)
                        v.wait_ge(S["tw"], C["tw"])
                        v.tensor_mul(tpc[:, 0:iw], gs_t[:, 0:iw],
                                     ps_u[:, pp, 0:iw])
                        v.reduce_max(cols[:, 36:37], tpc[:, 0:iw], AX,
                                     apply_absolute_value=True)
                        vlink(v, "v")
                        v.tensor_max(rmac[:, tt:tt + 1], rmac[:, tt:tt + 1],
                                     cols[:, 36:37])
                        inc(v.last, "v", 1)
                    ev_mark2[pp][0] = C["v"]
                    gscons[0] = C["v"]
                    tp_mark = C["v"]

                    @SEC('gpsimd')
                    def _(g, tt=tt, i0=i0, iw=iw, tp_mark=tp_mark):
                        g.wait_ge(S["v"], tp_mark)
                        inc(g.dma_start(out=d_t[tt * 128:(tt + 1) * 128,
                                                i0:i0 + iw],
                                        in_=tpc[:, 0:iw]), "tw", 16)
            emit_wd(10 ** 9)
            if db_wgq is not None:
                @SEC('gpsimd')
                def _(g):
                    wall(g, ["wgqw"])
                    inc(g.dma_start(out=db_wgq, in_=d_wgq), "sdbg", 16)
                    wall(g, ["sdbg"])
            tw_done = C["tw"]

            if db_t is not None:
                @SEC('gpsimd')
                def _(g):
                    g.wait_ge(S["tw"], tw_done)
                    inc(g.dma_start(out=db_t, in_=d_t), "sdbg", 16)
                    wall(g, ["sdbg"])

        @SEC('tensor')
        def _(t):
            wall(t, ["v", "sc"])
            inc(t.last, "mm", 1)
        barC = C["mm"]
        scope_q.close()

        pc2 = ExitStack()
        with pc2:
            ec2 = pc2.enter_context
            tbuf = ec2(nc.sbuf_tensor("tbuf", [128, 2, IF], F32))
            qf8t = ec2(nc.sbuf_tensor("qf8t", [128, IF], FP8))
            qtdq = ec2(nc.sbuf_tensor("qtdq", [128, IF], BF16))

            tcons = {0: 0, 1: 0}
            for tt in range(NTT):
                bk = tt % 2
                trs = "tr%d" % bk

                @SEC('sync')
                def _(sy, tt=tt, bk=bk, trs=trs):
                    if tt == 0:
                        sy.wait_ge(S["mm"], barC)
                    sy.wait_ge(S["tw"], tw_done)
                    sy.wait_ge(S["v"], tcons[bk])
                    inc(sy.dma_start(out=tbuf[:, bk, :],
                                     in_=d_t[tt * 128:(tt + 1) * 128, :]),
                        trs, 16)
                tb_mark = C[trs]

                @SEC('vector')
                def _(v, tt=tt, tb_mark=tb_mark, trs=trs):
                    v.wait_ge(S["mm"], barC)
                    v.wait_ge(S[trs], tb_mark)
                    v.reciprocal(cols[:, 31:32], rmac[:, tt:tt + 1])
                    vlink(v, "v")
                    v.tensor_scalar(cols[:, 32:33], cols[:, 31:32], 224.0,
                                    None, OP.mult)
                    v.tensor_scalar(cols[:, 33:34], rmac[:, tt:tt + 1],
                                    1.0 / 224.0, None, OP.mult)
                    vlink(v, "v")
                qs2_mark = C["v"]

                @SEC('vector')
                def _(v, bk=bk, qs2_mark=qs2_mark):
                    v.tensor_scalar(qf8t[:, :], tbuf[:, bk, :],
                                    cols[:, 32:33], None, OP.mult)
                    inc(v.last, "v", 1)
                q8t_mark = C["v"]
                tcons[bk] = C["v"]

                @SEC('vector')
                def _(v, q8t_mark=q8t_mark):
                    v.wait_ge(S["qdqw"], C["qdqw"])
                    v.tensor_scalar(qtdq[:, :], qf8t[:, :], cols[:, 33:34],
                                    None, OP.mult)
                    inc(v.last, "v", 1)
                qdqt_mark = C["v"]

                @SEC('gpsimd')
                def _(g, tt=tt, qdqt_mark=qdqt_mark):
                    g.wait_ge(S["v"], qdqt_mark)
                    inc(g.dma_start(out=d_qdq[tt * 128:(tt + 1) * 128, :],
                                    in_=qtdq[:, :]), "qdqw", 16)
                    if db_qtdq is not None:
                        g.wait_ge(S["qdqw"], C["qdqw"])
                        inc(g.dma_start(
                            out=db_qtdq[tt * 128:(tt + 1) * 128, :],
                            in_=qtdq[:, :]), "sdbg", 16)
                        wall(g, ["sdbg"])
            qdq_done = C["qdqw"]

        @SEC('tensor')
        def _(t):
            wall(t, ["v", "sc"])
            inc(t.last, "mm", 1)
        barC2 = C["mm"]

        pd = ExitStack()
        with pd:
            ed = pd.enter_context
            qdT = ed(nc.sbuf_tensor("qdT", [128, NIT, T], BF16))
            wdm = ed(nc.sbuf_tensor("wdm", [128, 2, NIT, 512], BF16))
            x2c = ed(nc.sbuf_tensor("x2c", [128, 2, 512], F32))
            r2c = ed(nc.sbuf_tensor("r2c", [128, 2, 512], F32))
            r2t = ed(nc.sbuf_tensor("r2t", [128, 1, D], F32))
            otb = ed(nc.sbuf_tensor("otb", [128, 1, D], F32))
            ps_d = ed(nc.psum_tensor("ps_d", [128, NTT * 512], F32))

            @SEC('sync')
            def _(sy):
                sy.wait_ge(S["mm"], barC2)
                sy.wait_ge(S["qdqw"], qdq_done)
                for i in range(NIT):
                    inc(sy.dma_start(out=qdT[:, i, :],
                                     in_=d_qdq[:, i * 128:(i + 1) * 128],
                                     transpose=True), "qdT", 16)
            qdT_mark = C["qdT"]

            def wdm_load(sy, dch):
                sem = "wdm%d" % (dch % 2)
                for i in range(NIT):
                    inc(sy.dma_start(
                        out=wdm[:, dch % 2, i, :],
                        in_=d_wdq[dch * 512:(dch + 1) * 512,
                                  i * 128:(i + 1) * 128],
                        transpose=True), sem, 16)
                return C[sem]

            @SEC('sync')
            def _(sy):
                wall(sy, ["wdqw"])
                wdm_load(sy, 0)
                wdm_load(sy, 1)
            wdm_marks = {0: C["wdm0"], 1: C["wdm1"]}

            dmm_marks = {}
            r2spill = {0: 0, 1: 0}
            for dch in range(4):
                @SEC('tensor')
                def _(t, dch=dch):
                    t.wait_ge(S["qdT"], qdT_mark)
                    t.wait_ge(S["wdm%d" % (dch % 2)], wdm_marks[dch])
                    t.wait_ge(S["v"], C["v"])
                    for i in range(NIT):
                        for tt in range(NTT):
                            t.matmul(ps_d[:, tt * 512:(tt + 1) * 512],
                                     qdT[:, i, tt * 128:(tt + 1) * 128],
                                     wdm[:, dch % 2, i, :],
                                     start=i == 0, stop=i == NIT - 1)
                    inc(t.last, "mm", 1)
                d_mark = C["mm"]
                dmm_marks[dch] = C["mm"]
                if dch + 2 < 4:
                    @SEC('sync')
                    def _(sy, dch=dch):
                        sy.wait_ge(S["mm"], dmm_marks[dch])
                        wdm_marks[dch + 2] = wdm_load(sy, dch + 2)

                for tt in range(NTT):
                    bk = tt % 2
                    sxc = "sx2c%d" % bk

                    @SEC('sync')
                    def _(sy, dch=dch, tt=tt, bk=bk, sxc=sxc):
                        sy.wait_ge(S["v"], C["v"])
                        wall(sy, ["x2w"])
                        inc(sy.dma_start(
                            out=x2c[:, bk, :],
                            in_=d_x2[tt * 128:(tt + 1) * 128,
                                     dch * 512:(dch + 1) * 512]), sxc, 16)
                    x2c_mark = C[sxc]

                    @SEC('vector')
                    def _(v, tt=tt, bk=bk, d_mark=d_mark, x2c_mark=x2c_mark,
                          sxc=sxc):
                        v.wait_ge(S["mm"], d_mark)
                        v.wait_ge(S[sxc], x2c_mark)
                        v.wait_ge(S["r2w"], r2spill[bk])
                        v.tensor_add(r2c[:, bk, :],
                                     ps_d[:, tt * 512:(tt + 1) * 512],
                                     x2c[:, bk, :])
                        inc(v.last, "v", 1)
                    r2c_mark = C["v"]

                    @SEC('gpsimd')
                    def _(g, dch=dch, tt=tt, bk=bk, r2c_mark=r2c_mark):
                        g.wait_ge(S["v"], r2c_mark)
                        inc(g.dma_start(
                            out=d_r2[tt * 128:(tt + 1) * 128,
                                     dch * 512:(dch + 1) * 512],
                            in_=r2c[:, bk, :]), "r2w", 16)
                    r2spill[bk] = C["r2w"]
            r2w_done = C["r2w"]

            ocons = {0: 0, 1: 0}
            ospill = {0: 0, 1: 0}
            for tt in range(NTT):
                bk = tt % 2
                r2r = "r2r%d" % bk

                @SEC('sync')
                def _(sy, tt=tt, bk=bk, r2r=r2r):
                    sy.wait_ge(S["r2w"], r2w_done)
                    sy.wait_ge(S["v"], ocons[0])
                    inc(sy.dma_start(out=r2t[:, 0, :],
                                     in_=d_r2[tt * 128:(tt + 1) * 128, :]),
                        r2r, 16)
                r2t_mark = C[r2r]

                @SEC('scalar')
                def _(s, bk=bk, r2t_mark=r2t_mark, r2r=r2r):
                    s.wait_ge(S[r2r], r2t_mark)
                    s.wait_ge(S["v"], C["v"])
                    s.wait_ge(S["outw"], ospill[0])
                    s.activation(otb[:, 0, :], r2t[:, 0, :], AF.Square,
                                 accum_out=cols[:, 40:41])
                    inc(s.last, "sc", 1)
                sq_mark = C["sc"]

                @SEC('vector')
                def _(v, bk=bk, sq_mark=sq_mark):
                    v.wait_ge(S["sc"], sq_mark)
                    v.reduce_sum(cols[:, 41:42], r2t[:, 0, :], AX)
                    vlink(v, "v")
                    v.tensor_scalar(cols[:, 42:43], cols[:, 41:42], 1.0 / D,
                                    None, OP.mult)
                    v.tensor_scalar(cols[:, 43:44], cols[:, 40:41], 1.0 / D,
                                    None, OP.mult)
                    vlink(v, "v")
                    v.tensor_mul(cols[:, 44:45], cols[:, 42:43],
                                 cols[:, 42:43])
                    vlink(v, "v")
                    v.tensor_sub(cols[:, 45:46], cols[:, 43:44],
                                 cols[:, 44:45])
                    vlink(v, "v")
                    v.tensor_scalar(cols[:, 45:46], cols[:, 45:46], 1.0, EPS,
                                    OP.mult, OP.add)
                    inc(v.last, "v", 1)
                var2_mark = C["v"]

                @SEC('scalar')
                def _(s, var2_mark=var2_mark):
                    s.wait_ge(S["v"], var2_mark)
                    s.activation(cols[:, 46:47], cols[:, 45:46], AF.Ln)
                    inc(s.last, "sc", 1)
                ln2_mark = C["sc"]

                @SEC('vector')
                def _(v, ln2_mark=ln2_mark):
                    v.wait_ge(S["sc"], ln2_mark)
                    v.tensor_scalar(cols[:, 46:47], cols[:, 46:47], -0.5,
                                    None, OP.mult)
                    inc(v.last, "v", 1)
                lh2_mark = C["v"]

                @SEC('scalar')
                def _(s, lh2_mark=lh2_mark):
                    s.wait_ge(S["v"], lh2_mark)
                    s.activation(cols[:, 46:47], cols[:, 46:47], AF.Exp)
                    inc(s.last, "sc", 1)
                rstd2_mark = C["sc"]

                @SEC('vector')
                def _(v, rstd2_mark=rstd2_mark):
                    v.wait_ge(S["sc"], rstd2_mark)
                    v.tensor_mul(cols[:, 47:48], cols[:, 42:43],
                                 cols[:, 46:47])
                    vlink(v, "v")
                    v.tensor_scalar(cols[:, 47:48], cols[:, 47:48], -1.0,
                                    None, OP.mult)
                    inc(v.last, "v", 1)
                nb2_mark = C["v"]

                @SEC('vector')
                def _(v, bk=bk, nb2_mark=nb2_mark):
                    v.wait_ge(S["v"], nb2_mark)
                    v.wait_ge(S["outw"], ospill[0])
                    v.tensor_scalar(otb[:, 0, :], r2t[:, 0, :],
                                    cols[:, 46:47], cols[:, 47:48],
                                    OP.mult, OP.add)
                    inc(v.last, "v", 1)
                ot_mark = C["v"]
                ocons[0] = C["v"]

                @SEC('gpsimd')
                def _(g, tt=tt, bk=bk, ot_mark=ot_mark):
                    g.wait_ge(S["v"], ot_mark)
                    inc(g.dma_start(out=d_out[tt * 128:(tt + 1) * 128, :],
                                    in_=otb[:, 0, :]), "outw", 16)
                ospill[0] = C["outw"]

            @SEC('gpsimd')
            def _(g):
                g.wait_ge(S["outw"], C["outw"])

    return nc


def kernel(**inputs):
    if "main" not in _CACHE:
        _CACHE["main"] = build()
    nc = _CACHE["main"]
    s = _stage_shared(inputs)
    in_maps = [_stage_core(s, c) for c in range(8)]
    res = run_bass_kernel_spmd(nc, in_maps, core_ids=list(range(8)))
    out = np.zeros((2, 4096, 2048), np.float32)
    for c in range(8):
        out[c // 4, 1024 * (c % 4):1024 * (c % 4 + 1)] = res.results[c]["out"]
    return out


# version 05
# speedup vs baseline: 1.0065x; 1.0065x over previous
import numpy as np
from contextlib import ExitStack

import concourse.bass as bass
import concourse.mybir as mybir
from concourse.bass_utils import run_bass_kernel_spmd

F32 = mybir.dt.float32
F32R = mybir.dt.float32r
BF16 = mybir.dt.bfloat16
FP8 = mybir.dt.float8e4
AX = mybir.AxisListType.X
OP = mybir.AluOpType
AF = mybir.ActivationFunctionType

T = 1024
NTT = 8
D = 2048
ND = 16
INNER = 4096
NCT = 32
GRP = 4
NG = NCT // GRP
IF = 5632
NIT = 44
EPS = 1e-5

_CACHE = {}


def _stage_shared(inputs):
    g = lambda k: np.asarray(inputs[k], dtype=np.float32)
    s = {}
    s["x_full"] = g("L0_attn_hidden_states")
    s["w_in_T"] = np.ascontiguousarray(g("L0_attn_in_proj_w").T)
    s["w_gate_T"] = np.ascontiguousarray(g("L0_attn_gate_proj_w").T)
    s["w_out_T"] = np.ascontiguousarray(g("L0_attn_out_proj_w").T)
    s["w_g_nat"] = np.ascontiguousarray(g("L0_ffn_w_gate"))
    s["w_u_nat"] = np.ascontiguousarray(g("L0_ffn_w_up"))
    dn_idx = np.asarray(inputs["L0_ffn_down_reorder_index"]).astype(np.int64)
    s["w_d_eff"] = np.ascontiguousarray(g("L0_ffn_w_down")[:, np.argsort(dn_idx)])

    def chanmajor(a):
        return np.ascontiguousarray(a.reshape(NCT, 128).T)

    s["a_logt"] = chanmajor(g("L0_attn_A_log").reshape(-1))
    s["bp_t"] = chanmajor(g("L0_attn_Bp").reshape(-1))
    s["cp_t"] = chanmajor(g("L0_attn_Cp").reshape(-1))
    s["dp_t"] = chanmajor(g("L0_attn_Dp").reshape(-1))
    s["conv_bt"] = chanmajor(g("L0_attn_conv_b"))
    cw = g("L0_attn_conv_w")
    s["conv_wt"] = np.ascontiguousarray(
        cw.reshape(NCT, 128, 4).transpose(1, 0, 2).reshape(128, NCT * 4))
    dtw = g("L0_attn_dt_w")
    dtw_stat = np.zeros((128, NCT * 8), np.float32)
    sel_stat = np.zeros((8, GRP * 128), np.float32)
    for ct in range(NCT):
        j0 = 2 * (ct % GRP)
        dtw_stat[0:64, ct * 8 + j0] = dtw[2 * ct]
        dtw_stat[64:128, ct * 8 + j0 + 1] = dtw[2 * ct + 1]
    for kk in range(GRP):
        sel_stat[2 * kk, kk * 128:kk * 128 + 64] = 1.0
        sel_stat[2 * kk + 1, kk * 128 + 64:kk * 128 + 128] = 1.0
    s["dtw_stat"] = dtw_stat
    s["sel_stat"] = sel_stat
    s["dt_bt"] = np.ascontiguousarray(g("L0_attn_dt_b").reshape(NG, 8).T)
    s["ident_f"] = np.eye(128, dtype=np.float32)
    return s


def _stage_core(s, core):
    q = core % 4
    xb = s["x_full"][core // 4]
    if q == 0:
        xs = np.concatenate([np.zeros((3, D), np.float32), xb[0:1024]], 0)
    else:
        xs = xb[1024 * q - 3:1024 * (q + 1)]
    m = np.zeros((128, 4), np.float32)
    m[:, :q] = 1.0
    return dict(
        x_sl=np.ascontiguousarray(xs),
        w_in_T=s["w_in_T"], w_gate_T=s["w_gate_T"], w_out_T=s["w_out_T"],
        w_g_nat=s["w_g_nat"], w_u_nat=s["w_u_nat"], w_d_eff=s["w_d_eff"],
        a_logt=s["a_logt"], bp_t=s["bp_t"], cp_t=s["cp_t"], dp_t=s["dp_t"],
        conv_bt=s["conv_bt"], conv_wt=s["conv_wt"],
        dtw_stat=s["dtw_stat"], sel_stat=s["sel_stat"], dt_bt=s["dt_bt"],
        mask_m=np.ascontiguousarray(m),
        mask_1m=np.ascontiguousarray(1.0 - m),
        ident_f=s["ident_f"],
    )


def build(debug=(), upto="full", ng_lim=NG):
    nc = bass.Bass("TRN2", target_bir_lowering=False, debug=False,
                   detect_race_conditions=False)

    d_x = nc.dram_tensor("x_sl", [1027, D], F32, kind="ExternalInput").ap()
    d_win = nc.dram_tensor("w_in_T", [D, INNER], F32, kind="ExternalInput").ap()
    d_wgt = nc.dram_tensor("w_gate_T", [D, INNER], F32, kind="ExternalInput").ap()
    d_wout = nc.dram_tensor("w_out_T", [INNER, D], F32, kind="ExternalInput").ap()
    d_wg = nc.dram_tensor("w_g_nat", [IF, D], F32, kind="ExternalInput").ap()
    d_wu = nc.dram_tensor("w_u_nat", [IF, D], F32, kind="ExternalInput").ap()
    d_wd = nc.dram_tensor("w_d_eff", [D, IF], F32, kind="ExternalInput").ap()
    d_alog = nc.dram_tensor("a_logt", [128, NCT], F32, kind="ExternalInput").ap()
    d_bp = nc.dram_tensor("bp_t", [128, NCT], F32, kind="ExternalInput").ap()
    d_cp = nc.dram_tensor("cp_t", [128, NCT], F32, kind="ExternalInput").ap()
    d_dp = nc.dram_tensor("dp_t", [128, NCT], F32, kind="ExternalInput").ap()
    d_cb = nc.dram_tensor("conv_bt", [128, NCT], F32, kind="ExternalInput").ap()
    d_cw = nc.dram_tensor("conv_wt", [128, NCT * 4], F32, kind="ExternalInput").ap()
    d_dtw = nc.dram_tensor("dtw_stat", [128, NCT * 8], F32, kind="ExternalInput").ap()
    d_sel = nc.dram_tensor("sel_stat", [8, GRP * 128], F32, kind="ExternalInput").ap()
    d_dtb = nc.dram_tensor("dt_bt", [8, NG], F32, kind="ExternalInput").ap()
    d_mm_ = nc.dram_tensor("mask_m", [128, 4], F32, kind="ExternalInput").ap()
    d_m1 = nc.dram_tensor("mask_1m", [128, 4], F32, kind="ExternalInput").ap()
    d_idf = nc.dram_tensor("ident_f", [128, 128], F32, kind="ExternalInput").ap()

    d_out = nc.dram_tensor("out", [T, D], F32, kind="ExternalOutput").ap()

    d_ag_in = nc.dram_tensor("ag_in", [NG, 8 * 128], F32).ap()
    d_ag_out = nc.dram_tensor("ag_out", [NG, 4 * 8 * 128], F32).ap()
    d_r1 = nc.dram_tensor("r1_sp", [T, D], F32).ap()
    d_x2 = nc.dram_tensor("x2_sp", [T, D], F32).ap()
    d_r2 = nc.dram_tensor("r2_sp", [T, D], F32).ap()
    d_t = nc.dram_tensor("t_sp", [T, IF], F32).ap()
    d_qdq = nc.dram_tensor("qdq_sp", [T, IF], BF16).ap()
    d_wgq = nc.dram_tensor("wgq_sp", [IF, D], BF16).ap()
    d_wuq = nc.dram_tensor("wuq_sp", [IF, D], BF16).ap()
    d_wdq = nc.dram_tensor("wdq_sp", [D, IF], BF16).ap()

    def mkdbg(name, shape, dtype=F32):
        if name in debug:
            return nc.dram_tensor("dbg_" + name, shape, dtype,
                                  kind="ExternalOutput").ap()
        return None

    db_decay = mkdbg("decay", [NCT * 128, T])
    db_states = mkdbg("states", [NCT * 128, T])
    db_dt = mkdbg("dt", [8, NG * T])
    db_comb = mkdbg("combined", [NCT * 128, T])
    db_r1 = mkdbg("r1", [T, D])
    db_x2 = mkdbg("x2", [T, D])
    db_qxdq = mkdbg("qxdq", [T, D])
    db_cols = mkdbg("lncols", [NTT * 128, 16])
    db_t = mkdbg("ffn_t", [T, IF])
    db_wgq = mkdbg("wgq", [IF, D], BF16)
    db_qtdq = mkdbg("qtdq", [T, IF])
    db_er1 = mkdbg("er1", [T, D])
    db_ex2 = mkdbg("ex2", [T, D])
    db_er2 = mkdbg("er2", [T, D])

    C, S = {}, {}

    def inc(instr, name, n=1):
        C[name] += n
        instr.then_inc(S[name], n)

    def wall(eng, names):
        for n in names:
            if C[n] > 0:
                eng.wait_ge(S[n], C[n])

    ALLSEM = (["mm", "v", "sc", "par", "sdbg",
               "sx0", "sx1", "sw0", "sw1", "agi", "ago",
               "swo0", "swo1", "sxr0", "sxr1", "r1w0", "r1w1", "r1r0", "r1r1", "x2w0", "x2w1",
               "sx2c0", "sx2c1", "wq0", "wq1", "wgqw", "wuqw", "wdqw",
               "swm0", "swm1", "tw", "tr0", "tr1", "qdqw",
               "qdT", "wdm0", "wdm1", "r2w0", "r2w1", "r2r0", "r2r1", "outw", "gcq", "ad0", "ad1"]
              + ["cc%d" % i for i in range(NG)])

    ctx = ExitStack()
    with ctx:
        e = ctx.enter_context
        for n in ALLSEM:
            S[n] = e(nc.semaphore(n))
            C[n] = 0

        blk = e(nc.Block())

        class W:
            def __init__(self, eng):
                object.__setattr__(self, "_e", eng)
                object.__setattr__(self, "last", None)

            def __getattr__(self, n):
                f = getattr(self._e, n)
                if callable(f):
                    def g(*a, **k):
                        r = f(*a, **k)
                        object.__setattr__(self, "last", r)
                        return r
                    return g
                return f

        def vlink(eng, sem):
            inc(eng.last, sem, 1)
            eng.wait_ge(S[sem], C[sem])

        def SEC(name):
            def deco(f):
                def wrapper(eng):
                    return f(W(eng))
                getattr(blk, name)(wrapper)
                return None
            return deco

        scope_x = ExitStack()
        scope_c = ExitStack()
        scope_q = ExitStack()
        comb = scope_c.enter_context(
            nc.sbuf_tensor("comb", [128, NCT, T], BF16, side="right"))
        xbf = scope_x.enter_context(
            nc.sbuf_tensor("xbf", [128, ND, 1028], BF16, side="right"))
        p_alog = e(nc.sbuf_tensor("p_alog", [128, NCT], F32))
        p_negA = e(nc.sbuf_tensor("p_negA", [128, NCT], F32))
        p_bp = e(nc.sbuf_tensor("p_bp", [128, NCT], F32))
        p_cp = e(nc.sbuf_tensor("p_cp", [128, NCT], F32))
        p_dp = e(nc.sbuf_tensor("p_dp", [128, NCT], F32))
        p_cb = e(nc.sbuf_tensor("p_cb", [128, NCT], F32))
        p_cw = e(nc.sbuf_tensor("p_cw", [128, NCT * 4], F32))
        p_dtw = e(nc.sbuf_tensor("p_dtw", [128, NCT * 8], BF16))
        p_sel = e(nc.sbuf_tensor("p_sel", [8, GRP * 128], F32R))
        p_dtb = e(nc.sbuf_tensor("p_dtb", [8, NG], F32))
        p_mm = e(nc.sbuf_tensor("p_mm", [128, 4], F32))
        p_m1 = e(nc.sbuf_tensor("p_m1", [128, 4], F32))
        p_idf = e(nc.sbuf_tensor("p_idf", [128, 128], F32))
        p_idb = e(nc.sbuf_tensor("p_idb", [128, 128], BF16))
        cols = e(nc.sbuf_tensor("cols", [128, 64], F32))
        rmac = e(nc.sbuf_tensor("rmac", [128, NTT], F32))

        @SEC('gpsimd')
        def _(g):
            for dst, src in [(p_alog, d_alog), (p_bp, d_bp), (p_cp, d_cp),
                             (p_dp, d_dp), (p_cb, d_cb), (p_cw, d_cw),
                             (p_dtw, d_dtw), (p_sel, d_sel), (p_dtb, d_dtb),
                             (p_mm, d_mm_), (p_m1, d_m1), (p_idf, d_idf),
                             (p_idb, d_idf)]:
                inc(g.dma_start(out=dst[:, :], in_=src), "par", 16)

        @SEC('scalar')
        def _(s):
            wall(s, ["par"])
            s.activation(p_negA[:, :], p_alog[:, :], AF.Exp)
            inc(s.last, "sc", 1)

        @SEC('vector')
        def _(v):
            wall(v, ["sc"])
            v.tensor_scalar(p_negA[:, :], p_negA[:, :], -1.0, None, OP.mult)
            inc(v.last, "v", 1)

        with ExitStack() as c0:
            xtok = c0.enter_context(nc.sbuf_tensor("xtok", [128, 2, D], F32))
            ps_tr = c0.enter_context(nc.psum_tensor("ps_tr", [128, 512], F32))

            xcons_mark = {0: 0, 1: 0}
            for tt in range(9):
                rows = 128 if tt < 8 else 3
                buf = tt % 2
                col0 = tt * 128
                sxn = "sx%d" % buf

                @SEC('sync')
                def _(sy, tt=tt, rows=rows, buf=buf, sxn=sxn):
                    sy.wait_ge(S["mm"], xcons_mark[buf])
                    inc(sy.dma_start(out=xtok[0:rows, buf, :],
                                     in_=d_x[tt * 128:tt * 128 + rows, :]),
                        sxn, 16)
                sd_mark = C[sxn]

                for d4 in range(4):
                    @SEC('tensor')
                    def _(t, rows=rows, buf=buf, d4=d4, sd_mark=sd_mark,
                          sxn=sxn):
                        t.wait_ge(S[sxn], sd_mark)
                        t.wait_ge(S["v"], C["v"])
                        last = None
                        for k in range(4):
                            di = d4 * 4 + k
                            last = t.transpose(
                                ps_tr[:, k * 128:k * 128 + rows],
                                xtok[0:rows, buf, di * 128:(di + 1) * 128],
                                p_idf[0:rows, 0:rows])
                        inc(last, "mm", 1)
                    mm_mark = C["mm"]

                    @SEC('vector')
                    def _(v, rows=rows, d4=d4, col0=col0, mm_mark=mm_mark):
                        v.wait_ge(S["mm"], mm_mark)
                        src = ps_tr[:, 0:512].rearrange(
                            "p (a b) -> p a b", a=4)[:, :, 0:rows]
                        v.tensor_copy(xbf[:, d4 * 4:(d4 + 1) * 4,
                                          col0:col0 + rows], src)
                        inc(v.last, "v", 1)
                xcons_mark[buf] = C["mm"]

        pa = ExitStack()
        with pa:
            ea = pa.enter_context
            projX = ea(nc.sbuf_tensor("projX", [128, 2, GRP, 1028], BF16))
            gateb = ea(nc.sbuf_tensor("gateb", [128, 2, GRP, T], BF16))
            win = ea(nc.sbuf_tensor("win", [128, 2, ND * 128], BF16))
            wgt = ea(nc.sbuf_tensor("wgt", [128, 2, ND * 128], BF16))
            scr = ea(nc.sbuf_tensor("scr", [128, 6, T], F32))
            tmpv = ea(nc.sbuf_tensor("tmpv", [128, T], F32))
            dtf = ea(nc.sbuf_tensor("dtf", [8, T], F32R))
            cgb = ea(nc.sbuf_tensor("cgb", [128, 2, GRP, T], BF16))
            agb = ea(nc.sbuf_tensor("agb", [128, 8], F32))
            agot = ea(nc.sbuf_tensor("agot", [128, 32], F32))
            sall = ea(nc.sbuf_tensor("sall", [128, GRP], F32))
            zer = ea(nc.sbuf_tensor("zer", [128, T], F32))
            ps_p = ea(nc.psum_tensor("ps_p", [128, 2048], F32))
            ps_dt = ea(nc.psum_tensor("ps_dt", [8, T], F32))
            ps_bc = ea(nc.psum_tensor("ps_bc", [128, T], F32))

            @SEC('vector')
            def _(v):
                v.memset(zer[:, :], 0.0)
                if ng_lim < NG:
                    for ct in range(NCT):
                        v.memset(comb[:, ct, :], 0.0)
                inc(v.last, "v", 1)

            wload_mark = {}

            def emit_w_load(g, ct):
                buf = ct % 2
                sw = "sw%d" % buf
                for wt, dd in ((win, d_win), (wgt, d_wgt)):
                    inc(g.dma_start(
                        out=wt[:, buf, :].rearrange("p (a b) -> p a b", a=ND),
                        in_=dd[:, ct * 128:(ct + 1) * 128].rearrange(
                            "(a p) c -> p a c", p=128)), sw, 16)
                wload_mark[ct] = C[sw]

            @SEC('gpsimd')
            def _(g):
                wall(g, ["v"])
                emit_w_load(g, 0)
                emit_w_load(g, 1)

            evA = {0: 0, 1: 0, 2: 0}
            evac_mark = {"dt": 0, "dtv": 0}
            pend_ag = []
            corr_marks = {}
            prev_scan_mark = 0
            prev_drv_mark = 0
            prev_dcy_mark = 0

            for it in range(ng_lim + 1):
                if it < ng_lim:
                    for k in range(GRP):
                        ct = it * GRP + k
                        wbuf = ct % 2
                        swn = "sw%d" % wbuf

                        @SEC('tensor')
                        def _(t, ct=ct, wbuf=wbuf, swn=swn):
                            t.wait_ge(S[swn], wload_mark[ct])
                            if ct == 0:
                                wall(t, ["v"])
                            t.wait_ge(S["v"], evA[0])
                            for di in range(ND):
                                w = win[:, wbuf, di * 128:(di + 1) * 128]
                                t.matmul(ps_p[:, 0:512], w, xbf[:, di, 0:512],
                                         start=di == 0, stop=di == ND - 1)
                            inc(t.last, "mm", 1)
                        m_p0 = C["mm"]

                        @SEC('tensor')
                        def _(t, ct=ct, wbuf=wbuf):
                            t.wait_ge(S["v"], evA[1])
                            for di in range(ND):
                                w = win[:, wbuf, di * 128:(di + 1) * 128]
                                t.matmul(ps_p[:, 512:1024], w,
                                         xbf[:, di, 512:1024],
                                         start=di == 0, stop=di == ND - 1)
                            inc(t.last, "mm", 1)
                        m_p1 = C["mm"]

                        @SEC('tensor')
                        def _(t, ct=ct, wbuf=wbuf):
                            t.wait_ge(S["sc"], evA[2])
                            for di in range(ND):
                                w = wgt[:, wbuf, di * 128:(di + 1) * 128]
                                t.matmul(ps_p[:, 1024:1536], w,
                                         xbf[:, di, 3:515],
                                         start=di == 0, stop=di == ND - 1)
                            for di in range(ND):
                                w = wgt[:, wbuf, di * 128:(di + 1) * 128]
                                t.matmul(ps_p[:, 1536:2048], w,
                                         xbf[:, di, 515:1027],
                                         start=di == 0, stop=di == ND - 1)
                            inc(t.last, "mm", 1)
                        m_g = C["mm"]

                        @SEC('vector')
                        def _(v, it=it, k=k, m_p0=m_p0, m_p1=m_p1):
                            v.wait_ge(S["mm"], m_p1)
                            v.tensor_copy(projX[:, it % 2, k, 0:512],
                                          ps_p[:, 0:512])
                            inc(v.last, "v", 1)
                            v.tensor_copy(projX[:, it % 2, k, 512:1024],
                                          ps_p[:, 512:1024])
                            inc(v.last, "v", 1)
                        evA[0] = C["v"] - 1
                        evA[1] = C["v"]
                        ev_halo_pre = C["v"] - 1

                        @SEC('tensor')
                        def _(t, ct=ct, wbuf=wbuf, ev=ev_halo_pre):
                            t.wait_ge(S["v"], ev)
                            for di in range(ND):
                                w = win[:, wbuf, di * 128:(di + 1) * 128]
                                t.matmul(ps_p[:, 0:3], w,
                                         xbf[:, di, 1024:1027],
                                         start=di == 0, stop=di == ND - 1)
                            inc(t.last, "mm", 1)
                        m_ph = C["mm"]

                        @SEC('vector')
                        def _(v, it=it, k=k, m_ph=m_ph):
                            v.wait_ge(S["mm"], m_ph)
                            v.tensor_copy(projX[:, it % 2, k, 1024:1027],
                                          ps_p[:, 0:3])
                            inc(v.last, "v", 1)
                        evA[0] = C["v"]
                        proj_done = C["v"]

                        @SEC('scalar')
                        def _(s, it=it, k=k, m_g=m_g):
                            s.wait_ge(S["mm"], m_g)
                            s.activation(gateb[:, it % 2, k, 0:1024],
                                         ps_p[:, 1024:2048], AF.Sigmoid)
                            inc(s.last, "sc", 1)
                        evA[2] = C["sc"]

                        if ct + 2 < ng_lim * GRP:
                            @SEC('gpsimd')
                            def _(g, ct=ct):
                                g.wait_ge(S["mm"], C["mm"])
                                emit_w_load(g, ct + 2)

                        @SEC('vector')
                        def _(v, it=it, ct=ct, k=k, proj_done=proj_done):
                            v.wait_ge(S["v"], proj_done)
                            pj = projX[:, it % 2, k, :]
                            cwc = lambda j: p_cw[:, ct * 4 + j:ct * 4 + j + 1]
                            a0 = scr[:, k % 2, :]
                            a1 = scr[:, 2 + k % 2, :]
                            v.tensor_scalar(a0, pj[:, 0:1024], cwc(0),
                                            p_cb[:, ct:ct + 1], OP.mult,
                                            OP.add)
                            v.scalar_tensor_tensor(a1, pj[:, 1:1025], cwc(1),
                                                   a0, OP.mult, OP.add)
                            v.scalar_tensor_tensor(a0, pj[:, 2:1026], cwc(2),
                                                   a1, OP.mult, OP.add)
                            v.scalar_tensor_tensor(a1, pj[:, 3:1027], cwc(3),
                                                   a0, OP.mult, OP.add)
                            inc(v.last, "v", 1)
                        conv_mark = C["v"]

                        @SEC('scalar')
                        def _(s, it=it, k=k, conv_mark=conv_mark):
                            s.wait_ge(S["v"], conv_mark)
                            s.activation(projX[:, it % 2, k, 0:1024],
                                         scr[:, 2 + k % 2, :], AF.Silu)
                            inc(s.last, "sc", 1)
                        x_mark = C["sc"]

                        @SEC('tensor')
                        def _(t, it=it, ct=ct, k=k, x_mark=x_mark):
                            t.wait_ge(S["sc"], x_mark)
                            if k == 0:
                                t.wait_ge(S["sc"], evac_mark["dt"])
                                t.wait_ge(S["v"], evac_mark["dtv"])
                            Xap = projX[:, it % 2, k, 0:1024]
                            t.matmul(ps_dt[:, 0:512],
                                     p_dtw[:, ct * 8:(ct + 1) * 8],
                                     Xap[:, 0:512],
                                     start=k == 0, stop=k == GRP - 1)
                            t.matmul(ps_dt[:, 512:1024],
                                     p_dtw[:, ct * 8:(ct + 1) * 8],
                                     Xap[:, 512:1024],
                                     start=k == 0, stop=k == GRP - 1)
                            inc(t.last, "mm", 1)
                    dtmm_mark = C["mm"]

                    @SEC('scalar')
                    def _(s, it=it, dtmm_mark=dtmm_mark):
                        s.wait_ge(S["mm"], dtmm_mark)
                        s.wait_ge(S["v"], C["v"])
                        az = scr[0:8, 5, :]
                        s.activation(az, ps_dt[:, :], AF.Abs,
                                     bias=p_dtb[:, it:it + 1])
                        s.activation(az, az, AF.Exp, scale=-1.0)
                        s.activation(az, az, AF.Ln, bias=1.0)
                        inc(s.last, "sc", 1)
                    sp_mark = C["sc"]
                    evac_mark["dt"] = C["sc"]

                    @SEC('vector')
                    def _(v, it=it, dtmm_mark=dtmm_mark, sp_mark=sp_mark):
                        v.wait_ge(S["mm"], dtmm_mark)
                        v.tensor_scalar(tmpv[0:8, :], ps_dt[:, :],
                                        p_dtb[:, it:it + 1], 0.0,
                                        OP.add, OP.max)
                        v.wait_ge(S["sc"], sp_mark)
                        v.tensor_add(dtf[:, :], tmpv[0:8, :], scr[0:8, 5, :])
                        inc(v.last, "v", 1)
                    dtv_mark = C["v"]
                    evac_mark["dtv"] = C["v"]

                    if db_dt is not None:
                        @SEC('gpsimd')
                        def _(g, it=it, dtv_mark=dtv_mark):
                            g.wait_ge(S["v"], dtv_mark)
                            inc(g.dma_start(out=db_dt[:, it * T:(it + 1) * T],
                                            in_=dtf[:, :]), "sdbg", 16)
                            wall(g, ["sdbg"])

                    for k in range(GRP):
                        ct = it * GRP + k

                        @SEC('tensor')
                        def _(t, k=k, dtv_mark=dtv_mark,
                              psm=prev_drv_mark, pdm=prev_dcy_mark):
                            t.wait_ge(S["v"], dtv_mark)
                            t.wait_ge(S["sc"], pdm)
                            t.wait_ge(S["v"], psm)
                            t.matmul(ps_bc[:, 0:512],
                                     p_sel[:, k * 128:(k + 1) * 128],
                                     dtf[:, 0:512], start=True, stop=True)
                            t.matmul(ps_bc[:, 512:1024],
                                     p_sel[:, k * 128:(k + 1) * 128],
                                     dtf[:, 512:1024], start=True, stop=True)
                            inc(t.last, "mm", 1)
                        bc_mark = C["mm"]

                        @SEC('scalar')
                        def _(s, ct=ct, bc_mark=bc_mark, psm=prev_scan_mark):
                            s.wait_ge(S["mm"], bc_mark)
                            s.wait_ge(S["v"], psm)
                            s.activation(scr[:, 0, :], ps_bc[:, :], AF.Exp,
                                         scale=p_negA[:, ct:ct + 1])
                            inc(s.last, "sc", 1)
                        dcy_mark = C["sc"]
                        prev_dcy_mark = dcy_mark

                        @SEC('vector')
                        def _(v, it=it, ct=ct, k=k, dcy_mark=dcy_mark):
                            v.wait_ge(S["sc"], dcy_mark)
                            if k == 0:
                                v.wait_ge(S["agi"], C["agi"])
                            X = projX[:, it % 2, k, 0:1024]
                            gt = gateb[:, it % 2, k, :]
                            dcy, drv = scr[:, 0, :], scr[:, 1, :]
                            stt, cdc = scr[:, 2, :], scr[:, 3, :]
                            v.scalar_tensor_tensor(drv, ps_bc[:, :],
                                                   p_bp[:, ct:ct + 1], X,
                                                   OP.mult, OP.mult)
                            inc(v.last, "v", 1)
                            v.tensor_tensor_scan(stt, dcy, drv, 0.0,
                                                 OP.mult, OP.add)
                            v.tensor_tensor_scan(cdc, dcy, zer[:, :], 1.0,
                                                 OP.mult, OP.add)
                            v.tensor_scalar(tmpv[:, :], X, p_dp[:, ct:ct + 1],
                                            None, OP.mult)
                            v.scalar_tensor_tensor(tmpv[:, :], stt,
                                                   p_cp[:, ct:ct + 1],
                                                   tmpv[:, :], OP.mult,
                                                   OP.add)
                            v.tensor_mul(comb[:, ct, :], tmpv[:, :], gt)
                            v.scalar_tensor_tensor(cgb[:, it % 2, k, :], cdc,
                                                   p_cp[:, ct:ct + 1], gt,
                                                   OP.mult, OP.mult)
                            v.tensor_copy(agb[:, k:k + 1], cdc[:, T - 1:T])
                            vlink(v, "v")
                            v.tensor_copy(agb[:, GRP + k:GRP + k + 1],
                                          stt[:, T - 1:T])
                            inc(v.last, "v", 1)
                        prev_scan_mark = C["v"]
                        prev_drv_mark = C["v"] - 1

                        if db_decay is not None:
                            @SEC('gpsimd')
                            def _(g, ct=ct, sm=prev_scan_mark):
                                g.wait_ge(S["v"], sm)
                                inc(g.dma_start(
                                    out=db_decay[ct * 128:(ct + 1) * 128, :],
                                    in_=scr[:, 0, :]), "sdbg", 16)
                                inc(g.dma_start(
                                    out=db_states[ct * 128:(ct + 1) * 128, :],
                                    in_=scr[:, 2, :]), "sdbg", 16)
                                wall(g, ["sdbg"])

                    @SEC('gpsimd')
                    def _(g, it=it, sm=prev_scan_mark):
                        g.wait_ge(S["v"], sm)
                        inc(g.dma_start(out=d_ag_in[it, :].rearrange(
                            "(p j) -> p j", p=128), in_=agb[:, :]), "agi", 16)
                        g.wait_ge(S["agi"], C["agi"])
                        cc = g.collective_compute(
                            "AllGather", OP.bypass,
                            replica_groups=[[0, 1, 2, 3], [4, 5, 6, 7]],
                            ins=[d_ag_in[it, :]],
                            outs=[d_ag_out[it, :]])
                        inc(cc, "cc%d" % it, 1)

                    @SEC('sync')
                    def _(sy, it=it):
                        sy.wait_ge(S["cc%d" % it], 1)
                        sy.wait_ge(S["v"], corr_marks.get(it - 1, 0))
                        inc(sy.dma_start(out=agot[:, :].rearrange(
                            "p (r j) -> p r j", r=4),
                            in_=d_ag_out[it, :].rearrange(
                                "(r p j) -> p r j", r=4, p=128)), "ago", 16)
                    pend_ag.append((it, C["ago"]))

                while pend_ag and (len(pend_ag) > 1 or it == ng_lim):
                    git, mark = pend_ag.pop(0)

                    @SEC('vector')
                    def _(v, git=git, mark=mark):
                        v.wait_ge(S["ago"], mark)
                        v.memset(sall[:, :], 0.0)
                        vlink(v, "v")
                        for r in range(4):
                            dcol = agot[:, r * 8:r * 8 + GRP]
                            fcol = agot[:, r * 8 + GRP:r * 8 + 8]
                            v.tensor_scalar(tmpv[:, 0:GRP], dcol,
                                            p_mm[:, r:r + 1],
                                            p_m1[:, r:r + 1],
                                            OP.mult, OP.add)
                            v.tensor_scalar(tmpv[:, GRP:2 * GRP], fcol,
                                            p_mm[:, r:r + 1], None, OP.mult)
                            vlink(v, "v")
                            v.tensor_mul(sall[:, :], sall[:, :],
                                         tmpv[:, 0:GRP])
                            vlink(v, "v")
                            v.tensor_add(sall[:, :], sall[:, :],
                                         tmpv[:, GRP:2 * GRP])
                            vlink(v, "v")
                        for k in range(GRP):
                            ct = git * GRP + k
                            v.scalar_tensor_tensor(
                                comb[:, ct, :], cgb[:, git % 2, k, :],
                                sall[:, k:k + 1], comb[:, ct, :],
                                OP.mult, OP.add)
                        inc(v.last, "v", 1)
                    corr_marks[git] = C["v"]

                    if db_comb is not None:
                        @SEC('gpsimd')
                        def _(g, git=git, cm=corr_marks[git]):
                            g.wait_ge(S["v"], cm)
                            for k in range(GRP):
                                ct = git * GRP + k
                                inc(g.dma_start(
                                    out=db_comb[ct * 128:(ct + 1) * 128, :],
                                    in_=comb[:, ct, :]), "sdbg", 16)
                            wall(g, ["sdbg"])

        @SEC('tensor')
        def _(t):
            wall(t, ["v", "sc"])
            inc(t.last, "mm", 1)
        barA = C["mm"]
        scope_x.close()
        scope_w = ExitStack()
        wslg = scope_w.enter_context(
            nc.sbuf_tensor("wslg", [128, 2, 2048], F32, side="right"))
        wq8g = scope_w.enter_context(
            nc.sbuf_tensor("wq8g", [128, 2048], FP8, side="right"))
        wdqg = scope_w.enter_context(
            nc.sbuf_tensor("wdqg", [128, 2048], BF16, side="right"))

        if upto == "A":
            with ExitStack() as pt:
                ot = pt.enter_context(nc.sbuf_tensor("ot", [128, T], F32))

                @SEC('gpsimd')
                def _(g):
                    g.wait_ge(S["mm"], barA)
                    for ct in range(min(ng_lim * GRP, 8)):
                        inc(g.dma_start(out=ot[:, :], in_=comb[:, ct, :]),
                            "outw", 16)
                        g.wait_ge(S["outw"], C["outw"])
                        inc(g.dma_start(
                            out=d_out[(ct % 8) * 128:(ct % 8) * 128 + 128,
                                      0:T], in_=ot[:, :]), "outw", 16)
                    g.wait_ge(S["outw"], C["outw"])
            return nc

        qxT = scope_q.enter_context(
            nc.sbuf_tensor("qxT", [128, ND, T], BF16))
        pb = ExitStack()
        with pb:
            eb = pb.enter_context
            woutb = eb(nc.sbuf_tensor("woutb", [128, 2, NCT * 256], BF16))
            xres = eb(nc.sbuf_tensor("xres", [128, 2, 256], F32))
            r1c = eb(nc.sbuf_tensor("r1c", [128, 2, 256], F32))
            r1t = eb(nc.sbuf_tensor("r1t", [128, 2, D], F32))
            x2t = eb(nc.sbuf_tensor("x2t", [128, 2, D], F32))
            qf8x = eb(nc.sbuf_tensor("qf8x", [128, D], FP8))
            qxdq = eb(nc.sbuf_tensor("qxdq", [128, D], BF16))
            sqs = eb(nc.sbuf_tensor("sqs", [128, D], F32))
            ps_o = eb(nc.psum_tensor("ps_o", [128, 1024], F32))
            ps_qt = eb(nc.psum_tensor("ps_qt", [128, 512], BF16))

            wgu_slabs = [(d_wg, d_wgq, "wgqw", sl) for sl in range(NIT)] + \
                        [(d_wu, d_wuq, "wuqw", sl) for sl in range(NIT)]
            wgu_state = {"i": 0, "cons": {0: 0, 1: 0}}

            def emit_wgu(n):
                for _ in range(n):
                    if wgu_state["i"] >= len(wgu_slabs):
                        return
                    idx = wgu_state["i"]
                    wgu_state["i"] += 1
                    srcw, dstw, dsem, sl = wgu_slabs[idx]
                    pk = idx % 2
                    wqs = "wq%d" % pk
                    cb = 20 + 4 * pk

                    @SEC('sync')
                    def _(sy, srcw=srcw, sl=sl, pk=pk, wqs=wqs, idx=idx):
                        sy.wait_ge(S["v"], wgu_state["cons"][pk])
                        inc(sy.dma_start(out=wslg[:, pk, :],
                                         in_=srcw[sl * 128:(sl + 1) * 128, :]),
                            wqs, 16)
                    slm = C[wqs]

                    @SEC('vector')
                    def _(v, pk=pk, slm=slm, wqs=wqs, cb=cb):
                        v.wait_ge(S[wqs], slm)
                        v.reduce_max(cols[:, cb:cb + 1], wslg[:, pk, :], AX,
                                     apply_absolute_value=True)
                        vlink(v, "v")
                        v.reciprocal(cols[:, cb + 1:cb + 2],
                                     cols[:, cb:cb + 1])
                        vlink(v, "v")
                        v.tensor_scalar(cols[:, cb + 2:cb + 3],
                                        cols[:, cb + 1:cb + 2], 224.0, None,
                                        OP.mult)
                        vlink(v, "v")
                        v.tensor_scalar(cols[:, cb + 3:cb + 4],
                                        cols[:, cb:cb + 1], 1.0 / 224.0,
                                        None, OP.mult)
                        vlink(v, "v")
                    qsm = C["v"]

                    @SEC('vector')
                    def _(v, pk=pk, qsm=qsm, cb=cb):
                        v.wait_ge(S["gcq"], wgu_state.get("dqA", 0))
                        v.tensor_scalar(wq8g[:, :], wslg[:, pk, :],
                                        cols[:, cb + 2:cb + 3], None, OP.mult)
                        inc(v.last, "v", 1)
                    q8m = C["v"]
                    wgu_state["cons"][pk] = C["v"]

                    @SEC('gpsimd')
                    def _(g, pk=pk, q8m=q8m, cb=cb, dsem=dsem, dstw=dstw,
                          sl=sl):
                        g.wait_ge(S["v"], q8m)
                        g.wait_ge(S["wgqw"], C["wgqw"])
                        g.wait_ge(S["wuqw"], C["wuqw"])
                        g.tensor_scalar(wdqg[:, :], wq8g[:, :],
                                        cols[:, cb + 3:cb + 4], None, OP.mult)
                        inc(g.last, "gcq", 1)
                        inc(g.dma_start(out=dstw[sl * 128:(sl + 1) * 128, :],
                                        in_=wdqg[:, :]), dsem, 16)
                    wgu_state["dqA"] = C["gcq"]

            def wout_load(g, dch):
                swo = "swo%d" % (dch % 2)
                inc(g.dma_start(
                    out=woutb[:, dch % 2, :].rearrange(
                        "p (a b) -> p a b", a=NCT),
                    in_=d_wout[:, dch * 256:(dch + 1) * 256].rearrange(
                        "(a p) c -> p a c", p=128)), swo, 16)

            @SEC('gpsimd')
            def _(g):
                g.wait_ge(S["mm"], barA)
                wout_load(g, 0)
            wo_mark = {0: C["swo0"]}

            evo_mark = {0: 0, 1: 0}
            r1spill = {0: 0, 1: 0}
            for dch in range(8):
                emit_wgu(11)
                if dch + 1 < 8:
                    @SEC('gpsimd')
                    def _(g, dch=dch):
                        g.wait_ge(S["mm"], C["mm"])
                        wout_load(g, dch + 1)
                    wo_mark[dch + 1] = C["swo%d" % ((dch + 1) % 2)]

                for tt in range(NTT):
                    bk = tt % 2
                    sxr = "sxr%d" % bk

                    @SEC('sync')
                    def _(sy, dch=dch, tt=tt, bk=bk, sxr=sxr):
                        sy.wait_ge(S["mm"], barA)
                        sy.wait_ge(S["v"], evo_mark[bk])
                        inc(sy.dma_start(
                            out=xres[:, bk, :],
                            in_=d_x[3 + tt * 128:3 + (tt + 1) * 128,
                                    dch * 256:(dch + 1) * 256]), sxr, 16)
                    xr_mark = C[sxr]

                    @SEC('tensor')
                    def _(t, dch=dch, tt=tt, bk=bk):
                        t.wait_ge(S["swo%d" % (dch % 2)], wo_mark[dch])
                        if evo_mark[bk]:
                            t.wait_ge(S["v"], evo_mark[bk])
                        for ct in range(NCT):
                            t.matmul(ps_o[:, bk * 512:bk * 512 + 256],
                                     comb[:, ct, tt * 128:(tt + 1) * 128],
                                     woutb[:, dch % 2,
                                           ct * 256:(ct + 1) * 256],
                                     start=ct == 0, stop=ct == NCT - 1)
                        inc(t.last, "mm", 1)
                    o_mark = C["mm"]

                    @SEC('vector')
                    def _(v, bk=bk, o_mark=o_mark, xr_mark=xr_mark, sxr=sxr,
                          spm=None):
                        v.wait_ge(S["mm"], o_mark)
                        v.wait_ge(S[sxr], xr_mark)
                        v.wait_ge(S["r1w%d" % bk], r1spill.get(bk, 0))
                        v.tensor_add(r1c[:, bk, :],
                                     ps_o[:, bk * 512:bk * 512 + 256],
                                     xres[:, bk, :])
                        inc(v.last, "v", 1)
                    evo_mark[bk] = C["v"]
                    r1c_mark = C["v"]

                    @SEC('gpsimd')
                    def _(g, dch=dch, tt=tt, bk=bk, r1c_mark=r1c_mark):
                        g.wait_ge(S["v"], r1c_mark)
                        inc(g.dma_start(
                            out=d_r1[tt * 128:(tt + 1) * 128,
                                     dch * 256:(dch + 1) * 256],
                            in_=r1c[:, bk, :]), "r1w%d" % bk, 16)
                    r1spill[bk] = C["r1w%d" % bk]
            r1w_done = (C["r1w0"], C["r1w1"])

            if db_r1 is not None:
                @SEC('gpsimd')
                def _(g):
                    g.wait_ge(S["r1w0"], r1w_done[0])
                    g.wait_ge(S["r1w1"], r1w_done[1])
                    inc(g.dma_start(out=db_r1, in_=d_r1), "sdbg", 16)
                    wall(g, ["sdbg"])

            x2cons = {0: 0, 1: 0}
            x2spill = {0: 0, 1: 0}
            for tt in range(NTT):
                bk = tt % 2
                r1r = "r1r%d" % bk

                @SEC('sync')
                def _(sy, tt=tt, bk=bk, r1r=r1r):
                    sy.wait_ge(S["r1w0"], r1w_done[0])
                    sy.wait_ge(S["r1w1"], r1w_done[1])
                    sy.wait_ge(S["v"], x2cons[bk])
                    inc(sy.dma_start(out=r1t[:, bk, :],
                                     in_=d_r1[tt * 128:(tt + 1) * 128, :]),
                        r1r, 16)
                r1t_mark = C[r1r]

                @SEC('scalar')
                def _(s, bk=bk, r1t_mark=r1t_mark, r1r=r1r):
                    s.wait_ge(S[r1r], r1t_mark)
                    s.wait_ge(S["v"], C["v"])
                    s.activation(sqs[:, :], r1t[:, bk, :], AF.Square,
                                 accum_out=cols[:, 0:1])
                    inc(s.last, "sc", 1)
                sq_mark = C["sc"]

                @SEC('vector')
                def _(v, bk=bk, sq_mark=sq_mark):
                    v.wait_ge(S["sc"], sq_mark)
                    v.reduce_sum(cols[:, 1:2], r1t[:, bk, :], AX)
                    vlink(v, "v")
                    v.tensor_scalar(cols[:, 2:3], cols[:, 1:2], 1.0 / D,
                                    None, OP.mult)
                    v.tensor_scalar(cols[:, 3:4], cols[:, 0:1], 1.0 / D,
                                    None, OP.mult)
                    vlink(v, "v")
                    v.tensor_mul(cols[:, 4:5], cols[:, 2:3], cols[:, 2:3])
                    vlink(v, "v")
                    v.tensor_sub(cols[:, 5:6], cols[:, 3:4], cols[:, 4:5])
                    vlink(v, "v")
                    v.tensor_scalar(cols[:, 5:6], cols[:, 5:6], 1.0, EPS,
                                    OP.mult, OP.add)
                    inc(v.last, "v", 1)
                var_mark = C["v"]

                @SEC('scalar')
                def _(s, var_mark=var_mark):
                    s.wait_ge(S["v"], var_mark)
                    s.activation(cols[:, 6:7], cols[:, 5:6], AF.Ln)
                    inc(s.last, "sc", 1)
                ln_mark = C["sc"]

                @SEC('vector')
                def _(v, ln_mark=ln_mark):
                    v.wait_ge(S["sc"], ln_mark)
                    v.tensor_scalar(cols[:, 6:7], cols[:, 6:7], -0.5, None,
                                    OP.mult)
                    inc(v.last, "v", 1)
                lh_mark = C["v"]

                @SEC('scalar')
                def _(s, lh_mark=lh_mark):
                    s.wait_ge(S["v"], lh_mark)
                    s.activation(cols[:, 6:7], cols[:, 6:7], AF.Exp)
                    inc(s.last, "sc", 1)
                rstd_mark = C["sc"]

                @SEC('vector')
                def _(v, rstd_mark=rstd_mark):
                    v.wait_ge(S["sc"], rstd_mark)
                    v.tensor_mul(cols[:, 7:8], cols[:, 2:3], cols[:, 6:7])
                    vlink(v, "v")
                    v.tensor_scalar(cols[:, 7:8], cols[:, 7:8], -1.0, None,
                                    OP.mult)
                    inc(v.last, "v", 1)
                nb_mark = C["v"]

                @SEC('vector')
                def _(v, bk=bk, nb_mark=nb_mark):
                    v.wait_ge(S["v"], nb_mark)
                    v.wait_ge(S["x2w%d" % bk], x2spill[bk])
                    v.tensor_scalar(x2t[:, bk, :], r1t[:, bk, :],
                                    cols[:, 6:7], cols[:, 7:8],
                                    OP.mult, OP.add)
                    inc(v.last, "v", 1)
                x2_mark = C["v"]

                @SEC('gpsimd')
                def _(g, tt=tt, bk=bk, x2_mark=x2_mark):
                    g.wait_ge(S["v"], x2_mark)
                    inc(g.dma_start(out=d_x2[tt * 128:(tt + 1) * 128, :],
                                    in_=x2t[:, bk, :]), "x2w%d" % bk, 16)
                    if db_cols is not None:
                        inc(g.dma_start(
                            out=db_cols[tt * 128:(tt + 1) * 128, :],
                            in_=cols[:, 0:16]), "sdbg", 16)
                        wall(g, ["sdbg"])
                x2spill[bk] = C["x2w%d" % bk]

                @SEC('vector')
                def _(v, bk=bk, x2_mark=x2_mark):
                    v.reduce_max(cols[:, 8:9], x2t[:, bk, :], AX,
                                 apply_absolute_value=True)
                    vlink(v, "v")
                    v.reciprocal(cols[:, 9:10], cols[:, 8:9])
                    vlink(v, "v")
                    v.tensor_scalar(cols[:, 10:11], cols[:, 9:10], 224.0,
                                    None, OP.mult)
                    vlink(v, "v")
                    v.tensor_scalar(cols[:, 11:12], cols[:, 8:9], 1.0 / 224.0,
                                    None, OP.mult)
                    vlink(v, "v")
                qs_mark = C["v"]

                @SEC('vector')
                def _(v, bk=bk, qs_mark=qs_mark):
                    v.tensor_scalar(qf8x[:, :], x2t[:, bk, :],
                                    cols[:, 10:11], None, OP.mult)
                    inc(v.last, "v", 1)
                q8_mark = C["v"]
                x2cons[bk] = C["v"]

                @SEC('vector')
                def _(v, q8_mark=q8_mark):
                    v.tensor_scalar(qxdq[:, :], qf8x[:, :], cols[:, 11:12],
                                    None, OP.mult)
                    inc(v.last, "v", 1)
                qdq_mark = C["v"]

                if db_qxdq is not None:
                    @SEC('gpsimd')
                    def _(g, tt=tt, qdq_mark=qdq_mark):
                        g.wait_ge(S["v"], qdq_mark)
                        inc(g.dma_start(
                            out=db_qxdq[tt * 128:(tt + 1) * 128, :],
                            in_=qxdq[:, :]), "sdbg", 16)
                        wall(g, ["sdbg"])

                for d4 in range(4):
                    @SEC('tensor')
                    def _(t, d4=d4, qdq_mark=qdq_mark):
                        t.wait_ge(S["v"], qdq_mark)
                        t.wait_ge(S["v"], C["v"])
                        last = None
                        for kk in range(4):
                            di = d4 * 4 + kk
                            last = t.transpose(
                                ps_qt[:, kk * 128:(kk + 1) * 128],
                                qxdq[:, di * 128:(di + 1) * 128],
                                p_idb[:, :])
                        inc(last, "mm", 1)
                    tr_mark = C["mm"]

                    @SEC('vector')
                    def _(v, tt=tt, d4=d4, tr_mark=tr_mark):
                        v.wait_ge(S["mm"], tr_mark)
                        v.tensor_copy(
                            qxT[:, d4 * 4:(d4 + 1) * 4,
                                tt * 128:(tt + 1) * 128],
                            ps_qt[:, 0:512].rearrange("p (a b) -> p a b",
                                                      a=4))
                        inc(v.last, "v", 1)

            emit_wgu(10 ** 9)

            if db_x2 is not None:
                @SEC('gpsimd')
                def _(g):
                    wall(g, ["x2w0", "x2w1"])
                    inc(g.dma_start(out=db_x2, in_=d_x2), "sdbg", 16)
                    wall(g, ["sdbg"])

            @SEC('tensor')
            def _(t):
                wall(t, ["v", "sc", "gcq"])
                inc(t.last, "mm", 1)
            barB = C["mm"]

            if upto == "mixer":
                with ExitStack() as pt:
                    ot = pt.enter_context(nc.sbuf_tensor("ot", [128, D], F32))

                    @SEC('gpsimd')
                    def _(g):
                        g.wait_ge(S["mm"], barB)
                        wall(g, ["x2w0", "x2w1"])
                        for tt in range(NTT):
                            inc(g.dma_start(
                                out=ot[:, :],
                                in_=d_x2[tt * 128:(tt + 1) * 128, :]),
                                "outw", 16)
                            g.wait_ge(S["outw"], C["outw"])
                            inc(g.dma_start(
                                out=d_out[tt * 128:(tt + 1) * 128, :],
                                in_=ot[:, :]), "outw", 16)
                        g.wait_ge(S["outw"], C["outw"])
                return nc
        scope_w.close()
        scope_c.close()

        ICH = [(0, 1024), (1024, 1024), (2048, 1024), (3072, 1024),
               (4096, 1024), (5120, 512)]
        pc = ExitStack()
        with pc:
            ec = pc.enter_context
            wmv = ec(nc.sbuf_tensor("wmv", [128, 2, 2, ND * 1024], BF16))
            gs_t = ec(nc.sbuf_tensor("gs_t", [128, 1024], F32))
            tpc = ec(nc.sbuf_tensor("tpc", [128, 1024], F32))
            wsld = ec(nc.sbuf_tensor("wsld", [128, 2, 2816], F32))
            wq8d = ec(nc.sbuf_tensor("wq8d", [128, 2816], FP8))
            wdqd = ec(nc.sbuf_tensor("wdqd", [128, 2816], BF16))
            ps_g = ec(nc.psum_tensor("ps_g", [128, 2, 1024], F32))
            ps_u = ec(nc.psum_tensor("ps_u", [128, 2, 1024], F32))

            wd_state = {"sl": 0, "cons": 0, "dq": 0}

            def emit_wd(n):
                for _ in range(n):
                    if wd_state["sl"] >= ND:
                        return
                    sl = wd_state["sl"]
                    wd_state["sl"] += 1
                    adn = "ad%d" % (sl % 2)

                    @SEC('scalar')
                    def _(s, sl=sl, adn=adn):
                        s.wait_ge(S["mm"], barB)
                        s.wait_ge(S["v"], wd_state["cons"])
                        for h in range(2):
                            inc(s.dma_start(
                                out=wsld[:, h, :],
                                in_=d_wd[sl * 128:(sl + 1) * 128,
                                         h * 2816:(h + 1) * 2816]), adn, 16)
                    slm = C[adn]

                    @SEC('vector')
                    def _(v, slm=slm, adn=adn):
                        v.wait_ge(S[adn], slm)
                        v.reduce_max(cols[:, 28:29], wsld[:, 0, :], AX,
                                     apply_absolute_value=True)
                        v.reduce_max(cols[:, 29:30], wsld[:, 1, :], AX,
                                     apply_absolute_value=True)
                        vlink(v, "v")
                        v.tensor_max(cols[:, 28:29], cols[:, 28:29],
                                     cols[:, 29:30])
                        vlink(v, "v")
                        v.reciprocal(cols[:, 30:31], cols[:, 28:29])
                        vlink(v, "v")
                        v.tensor_scalar(cols[:, 31:32], cols[:, 30:31],
                                        224.0, None, OP.mult)
                        vlink(v, "v")
                        v.tensor_scalar(cols[:, 29:30], cols[:, 28:29],
                                        1.0 / 224.0, None, OP.mult)
                        vlink(v, "v")
                    qsm = C["v"]

                    for h in range(2):
                        @SEC('vector')
                        def _(v, h=h, qsm=qsm):
                            v.wait_ge(S["gcq"], wd_state["dq"])
                            v.tensor_scalar(wq8d[:, :], wsld[:, h, :],
                                            cols[:, 31:32], None, OP.mult)
                            inc(v.last, "v", 1)
                        q8m = C["v"]
                        if h == 1:
                            wd_state["cons"] = C["v"]

                        @SEC('gpsimd')
                        def _(g, sl=sl, h=h, q8m=q8m):
                            g.wait_ge(S["v"], q8m)
                            g.wait_ge(S["wdqw"], C["wdqw"])
                            g.tensor_scalar(wdqd[:, :], wq8d[:, :],
                                            cols[:, 29:30], None, OP.mult)
                            inc(g.last, "gcq", 1)
                            inc(g.dma_start(
                                out=d_wdq[sl * 128:(sl + 1) * 128,
                                          h * 2816:(h + 1) * 2816],
                                in_=wdqd[:, :]), "wdqw", 16)
                        wd_state["dq"] = C["gcq"]

            @SEC('vector')
            def _(v):
                v.wait_ge(S["mm"], barB)
                v.memset(rmac[:, :], 0.0)
                inc(v.last, "v", 1)

            def wmv_load(sy, ic, i0, iw):
                swm = "swm%d" % (ic % 2)
                for wi, dd in ((0, d_wgq), (1, d_wuq)):
                    for di in range(ND):
                        inc(sy.dma_start(
                            out=wmv[:, ic % 2, wi,
                                    di * 1024:di * 1024 + iw],
                            in_=dd[i0:i0 + iw, di * 128:(di + 1) * 128],
                            transpose=True), swm, 16)

            @SEC('sync')
            def _(sy):
                sy.wait_ge(S["mm"], barB)
                wall(sy, ["wgqw", "wuqw"])
                wmv_load(sy, 0, ICH[0][0], ICH[0][1])
            wm_mark = {0: C["swm0"]}

            ev_mark2 = {0: [0, 0], 1: [0, 0]}
            gscons = [0]
            for ic, (i0, iw) in enumerate(ICH):
                emit_wd(3)
                if ic + 1 < len(ICH):
                    @SEC('sync')
                    def _(sy, ic=ic):
                        sy.wait_ge(S["mm"], C["mm"])
                        wmv_load(sy, ic + 1, ICH[ic + 1][0], ICH[ic + 1][1])
                    wm_mark[ic + 1] = C["swm%d" % ((ic + 1) % 2)]

                for tt in range(NTT):
                    pp = (ic * NTT + tt) % 2

                    @SEC('tensor')
                    def _(t, ic=ic, tt=tt, iw=iw, pp=pp):
                        t.wait_ge(S["swm%d" % (ic % 2)], wm_mark[ic])
                        t.wait_ge(S["v"], ev_mark2[pp][0])
                        t.wait_ge(S["sc"], ev_mark2[pp][1])
                        for half in range(2 if iw == 1024 else 1):
                            for di in range(ND):
                                qx = qxT[:, di, tt * 128:(tt + 1) * 128]
                                base = di * 1024 + half * 512
                                t.matmul(ps_g[:, pp,
                                              half * 512:half * 512 + 512],
                                         qx, wmv[:, ic % 2, 0,
                                                 base:base + 512],
                                         start=di == 0, stop=di == ND - 1)
                                t.matmul(ps_u[:, pp,
                                              half * 512:half * 512 + 512],
                                         qx, wmv[:, ic % 2, 1,
                                                 base:base + 512],
                                         start=di == 0, stop=di == ND - 1)
                        inc(t.last, "mm", 1)
                    gu_mark = C["mm"]

                    @SEC('scalar')
                    def _(s, iw=iw, gu_mark=gu_mark, pp=pp):
                        s.wait_ge(S["mm"], gu_mark)
                        s.wait_ge(S["v"], gscons[0])
                        s.activation(gs_t[:, 0:iw], ps_g[:, pp, 0:iw],
                                     AF.Silu)
                        inc(s.last, "sc", 1)
                    ev_mark2[pp][1] = C["sc"]
                    gs_mark = C["sc"]

                    @SEC('vector')
                    def _(v, tt=tt, iw=iw, gs_mark=gs_mark, pp=pp):
                        v.wait_ge(S["sc"], gs_mark)
                        v.wait_ge(S["tw"], C["tw"])
                        v.tensor_mul(tpc[:, 0:iw], gs_t[:, 0:iw],
                                     ps_u[:, pp, 0:iw])
                        v.reduce_max(cols[:, 36:37], tpc[:, 0:iw], AX,
                                     apply_absolute_value=True)
                        vlink(v, "v")
                        v.tensor_max(rmac[:, tt:tt + 1], rmac[:, tt:tt + 1],
                                     cols[:, 36:37])
                        inc(v.last, "v", 1)
                    ev_mark2[pp][0] = C["v"]
                    gscons[0] = C["v"]
                    tp_mark = C["v"]

                    @SEC('gpsimd')
                    def _(g, tt=tt, i0=i0, iw=iw, tp_mark=tp_mark):
                        g.wait_ge(S["v"], tp_mark)
                        inc(g.dma_start(out=d_t[tt * 128:(tt + 1) * 128,
                                                i0:i0 + iw],
                                        in_=tpc[:, 0:iw]), "tw", 16)
            emit_wd(10 ** 9)
            if db_wgq is not None:
                @SEC('gpsimd')
                def _(g):
                    wall(g, ["wgqw"])
                    inc(g.dma_start(out=db_wgq, in_=d_wgq), "sdbg", 16)
                    wall(g, ["sdbg"])
            tw_done = C["tw"]

            if db_t is not None:
                @SEC('gpsimd')
                def _(g):
                    g.wait_ge(S["tw"], tw_done)
                    inc(g.dma_start(out=db_t, in_=d_t), "sdbg", 16)
                    wall(g, ["sdbg"])

        @SEC('tensor')
        def _(t):
            wall(t, ["v", "sc"])
            inc(t.last, "mm", 1)
        barC = C["mm"]
        scope_q.close()

        pc2 = ExitStack()
        with pc2:
            ec2 = pc2.enter_context
            tbuf = ec2(nc.sbuf_tensor("tbuf", [128, 2, IF], F32))
            qf8t = ec2(nc.sbuf_tensor("qf8t", [128, IF], FP8))
            qtdq = ec2(nc.sbuf_tensor("qtdq", [128, IF], BF16))

            tcons = {0: 0, 1: 0}
            for tt in range(NTT):
                bk = tt % 2
                trs = "tr%d" % bk

                @SEC('sync')
                def _(sy, tt=tt, bk=bk, trs=trs):
                    if tt == 0:
                        sy.wait_ge(S["mm"], barC)
                    sy.wait_ge(S["tw"], tw_done)
                    sy.wait_ge(S["v"], tcons[bk])
                    inc(sy.dma_start(out=tbuf[:, bk, :],
                                     in_=d_t[tt * 128:(tt + 1) * 128, :]),
                        trs, 16)
                tb_mark = C[trs]

                @SEC('vector')
                def _(v, tt=tt, tb_mark=tb_mark, trs=trs):
                    v.wait_ge(S["mm"], barC)
                    v.wait_ge(S[trs], tb_mark)
                    v.reciprocal(cols[:, 31:32], rmac[:, tt:tt + 1])
                    vlink(v, "v")
                    v.tensor_scalar(cols[:, 32:33], cols[:, 31:32], 224.0,
                                    None, OP.mult)
                    vlink(v, "v")
                    v.tensor_scalar(cols[:, 33:34], rmac[:, tt:tt + 1],
                                    1.0 / 224.0, None, OP.mult)
                    vlink(v, "v")
                qs2_mark = C["v"]

                @SEC('vector')
                def _(v, bk=bk, qs2_mark=qs2_mark):
                    v.tensor_scalar(qf8t[:, :], tbuf[:, bk, :],
                                    cols[:, 32:33], None, OP.mult)
                    inc(v.last, "v", 1)
                q8t_mark = C["v"]
                tcons[bk] = C["v"]

                @SEC('vector')
                def _(v, q8t_mark=q8t_mark):
                    v.wait_ge(S["qdqw"], C["qdqw"])
                    v.tensor_scalar(qtdq[:, :], qf8t[:, :], cols[:, 33:34],
                                    None, OP.mult)
                    inc(v.last, "v", 1)
                qdqt_mark = C["v"]

                @SEC('gpsimd')
                def _(g, tt=tt, qdqt_mark=qdqt_mark):
                    g.wait_ge(S["v"], qdqt_mark)
                    inc(g.dma_start(out=d_qdq[tt * 128:(tt + 1) * 128, :],
                                    in_=qtdq[:, :]), "qdqw", 16)
                    if db_qtdq is not None:
                        g.wait_ge(S["qdqw"], C["qdqw"])
                        inc(g.dma_start(
                            out=db_qtdq[tt * 128:(tt + 1) * 128, :],
                            in_=qtdq[:, :]), "sdbg", 16)
                        wall(g, ["sdbg"])
            qdq_done = C["qdqw"]

        @SEC('tensor')
        def _(t):
            wall(t, ["v", "sc"])
            inc(t.last, "mm", 1)
        barC2 = C["mm"]

        pd = ExitStack()
        with pd:
            ed = pd.enter_context
            qdT = ed(nc.sbuf_tensor("qdT", [128, NIT, T], BF16))
            wdm = ed(nc.sbuf_tensor("wdm", [128, 2, NIT, 512], BF16))
            x2c = ed(nc.sbuf_tensor("x2c", [128, 2, 512], F32))
            r2c = ed(nc.sbuf_tensor("r2c", [128, 2, 512], F32))
            r2t = ed(nc.sbuf_tensor("r2t", [128, 1, D], F32))
            otb = ed(nc.sbuf_tensor("otb", [128, 1, D], F32))
            ps_d = ed(nc.psum_tensor("ps_d", [128, NTT * 512], F32))

            @SEC('sync')
            def _(sy):
                sy.wait_ge(S["mm"], barC2)
                sy.wait_ge(S["qdqw"], qdq_done)
                for i in range(NIT):
                    inc(sy.dma_start(out=qdT[:, i, :],
                                     in_=d_qdq[:, i * 128:(i + 1) * 128],
                                     transpose=True), "qdT", 16)
            qdT_mark = C["qdT"]

            def wdm_load(sy, dch):
                sem = "wdm%d" % (dch % 2)
                for i in range(NIT):
                    inc(sy.dma_start(
                        out=wdm[:, dch % 2, i, :],
                        in_=d_wdq[dch * 512:(dch + 1) * 512,
                                  i * 128:(i + 1) * 128],
                        transpose=True), sem, 16)
                return C[sem]

            @SEC('sync')
            def _(sy):
                wall(sy, ["wdqw"])
                wdm_load(sy, 0)
                wdm_load(sy, 1)
            wdm_marks = {0: C["wdm0"], 1: C["wdm1"]}

            dmm_marks = {}
            r2spill = {0: 0, 1: 0}
            for dch in range(4):
                @SEC('tensor')
                def _(t, dch=dch):
                    t.wait_ge(S["qdT"], qdT_mark)
                    t.wait_ge(S["wdm%d" % (dch % 2)], wdm_marks[dch])
                    t.wait_ge(S["v"], C["v"])
                    for i in range(NIT):
                        for tt in range(NTT):
                            t.matmul(ps_d[:, tt * 512:(tt + 1) * 512],
                                     qdT[:, i, tt * 128:(tt + 1) * 128],
                                     wdm[:, dch % 2, i, :],
                                     start=i == 0, stop=i == NIT - 1)
                    inc(t.last, "mm", 1)
                d_mark = C["mm"]
                dmm_marks[dch] = C["mm"]
                if dch + 2 < 4:
                    @SEC('sync')
                    def _(sy, dch=dch):
                        sy.wait_ge(S["mm"], dmm_marks[dch])
                        wdm_marks[dch + 2] = wdm_load(sy, dch + 2)

                for tt in range(NTT):
                    bk = tt % 2
                    sxc = "sx2c%d" % bk

                    @SEC('sync')
                    def _(sy, dch=dch, tt=tt, bk=bk, sxc=sxc):
                        sy.wait_ge(S["v"], C["v"])
                        wall(sy, ["x2w0", "x2w1"])
                        inc(sy.dma_start(
                            out=x2c[:, bk, :],
                            in_=d_x2[tt * 128:(tt + 1) * 128,
                                     dch * 512:(dch + 1) * 512]), sxc, 16)
                    x2c_mark = C[sxc]

                    @SEC('vector')
                    def _(v, tt=tt, bk=bk, d_mark=d_mark, x2c_mark=x2c_mark,
                          sxc=sxc):
                        v.wait_ge(S["mm"], d_mark)
                        v.wait_ge(S[sxc], x2c_mark)
                        v.wait_ge(S["r2w%d" % bk], r2spill[bk])
                        v.tensor_add(r2c[:, bk, :],
                                     ps_d[:, tt * 512:(tt + 1) * 512],
                                     x2c[:, bk, :])
                        inc(v.last, "v", 1)
                    r2c_mark = C["v"]

                    @SEC('gpsimd')
                    def _(g, dch=dch, tt=tt, bk=bk, r2c_mark=r2c_mark):
                        g.wait_ge(S["v"], r2c_mark)
                        inc(g.dma_start(
                            out=d_r2[tt * 128:(tt + 1) * 128,
                                     dch * 512:(dch + 1) * 512],
                            in_=r2c[:, bk, :]), "r2w%d" % bk, 16)
                    r2spill[bk] = C["r2w%d" % bk]
            r2w_done = (C["r2w0"], C["r2w1"])

            ocons = {0: 0, 1: 0}
            ospill = {0: 0, 1: 0}
            for tt in range(NTT):
                bk = tt % 2
                r2r = "r2r%d" % bk

                @SEC('sync')
                def _(sy, tt=tt, bk=bk, r2r=r2r):
                    sy.wait_ge(S["r2w0"], r2w_done[0])
                    sy.wait_ge(S["r2w1"], r2w_done[1])
                    sy.wait_ge(S["v"], ocons[0])
                    inc(sy.dma_start(out=r2t[:, 0, :],
                                     in_=d_r2[tt * 128:(tt + 1) * 128, :]),
                        r2r, 16)
                r2t_mark = C[r2r]

                @SEC('scalar')
                def _(s, bk=bk, r2t_mark=r2t_mark, r2r=r2r):
                    s.wait_ge(S[r2r], r2t_mark)
                    s.wait_ge(S["v"], C["v"])
                    s.wait_ge(S["outw"], ospill[0])
                    s.activation(otb[:, 0, :], r2t[:, 0, :], AF.Square,
                                 accum_out=cols[:, 40:41])
                    inc(s.last, "sc", 1)
                sq_mark = C["sc"]

                @SEC('vector')
                def _(v, bk=bk, sq_mark=sq_mark):
                    v.wait_ge(S["sc"], sq_mark)
                    v.reduce_sum(cols[:, 41:42], r2t[:, 0, :], AX)
                    vlink(v, "v")
                    v.tensor_scalar(cols[:, 42:43], cols[:, 41:42], 1.0 / D,
                                    None, OP.mult)
                    v.tensor_scalar(cols[:, 43:44], cols[:, 40:41], 1.0 / D,
                                    None, OP.mult)
                    vlink(v, "v")
                    v.tensor_mul(cols[:, 44:45], cols[:, 42:43],
                                 cols[:, 42:43])
                    vlink(v, "v")
                    v.tensor_sub(cols[:, 45:46], cols[:, 43:44],
                                 cols[:, 44:45])
                    vlink(v, "v")
                    v.tensor_scalar(cols[:, 45:46], cols[:, 45:46], 1.0, EPS,
                                    OP.mult, OP.add)
                    inc(v.last, "v", 1)
                var2_mark = C["v"]

                @SEC('scalar')
                def _(s, var2_mark=var2_mark):
                    s.wait_ge(S["v"], var2_mark)
                    s.activation(cols[:, 46:47], cols[:, 45:46], AF.Ln)
                    inc(s.last, "sc", 1)
                ln2_mark = C["sc"]

                @SEC('vector')
                def _(v, ln2_mark=ln2_mark):
                    v.wait_ge(S["sc"], ln2_mark)
                    v.tensor_scalar(cols[:, 46:47], cols[:, 46:47], -0.5,
                                    None, OP.mult)
                    inc(v.last, "v", 1)
                lh2_mark = C["v"]

                @SEC('scalar')
                def _(s, lh2_mark=lh2_mark):
                    s.wait_ge(S["v"], lh2_mark)
                    s.activation(cols[:, 46:47], cols[:, 46:47], AF.Exp)
                    inc(s.last, "sc", 1)
                rstd2_mark = C["sc"]

                @SEC('vector')
                def _(v, rstd2_mark=rstd2_mark):
                    v.wait_ge(S["sc"], rstd2_mark)
                    v.tensor_mul(cols[:, 47:48], cols[:, 42:43],
                                 cols[:, 46:47])
                    vlink(v, "v")
                    v.tensor_scalar(cols[:, 47:48], cols[:, 47:48], -1.0,
                                    None, OP.mult)
                    inc(v.last, "v", 1)
                nb2_mark = C["v"]

                @SEC('vector')
                def _(v, bk=bk, nb2_mark=nb2_mark):
                    v.wait_ge(S["v"], nb2_mark)
                    v.wait_ge(S["outw"], ospill[0])
                    v.tensor_scalar(otb[:, 0, :], r2t[:, 0, :],
                                    cols[:, 46:47], cols[:, 47:48],
                                    OP.mult, OP.add)
                    inc(v.last, "v", 1)
                ot_mark = C["v"]
                ocons[0] = C["v"]

                @SEC('gpsimd')
                def _(g, tt=tt, bk=bk, ot_mark=ot_mark):
                    g.wait_ge(S["v"], ot_mark)
                    inc(g.dma_start(out=d_out[tt * 128:(tt + 1) * 128, :],
                                    in_=otb[:, 0, :]), "outw", 16)
                ospill[0] = C["outw"]

            @SEC('gpsimd')
            def _(g):
                g.wait_ge(S["outw"], C["outw"])
                for dbp, srcp in ((db_er1, d_r1), (db_ex2, d_x2),
                                  (db_er2, d_r2)):
                    if dbp is not None:
                        inc(g.dma_start(out=dbp, in_=srcp), "sdbg", 16)
                if C["sdbg"]:
                    wall(g, ["sdbg"])

    return nc


def kernel(**inputs):
    if "main" not in _CACHE:
        _CACHE["main"] = build()
    nc = _CACHE["main"]
    s = _stage_shared(inputs)
    in_maps = [_stage_core(s, c) for c in range(8)]
    res = run_bass_kernel_spmd(nc, in_maps, core_ids=list(range(8)))
    out = np.zeros((2, 4096, 2048), np.float32)
    for c in range(8):
        out[c // 4, 1024 * (c % 4):1024 * (c % 4 + 1)] = res.results[c]["out"]
    return out


# version 08
# speedup vs baseline: 1.6348x; 1.6242x over previous
import numpy as np
from contextlib import ExitStack

import concourse.bass as bass
import concourse.mybir as mybir
from concourse.bass_utils import run_bass_kernel_spmd

F32 = mybir.dt.float32
F32R = mybir.dt.float32r
BF16 = mybir.dt.bfloat16
FP8 = mybir.dt.float8e4
AX = mybir.AxisListType.X
OP = mybir.AluOpType
AF = mybir.ActivationFunctionType

T = 1024
NTT = 8
D = 2048
ND = 16
INNER = 4096
NCT = 32
GRP = 4
NG = NCT // GRP
IF = 5632
NIT = 44
EPS = 1e-5

_CACHE = {}


def _stage_shared(inputs):
    g = lambda k: np.asarray(inputs[k], dtype=np.float32)
    s = {}
    s["x_full"] = g("L0_attn_hidden_states")
    s["w_in_T"] = np.ascontiguousarray(g("L0_attn_in_proj_w").T)
    s["w_gate_T"] = np.ascontiguousarray(g("L0_attn_gate_proj_w").T)
    s["w_out_T"] = np.ascontiguousarray(g("L0_attn_out_proj_w").T)
    s["w_g_nat"] = np.ascontiguousarray(g("L0_ffn_w_gate"))
    s["w_u_nat"] = np.ascontiguousarray(g("L0_ffn_w_up"))
    dn_idx = np.asarray(inputs["L0_ffn_down_reorder_index"]).astype(np.int64)
    s["w_d_eff"] = np.ascontiguousarray(g("L0_ffn_w_down")[:, np.argsort(dn_idx)])

    def chanmajor(a):
        return np.ascontiguousarray(a.reshape(NCT, 128).T)

    s["a_logt"] = chanmajor(g("L0_attn_A_log").reshape(-1))
    s["bp_t"] = chanmajor(g("L0_attn_Bp").reshape(-1))
    s["cp_t"] = chanmajor(g("L0_attn_Cp").reshape(-1))
    s["dp_t"] = chanmajor(g("L0_attn_Dp").reshape(-1))
    s["conv_bt"] = chanmajor(g("L0_attn_conv_b"))
    cw = g("L0_attn_conv_w")
    s["conv_wt"] = np.ascontiguousarray(
        cw.reshape(NCT, 128, 4).transpose(1, 0, 2).reshape(128, NCT * 4))
    dtw = g("L0_attn_dt_w")
    dtw_stat = np.zeros((128, NCT * 8), np.float32)
    sel_stat = np.zeros((8, GRP * 128), np.float32)
    for ct in range(NCT):
        j0 = 2 * (ct % GRP)
        dtw_stat[0:64, ct * 8 + j0] = dtw[2 * ct]
        dtw_stat[64:128, ct * 8 + j0 + 1] = dtw[2 * ct + 1]
    for kk in range(GRP):
        sel_stat[2 * kk, kk * 128:kk * 128 + 64] = 1.0
        sel_stat[2 * kk + 1, kk * 128 + 64:kk * 128 + 128] = 1.0
    s["dtw_stat"] = dtw_stat
    s["sel_stat"] = sel_stat
    s["dt_bt"] = np.ascontiguousarray(g("L0_attn_dt_b").reshape(NG, 8).T)
    s["ident_f"] = np.eye(128, dtype=np.float32)
    return s


def _stage_core(s, core):
    q = core % 4
    xb = s["x_full"][core // 4]
    if q == 0:
        xs = np.concatenate([np.zeros((3, D), np.float32), xb[0:1024]], 0)
    else:
        xs = xb[1024 * q - 3:1024 * (q + 1)]
    m = np.zeros((128, 4), np.float32)
    m[:, :q] = 1.0
    return dict(
        x_sl=np.ascontiguousarray(xs),
        w_in_T=s["w_in_T"], w_gate_T=s["w_gate_T"], w_out_T=s["w_out_T"],
        w_g_nat=s["w_g_nat"], w_u_nat=s["w_u_nat"], w_d_eff=s["w_d_eff"],
        a_logt=s["a_logt"], bp_t=s["bp_t"], cp_t=s["cp_t"], dp_t=s["dp_t"],
        conv_bt=s["conv_bt"], conv_wt=s["conv_wt"],
        dtw_stat=s["dtw_stat"], sel_stat=s["sel_stat"], dt_bt=s["dt_bt"],
        mask_m=np.ascontiguousarray(m),
        mask_1m=np.ascontiguousarray(1.0 - m),
        ident_f=s["ident_f"],
    )


def build(debug=(), upto="full", ng_lim=NG):
    nc = bass.Bass("TRN2", target_bir_lowering=False, debug=False,
                   detect_race_conditions=False)

    d_x = nc.dram_tensor("x_sl", [1027, D], F32, kind="ExternalInput").ap()
    d_win = nc.dram_tensor("w_in_T", [D, INNER], F32, kind="ExternalInput").ap()
    d_wgt = nc.dram_tensor("w_gate_T", [D, INNER], F32, kind="ExternalInput").ap()
    d_wout = nc.dram_tensor("w_out_T", [INNER, D], F32, kind="ExternalInput").ap()
    d_wg = nc.dram_tensor("w_g_nat", [IF, D], F32, kind="ExternalInput").ap()
    d_wu = nc.dram_tensor("w_u_nat", [IF, D], F32, kind="ExternalInput").ap()
    d_wd = nc.dram_tensor("w_d_eff", [D, IF], F32, kind="ExternalInput").ap()
    d_alog = nc.dram_tensor("a_logt", [128, NCT], F32, kind="ExternalInput").ap()
    d_bp = nc.dram_tensor("bp_t", [128, NCT], F32, kind="ExternalInput").ap()
    d_cp = nc.dram_tensor("cp_t", [128, NCT], F32, kind="ExternalInput").ap()
    d_dp = nc.dram_tensor("dp_t", [128, NCT], F32, kind="ExternalInput").ap()
    d_cb = nc.dram_tensor("conv_bt", [128, NCT], F32, kind="ExternalInput").ap()
    d_cw = nc.dram_tensor("conv_wt", [128, NCT * 4], F32, kind="ExternalInput").ap()
    d_dtw = nc.dram_tensor("dtw_stat", [128, NCT * 8], F32, kind="ExternalInput").ap()
    d_sel = nc.dram_tensor("sel_stat", [8, GRP * 128], F32, kind="ExternalInput").ap()
    d_dtb = nc.dram_tensor("dt_bt", [8, NG], F32, kind="ExternalInput").ap()
    d_mm_ = nc.dram_tensor("mask_m", [128, 4], F32, kind="ExternalInput").ap()
    d_m1 = nc.dram_tensor("mask_1m", [128, 4], F32, kind="ExternalInput").ap()
    d_idf = nc.dram_tensor("ident_f", [128, 128], F32, kind="ExternalInput").ap()

    d_out = nc.dram_tensor("out", [T, D], F32, kind="ExternalOutput").ap()

    d_ag_in = nc.dram_tensor("ag_in", [NG, 8 * 128], F32).ap()
    d_ag_out = nc.dram_tensor("ag_out", [NG, 4 * 8 * 128], F32).ap()
    d_r1 = nc.dram_tensor("r1_sp", [T, D], F32).ap()
    d_x2 = nc.dram_tensor("x2_sp", [T, D], F32).ap()
    d_r2 = nc.dram_tensor("r2_sp", [T, D], F32).ap()
    d_t = nc.dram_tensor("t_sp", [T, IF], F32).ap()
    d_qdq = nc.dram_tensor("qdq_sp", [T, IF], BF16).ap()
    d_wgq = nc.dram_tensor("wgq_sp", [IF, D], BF16).ap()
    d_wuq = nc.dram_tensor("wuq_sp", [IF, D], BF16).ap()
    d_wdq = nc.dram_tensor("wdq_sp", [D, IF], BF16).ap()

    def mkdbg(name, shape, dtype=F32):
        if name in debug:
            return nc.dram_tensor("dbg_" + name, shape, dtype,
                                  kind="ExternalOutput").ap()
        return None

    db_decay = mkdbg("decay", [NCT * 128, T])
    db_states = mkdbg("states", [NCT * 128, T])
    db_dt = mkdbg("dt", [8, NG * T])
    db_comb = mkdbg("combined", [NCT * 128, T])
    db_r1 = mkdbg("r1", [T, D])
    db_x2 = mkdbg("x2", [T, D])
    db_qxdq = mkdbg("qxdq", [T, D])
    db_cols = mkdbg("lncols", [NTT * 128, 16])
    db_t = mkdbg("ffn_t", [T, IF])
    db_wgq = mkdbg("wgq", [IF, D], BF16)
    db_qtdq = mkdbg("qtdq", [T, IF])
    db_er1 = mkdbg("er1", [T, D])
    db_ex2 = mkdbg("ex2", [T, D])
    db_er2 = mkdbg("er2", [T, D])

    C, S = {}, {}

    def inc(instr, name, n=1):
        C[name] += n
        instr.then_inc(S[name], n)

    def wall(eng, names):
        for n in names:
            if C[n] > 0:
                eng.wait_ge(S[n], C[n])

    ALLSEM = (["mm", "v", "sc", "par", "sdbg",
               "sx0", "sx1", "sw0", "sw1", "agi", "ago",
               "swo0", "swo1", "sxr0", "sxr1", "r1w0", "r1w1", "r1r0", "r1r1", "x2w0", "x2w1",
               "sx2c0", "sx2c1", "wq0", "wq1", "wgqw", "wuqw", "wdqw",
               "swm0", "swm1", "tw", "tr0", "tr1", "qdqw",
               "qdT", "wdm0", "wdm1", "r2w0", "r2w1", "r2r0", "r2r1", "outw", "gcq", "qscl", "qc8", "ad0", "ad1"]
              + ["cc%d" % i for i in range(NG)])

    ctx = ExitStack()
    with ctx:
        e = ctx.enter_context
        for n in ALLSEM:
            S[n] = e(nc.semaphore(n))
            C[n] = 0

        blk = e(nc.Block())

        class W:
            def __init__(self, eng):
                object.__setattr__(self, "_e", eng)
                object.__setattr__(self, "last", None)

            def __getattr__(self, n):
                f = getattr(self._e, n)
                if callable(f):
                    def g(*a, **k):
                        r = f(*a, **k)
                        object.__setattr__(self, "last", r)
                        return r
                    return g
                return f

        def vlink(eng, sem):
            inc(eng.last, sem, 1)
            eng.wait_ge(S[sem], C[sem])

        def SEC(name):
            def deco(f):
                def wrapper(eng):
                    return f(W(eng))
                getattr(blk, name)(wrapper)
                return None
            return deco

        scope_x = ExitStack()
        scope_c = ExitStack()
        scope_q = ExitStack()
        comb = scope_c.enter_context(
            nc.sbuf_tensor("comb", [128, NCT, T], BF16, side="right"))
        xbf = scope_x.enter_context(
            nc.sbuf_tensor("xbf", [128, ND, 1028], BF16, side="right"))
        p_alog = e(nc.sbuf_tensor("p_alog", [128, NCT], F32))
        p_negA = e(nc.sbuf_tensor("p_negA", [128, NCT], F32))
        p_bp = e(nc.sbuf_tensor("p_bp", [128, NCT], F32))
        p_cp = e(nc.sbuf_tensor("p_cp", [128, NCT], F32))
        p_dp = e(nc.sbuf_tensor("p_dp", [128, NCT], F32))
        p_cb = e(nc.sbuf_tensor("p_cb", [128, NCT], F32))
        p_cw = e(nc.sbuf_tensor("p_cw", [128, NCT * 4], F32))
        p_dtw = e(nc.sbuf_tensor("p_dtw", [128, NCT * 8], BF16))
        p_sel = e(nc.sbuf_tensor("p_sel", [8, GRP * 128], F32R))
        p_dtb = e(nc.sbuf_tensor("p_dtb", [8, NG], F32))
        p_mm = e(nc.sbuf_tensor("p_mm", [128, 4], F32))
        p_m1 = e(nc.sbuf_tensor("p_m1", [128, 4], F32))
        p_idf = e(nc.sbuf_tensor("p_idf", [128, 128], F32))
        p_idb = e(nc.sbuf_tensor("p_idb", [128, 128], BF16))
        cols = e(nc.sbuf_tensor("cols", [128, 64], F32))
        rmac = e(nc.sbuf_tensor("rmac", [128, NTT], F32))

        @SEC('gpsimd')
        def _(g):
            for dst, src in [(p_alog, d_alog), (p_bp, d_bp), (p_cp, d_cp),
                             (p_dp, d_dp), (p_cb, d_cb), (p_cw, d_cw),
                             (p_dtw, d_dtw), (p_sel, d_sel), (p_dtb, d_dtb),
                             (p_mm, d_mm_), (p_m1, d_m1), (p_idf, d_idf),
                             (p_idb, d_idf)]:
                inc(g.dma_start(out=dst[:, :], in_=src), "par", 16)

        @SEC('scalar')
        def _(s):
            wall(s, ["par"])
            s.activation(p_negA[:, :], p_alog[:, :], AF.Exp)
            inc(s.last, "sc", 1)

        @SEC('vector')
        def _(v):
            wall(v, ["sc"])
            v.tensor_scalar(p_negA[:, :], p_negA[:, :], -1.0, None, OP.mult)
            inc(v.last, "v", 1)

        with ExitStack() as c0:
            xtok = c0.enter_context(nc.sbuf_tensor("xtok", [128, 2, D], F32))
            ps_tr = c0.enter_context(nc.psum_tensor("ps_tr", [128, 512], F32))

            xcons_mark = {0: 0, 1: 0}
            for tt in range(9):
                rows = 128 if tt < 8 else 3
                buf = tt % 2
                col0 = tt * 128
                sxn = "sx%d" % buf

                @SEC('sync')
                def _(sy, tt=tt, rows=rows, buf=buf, sxn=sxn):
                    sy.wait_ge(S["mm"], xcons_mark[buf])
                    inc(sy.dma_start(out=xtok[0:rows, buf, :],
                                     in_=d_x[tt * 128:tt * 128 + rows, :]),
                        sxn, 16)
                sd_mark = C[sxn]

                for d4 in range(4):
                    @SEC('tensor')
                    def _(t, rows=rows, buf=buf, d4=d4, sd_mark=sd_mark,
                          sxn=sxn):
                        t.wait_ge(S[sxn], sd_mark)
                        t.wait_ge(S["v"], C["v"])
                        last = None
                        for k in range(4):
                            di = d4 * 4 + k
                            last = t.transpose(
                                ps_tr[:, k * 128:k * 128 + rows],
                                xtok[0:rows, buf, di * 128:(di + 1) * 128],
                                p_idf[0:rows, 0:rows])
                        inc(last, "mm", 1)
                    mm_mark = C["mm"]

                    @SEC('vector')
                    def _(v, rows=rows, d4=d4, col0=col0, mm_mark=mm_mark):
                        v.wait_ge(S["mm"], mm_mark)
                        src = ps_tr[:, 0:512].rearrange(
                            "p (a b) -> p a b", a=4)[:, :, 0:rows]
                        v.tensor_copy(xbf[:, d4 * 4:(d4 + 1) * 4,
                                          col0:col0 + rows], src)
                        inc(v.last, "v", 1)
                xcons_mark[buf] = C["mm"]

        pa = ExitStack()
        with pa:
            ea = pa.enter_context
            projX = ea(nc.sbuf_tensor("projX", [128, 2, GRP, 1028], BF16))
            gateb = ea(nc.sbuf_tensor("gateb", [128, 2, GRP, T], BF16))
            win = ea(nc.sbuf_tensor("win", [128, 2, ND * 128], BF16))
            wgt = ea(nc.sbuf_tensor("wgt", [128, 2, ND * 128], BF16))
            scr = ea(nc.sbuf_tensor("scr", [128, 6, T], F32))
            tmpv = ea(nc.sbuf_tensor("tmpv", [128, T], F32))
            dtf = ea(nc.sbuf_tensor("dtf", [8, T], F32R))
            cgb = ea(nc.sbuf_tensor("cgb", [128, 2, GRP, T], BF16))
            agb = ea(nc.sbuf_tensor("agb", [128, 8], F32))
            agot = ea(nc.sbuf_tensor("agot", [128, 32], F32))
            sall = ea(nc.sbuf_tensor("sall", [128, GRP], F32))
            zer = ea(nc.sbuf_tensor("zer", [128, T], F32))
            ps_p = ea(nc.psum_tensor("ps_p", [128, 2048], F32))
            ps_dt = ea(nc.psum_tensor("ps_dt", [8, T], F32))
            ps_bc = ea(nc.psum_tensor("ps_bc", [128, T], F32))

            @SEC('vector')
            def _(v):
                v.memset(zer[:, :], 0.0)
                if ng_lim < NG:
                    for ct in range(NCT):
                        v.memset(comb[:, ct, :], 0.0)
                inc(v.last, "v", 1)

            wload_mark = {}

            def emit_w_load(g, ct):
                buf = ct % 2
                sw = "sw%d" % buf
                for wt, dd in ((win, d_win), (wgt, d_wgt)):
                    inc(g.dma_start(
                        out=wt[:, buf, :].rearrange("p (a b) -> p a b", a=ND),
                        in_=dd[:, ct * 128:(ct + 1) * 128].rearrange(
                            "(a p) c -> p a c", p=128)), sw, 16)
                wload_mark[ct] = C[sw]

            @SEC('gpsimd')
            def _(g):
                wall(g, ["v"])
                emit_w_load(g, 0)
                emit_w_load(g, 1)

            evA = {0: 0, 1: 0, 2: 0}
            evac_mark = {"dt": 0, "dtv": 0}
            pend_ag = []
            corr_marks = {}
            prev_scan_mark = 0
            prev_drv_mark = 0
            prev_dcy_mark = 0

            for it in range(ng_lim + 1):
                if it < ng_lim:
                    for k in range(GRP):
                        ct = it * GRP + k
                        wbuf = ct % 2
                        swn = "sw%d" % wbuf

                        @SEC('tensor')
                        def _(t, ct=ct, wbuf=wbuf, swn=swn):
                            t.wait_ge(S[swn], wload_mark[ct])
                            if ct == 0:
                                wall(t, ["v"])
                            t.wait_ge(S["v"], evA[0])
                            for di in range(ND):
                                w = win[:, wbuf, di * 128:(di + 1) * 128]
                                t.matmul(ps_p[:, 0:512], w, xbf[:, di, 0:512],
                                         start=di == 0, stop=di == ND - 1)
                            inc(t.last, "mm", 1)
                        m_p0 = C["mm"]

                        @SEC('tensor')
                        def _(t, ct=ct, wbuf=wbuf):
                            t.wait_ge(S["v"], evA[1])
                            for di in range(ND):
                                w = win[:, wbuf, di * 128:(di + 1) * 128]
                                t.matmul(ps_p[:, 512:1024], w,
                                         xbf[:, di, 512:1024],
                                         start=di == 0, stop=di == ND - 1)
                            inc(t.last, "mm", 1)
                        m_p1 = C["mm"]

                        @SEC('tensor')
                        def _(t, ct=ct, wbuf=wbuf):
                            t.wait_ge(S["sc"], evA[2])
                            for di in range(ND):
                                w = wgt[:, wbuf, di * 128:(di + 1) * 128]
                                t.matmul(ps_p[:, 1024:1536], w,
                                         xbf[:, di, 3:515],
                                         start=di == 0, stop=di == ND - 1)
                            for di in range(ND):
                                w = wgt[:, wbuf, di * 128:(di + 1) * 128]
                                t.matmul(ps_p[:, 1536:2048], w,
                                         xbf[:, di, 515:1027],
                                         start=di == 0, stop=di == ND - 1)
                            inc(t.last, "mm", 1)
                        m_g = C["mm"]

                        @SEC('vector')
                        def _(v, it=it, k=k, m_p0=m_p0, m_p1=m_p1):
                            v.wait_ge(S["mm"], m_p1)
                            v.tensor_copy(projX[:, it % 2, k, 0:512],
                                          ps_p[:, 0:512])
                            inc(v.last, "v", 1)
                            v.tensor_copy(projX[:, it % 2, k, 512:1024],
                                          ps_p[:, 512:1024])
                            inc(v.last, "v", 1)
                        evA[0] = C["v"] - 1
                        evA[1] = C["v"]
                        ev_halo_pre = C["v"] - 1

                        @SEC('tensor')
                        def _(t, ct=ct, wbuf=wbuf, ev=ev_halo_pre):
                            t.wait_ge(S["v"], ev)
                            for di in range(ND):
                                w = win[:, wbuf, di * 128:(di + 1) * 128]
                                t.matmul(ps_p[:, 0:3], w,
                                         xbf[:, di, 1024:1027],
                                         start=di == 0, stop=di == ND - 1)
                            inc(t.last, "mm", 1)
                        m_ph = C["mm"]

                        @SEC('vector')
                        def _(v, it=it, k=k, m_ph=m_ph):
                            v.wait_ge(S["mm"], m_ph)
                            v.tensor_copy(projX[:, it % 2, k, 1024:1027],
                                          ps_p[:, 0:3])
                            inc(v.last, "v", 1)
                        evA[0] = C["v"]
                        proj_done = C["v"]

                        @SEC('scalar')
                        def _(s, it=it, k=k, m_g=m_g):
                            s.wait_ge(S["mm"], m_g)
                            s.activation(gateb[:, it % 2, k, 0:1024],
                                         ps_p[:, 1024:2048], AF.Sigmoid)
                            inc(s.last, "sc", 1)
                        evA[2] = C["sc"]

                        if ct + 2 < ng_lim * GRP:
                            @SEC('gpsimd')
                            def _(g, ct=ct):
                                g.wait_ge(S["mm"], C["mm"])
                                emit_w_load(g, ct + 2)

                        @SEC('vector')
                        def _(v, it=it, ct=ct, k=k, proj_done=proj_done):
                            v.wait_ge(S["v"], proj_done)
                            pj = projX[:, it % 2, k, :]
                            cwc = lambda j: p_cw[:, ct * 4 + j:ct * 4 + j + 1]
                            a0 = scr[:, k % 2, :]
                            a1 = scr[:, 2 + k % 2, :]
                            v.tensor_scalar(a0, pj[:, 0:1024], cwc(0),
                                            p_cb[:, ct:ct + 1], OP.mult,
                                            OP.add)
                            v.scalar_tensor_tensor(a1, pj[:, 1:1025], cwc(1),
                                                   a0, OP.mult, OP.add)
                            v.scalar_tensor_tensor(a0, pj[:, 2:1026], cwc(2),
                                                   a1, OP.mult, OP.add)
                            v.scalar_tensor_tensor(a1, pj[:, 3:1027], cwc(3),
                                                   a0, OP.mult, OP.add)
                            inc(v.last, "v", 1)
                        conv_mark = C["v"]

                        @SEC('scalar')
                        def _(s, it=it, k=k, conv_mark=conv_mark):
                            s.wait_ge(S["v"], conv_mark)
                            s.activation(projX[:, it % 2, k, 0:1024],
                                         scr[:, 2 + k % 2, :], AF.Silu)
                            inc(s.last, "sc", 1)
                        x_mark = C["sc"]

                        @SEC('tensor')
                        def _(t, it=it, ct=ct, k=k, x_mark=x_mark):
                            t.wait_ge(S["sc"], x_mark)
                            if k == 0:
                                t.wait_ge(S["sc"], evac_mark["dt"])
                                t.wait_ge(S["v"], evac_mark["dtv"])
                            Xap = projX[:, it % 2, k, 0:1024]
                            t.matmul(ps_dt[:, 0:512],
                                     p_dtw[:, ct * 8:(ct + 1) * 8],
                                     Xap[:, 0:512],
                                     start=k == 0, stop=k == GRP - 1)
                            t.matmul(ps_dt[:, 512:1024],
                                     p_dtw[:, ct * 8:(ct + 1) * 8],
                                     Xap[:, 512:1024],
                                     start=k == 0, stop=k == GRP - 1)
                            inc(t.last, "mm", 1)
                    dtmm_mark = C["mm"]

                    @SEC('scalar')
                    def _(s, it=it, dtmm_mark=dtmm_mark):
                        s.wait_ge(S["mm"], dtmm_mark)
                        s.wait_ge(S["v"], C["v"])
                        az = scr[0:8, 5, :]
                        s.activation(az, ps_dt[:, :], AF.Abs,
                                     bias=p_dtb[:, it:it + 1])
                        s.activation(az, az, AF.Exp, scale=-1.0)
                        s.activation(az, az, AF.Ln, bias=1.0)
                        inc(s.last, "sc", 1)
                    sp_mark = C["sc"]
                    evac_mark["dt"] = C["sc"]

                    @SEC('vector')
                    def _(v, it=it, dtmm_mark=dtmm_mark, sp_mark=sp_mark):
                        v.wait_ge(S["mm"], dtmm_mark)
                        v.tensor_scalar(tmpv[0:8, :], ps_dt[:, :],
                                        p_dtb[:, it:it + 1], 0.0,
                                        OP.add, OP.max)
                        v.wait_ge(S["sc"], sp_mark)
                        v.tensor_add(dtf[:, :], tmpv[0:8, :], scr[0:8, 5, :])
                        inc(v.last, "v", 1)
                    dtv_mark = C["v"]
                    evac_mark["dtv"] = C["v"]

                    if db_dt is not None:
                        @SEC('gpsimd')
                        def _(g, it=it, dtv_mark=dtv_mark):
                            g.wait_ge(S["v"], dtv_mark)
                            inc(g.dma_start(out=db_dt[:, it * T:(it + 1) * T],
                                            in_=dtf[:, :]), "sdbg", 16)
                            wall(g, ["sdbg"])

                    for k in range(GRP):
                        ct = it * GRP + k

                        @SEC('tensor')
                        def _(t, k=k, dtv_mark=dtv_mark,
                              psm=prev_drv_mark, pdm=prev_dcy_mark):
                            t.wait_ge(S["v"], dtv_mark)
                            t.wait_ge(S["sc"], pdm)
                            t.wait_ge(S["v"], psm)
                            t.matmul(ps_bc[:, 0:512],
                                     p_sel[:, k * 128:(k + 1) * 128],
                                     dtf[:, 0:512], start=True, stop=True)
                            t.matmul(ps_bc[:, 512:1024],
                                     p_sel[:, k * 128:(k + 1) * 128],
                                     dtf[:, 512:1024], start=True, stop=True)
                            inc(t.last, "mm", 1)
                        bc_mark = C["mm"]

                        @SEC('scalar')
                        def _(s, ct=ct, bc_mark=bc_mark, psm=prev_scan_mark):
                            s.wait_ge(S["mm"], bc_mark)
                            s.wait_ge(S["v"], psm)
                            s.activation(scr[:, 0, :], ps_bc[:, :], AF.Exp,
                                         scale=p_negA[:, ct:ct + 1])
                            inc(s.last, "sc", 1)
                        dcy_mark = C["sc"]
                        prev_dcy_mark = dcy_mark

                        @SEC('vector')
                        def _(v, it=it, ct=ct, k=k, dcy_mark=dcy_mark):
                            v.wait_ge(S["sc"], dcy_mark)
                            if k == 0:
                                v.wait_ge(S["agi"], C["agi"])
                            X = projX[:, it % 2, k, 0:1024]
                            gt = gateb[:, it % 2, k, :]
                            dcy, drv = scr[:, 0, :], scr[:, 1, :]
                            stt, cdc = scr[:, 2, :], scr[:, 3, :]
                            v.scalar_tensor_tensor(drv, ps_bc[:, :],
                                                   p_bp[:, ct:ct + 1], X,
                                                   OP.mult, OP.mult)
                            inc(v.last, "v", 1)
                            v.tensor_tensor_scan(stt, dcy, drv, 0.0,
                                                 OP.mult, OP.add)
                            v.tensor_tensor_scan(cdc, dcy, zer[:, :], 1.0,
                                                 OP.mult, OP.add)
                            v.tensor_scalar(tmpv[:, :], X, p_dp[:, ct:ct + 1],
                                            None, OP.mult)
                            v.scalar_tensor_tensor(tmpv[:, :], stt,
                                                   p_cp[:, ct:ct + 1],
                                                   tmpv[:, :], OP.mult,
                                                   OP.add)
                            v.tensor_mul(comb[:, ct, :], tmpv[:, :], gt)
                            v.scalar_tensor_tensor(cgb[:, it % 2, k, :], cdc,
                                                   p_cp[:, ct:ct + 1], gt,
                                                   OP.mult, OP.mult)
                            v.tensor_copy(agb[:, k:k + 1], cdc[:, T - 1:T])
                            vlink(v, "v")
                            v.tensor_copy(agb[:, GRP + k:GRP + k + 1],
                                          stt[:, T - 1:T])
                            inc(v.last, "v", 1)
                        prev_scan_mark = C["v"]
                        prev_drv_mark = C["v"] - 1

                        if db_decay is not None:
                            @SEC('gpsimd')
                            def _(g, ct=ct, sm=prev_scan_mark):
                                g.wait_ge(S["v"], sm)
                                inc(g.dma_start(
                                    out=db_decay[ct * 128:(ct + 1) * 128, :],
                                    in_=scr[:, 0, :]), "sdbg", 16)
                                inc(g.dma_start(
                                    out=db_states[ct * 128:(ct + 1) * 128, :],
                                    in_=scr[:, 2, :]), "sdbg", 16)
                                wall(g, ["sdbg"])

                    @SEC('gpsimd')
                    def _(g, it=it, sm=prev_scan_mark):
                        g.wait_ge(S["v"], sm)
                        inc(g.dma_start(out=d_ag_in[it, :].rearrange(
                            "(p j) -> p j", p=128), in_=agb[:, :]), "agi", 16)
                        g.wait_ge(S["agi"], C["agi"])
                        cc = g.collective_compute(
                            "AllGather", OP.bypass,
                            replica_groups=[[0, 1, 2, 3], [4, 5, 6, 7]],
                            ins=[d_ag_in[it, :]],
                            outs=[d_ag_out[it, :]])
                        inc(cc, "cc%d" % it, 1)

                    @SEC('sync')
                    def _(sy, it=it):
                        sy.wait_ge(S["cc%d" % it], 1)
                        sy.wait_ge(S["v"], corr_marks.get(it - 1, 0))
                        inc(sy.dma_start(out=agot[:, :].rearrange(
                            "p (r j) -> p r j", r=4),
                            in_=d_ag_out[it, :].rearrange(
                                "(r p j) -> p r j", r=4, p=128)), "ago", 16)
                    pend_ag.append((it, C["ago"]))

                while pend_ag and (len(pend_ag) > 1 or it == ng_lim):
                    git, mark = pend_ag.pop(0)

                    @SEC('vector')
                    def _(v, git=git, mark=mark):
                        v.wait_ge(S["ago"], mark)
                        v.memset(sall[:, :], 0.0)
                        vlink(v, "v")
                        for r in range(4):
                            dcol = agot[:, r * 8:r * 8 + GRP]
                            fcol = agot[:, r * 8 + GRP:r * 8 + 8]
                            v.tensor_scalar(tmpv[:, 0:GRP], dcol,
                                            p_mm[:, r:r + 1],
                                            p_m1[:, r:r + 1],
                                            OP.mult, OP.add)
                            v.tensor_scalar(tmpv[:, GRP:2 * GRP], fcol,
                                            p_mm[:, r:r + 1], None, OP.mult)
                            vlink(v, "v")
                            v.tensor_mul(sall[:, :], sall[:, :],
                                         tmpv[:, 0:GRP])
                            vlink(v, "v")
                            v.tensor_add(sall[:, :], sall[:, :],
                                         tmpv[:, GRP:2 * GRP])
                            vlink(v, "v")
                        for k in range(GRP):
                            ct = git * GRP + k
                            v.scalar_tensor_tensor(
                                comb[:, ct, :], cgb[:, git % 2, k, :],
                                sall[:, k:k + 1], comb[:, ct, :],
                                OP.mult, OP.add)
                        inc(v.last, "v", 1)
                    corr_marks[git] = C["v"]

                    if db_comb is not None:
                        @SEC('gpsimd')
                        def _(g, git=git, cm=corr_marks[git]):
                            g.wait_ge(S["v"], cm)
                            for k in range(GRP):
                                ct = git * GRP + k
                                inc(g.dma_start(
                                    out=db_comb[ct * 128:(ct + 1) * 128, :],
                                    in_=comb[:, ct, :]), "sdbg", 16)
                            wall(g, ["sdbg"])

        @SEC('tensor')
        def _(t):
            wall(t, ["v", "sc"])
            inc(t.last, "mm", 1)
        barA = C["mm"]
        scope_x.close()
        scope_w = ExitStack()
        wslg = scope_w.enter_context(
            nc.sbuf_tensor("wslg", [128, 2, 2048], F32, side="right"))
        wq8g = scope_w.enter_context(
            nc.sbuf_tensor("wq8g", [128, 2, 2048], FP8, side="right"))
        wdqg = scope_w.enter_context(
            nc.sbuf_tensor("wdqg", [128, 2048], BF16, side="right"))

        if upto == "A":
            with ExitStack() as pt:
                ot = pt.enter_context(nc.sbuf_tensor("ot", [128, T], F32))

                @SEC('gpsimd')
                def _(g):
                    g.wait_ge(S["mm"], barA)
                    for ct in range(min(ng_lim * GRP, 8)):
                        inc(g.dma_start(out=ot[:, :], in_=comb[:, ct, :]),
                            "outw", 16)
                        g.wait_ge(S["outw"], C["outw"])
                        inc(g.dma_start(
                            out=d_out[(ct % 8) * 128:(ct % 8) * 128 + 128,
                                      0:T], in_=ot[:, :]), "outw", 16)
                    g.wait_ge(S["outw"], C["outw"])
            return nc

        qxT = scope_q.enter_context(
            nc.sbuf_tensor("qxT", [128, ND, T], BF16))
        pb = ExitStack()
        with pb:
            eb = pb.enter_context
            woutb = eb(nc.sbuf_tensor("woutb", [128, 2, NCT * 256], BF16))
            xres = eb(nc.sbuf_tensor("xres", [128, 2, 256], F32))
            r1c = eb(nc.sbuf_tensor("r1c", [128, 2, 256], F32))
            r1t = eb(nc.sbuf_tensor("r1t", [128, 2, D], F32))
            x2t = eb(nc.sbuf_tensor("x2t", [128, 2, D], F32))
            qf8x = eb(nc.sbuf_tensor("qf8x", [128, D], FP8))
            qxdq = eb(nc.sbuf_tensor("qxdq", [128, D], BF16))
            sqs = eb(nc.sbuf_tensor("sqs", [128, D], F32))
            ps_o = eb(nc.psum_tensor("ps_o", [128, 1024], F32))
            ps_qt = eb(nc.psum_tensor("ps_qt", [128, 512], BF16))

            wgu_slabs = [(d_wg, d_wgq, "wgqw", sl) for sl in range(NIT)] + \
                        [(d_wu, d_wuq, "wuqw", sl) for sl in range(NIT)]
            wgu_state = {"i": 0, "cons": {0: 0, 1: 0}, "dqA": {0: 0, 1: 0}}

            def emit_wgu(n):
                for _ in range(n):
                    if wgu_state["i"] >= len(wgu_slabs):
                        return
                    idx = wgu_state["i"]
                    wgu_state["i"] += 1
                    srcw, dstw, dsem, sl = wgu_slabs[idx]
                    pk = idx % 2
                    wqs = "wq%d" % pk
                    cb = {0: 20, 1: 24, 2: 12, 3: 16}[idx % 4]

                    @SEC('sync')
                    def _(sy, srcw=srcw, sl=sl, pk=pk, wqs=wqs, idx=idx):
                        sy.wait_ge(S["qc8"], wgu_state["cons"][pk])
                        inc(sy.dma_start(out=wslg[:, pk, :],
                                         in_=srcw[sl * 128:(sl + 1) * 128, :]),
                            wqs, 16)
                    slm = C[wqs]

                    @SEC('vector')
                    def _(v, pk=pk, slm=slm, wqs=wqs, cb=cb):
                        v.wait_ge(S[wqs], slm)
                        v.reduce_max(cols[:, cb:cb + 1], wslg[:, pk, :], AX,
                                     apply_absolute_value=True)
                        vlink(v, "v")
                        v.reciprocal(cols[:, cb + 1:cb + 2],
                                     cols[:, cb:cb + 1])
                        vlink(v, "v")
                        v.tensor_scalar(cols[:, cb + 2:cb + 3],
                                        cols[:, cb + 1:cb + 2], 224.0, None,
                                        OP.mult)
                        vlink(v, "v")
                        v.tensor_scalar(cols[:, cb + 3:cb + 4],
                                        cols[:, cb:cb + 1], 1.0 / 224.0,
                                        None, OP.mult)
                        vlink(v, "qscl")
                    qsm = C["qscl"]

                    @SEC('vector')
                    def _(v, pk=pk, cb=cb):
                        v.wait_ge(S["gcq"], wgu_state["dqA"][pk])
                        v.tensor_scalar(wq8g[:, pk, :], wslg[:, pk, :],
                                        cols[:, cb + 2:cb + 3], None, OP.mult)
                        inc(v.last, "qc8", 1)
                    c8m = C["qc8"]
                    wgu_state["cons"][pk] = C["qc8"]

                    @SEC('scalar')
                    def _(s, c8m=c8m, cb=cb):
                        s.wait_ge(S["qc8"], c8m)
                        s.wait_ge(S["wgqw"], C["wgqw"])
                        s.wait_ge(S["wuqw"], C["wuqw"])
                        s.activation(wdqg[:, :], wq8g[:, pk, :], AF.Identity,
                                     scale=cols[:, cb + 3:cb + 4])
                        inc(s.last, "gcq", 1)
                    dqm = C["gcq"]
                    wgu_state["dqA"][pk] = C["gcq"]

                    @SEC('gpsimd')
                    def _(g, dqm=dqm, dsem=dsem, dstw=dstw, sl=sl):
                        g.wait_ge(S["gcq"], dqm)
                        inc(g.dma_start(out=dstw[sl * 128:(sl + 1) * 128, :],
                                        in_=wdqg[:, :]), dsem, 16)

            def wout_load(g, dch):
                swo = "swo%d" % (dch % 2)
                inc(g.dma_start(
                    out=woutb[:, dch % 2, :].rearrange(
                        "p (a b) -> p a b", a=NCT),
                    in_=d_wout[:, dch * 256:(dch + 1) * 256].rearrange(
                        "(a p) c -> p a c", p=128)), swo, 16)

            @SEC('gpsimd')
            def _(g):
                g.wait_ge(S["mm"], barA)
                wout_load(g, 0)
            wo_mark = {0: C["swo0"]}

            evo_mark = {0: 0, 1: 0}
            r1spill = {0: 0, 1: 0}
            for dch in range(8):
                emit_wgu(11)
                if dch + 1 < 8:
                    @SEC('gpsimd')
                    def _(g, dch=dch):
                        g.wait_ge(S["mm"], C["mm"])
                        wout_load(g, dch + 1)
                    wo_mark[dch + 1] = C["swo%d" % ((dch + 1) % 2)]

                for tt in range(NTT):
                    bk = tt % 2
                    sxr = "sxr%d" % bk

                    @SEC('sync')
                    def _(sy, dch=dch, tt=tt, bk=bk, sxr=sxr):
                        sy.wait_ge(S["mm"], barA)
                        sy.wait_ge(S["v"], evo_mark[bk])
                        inc(sy.dma_start(
                            out=xres[:, bk, :],
                            in_=d_x[3 + tt * 128:3 + (tt + 1) * 128,
                                    dch * 256:(dch + 1) * 256]), sxr, 16)
                    xr_mark = C[sxr]

                    @SEC('tensor')
                    def _(t, dch=dch, tt=tt, bk=bk):
                        t.wait_ge(S["swo%d" % (dch % 2)], wo_mark[dch])
                        if evo_mark[bk]:
                            t.wait_ge(S["v"], evo_mark[bk])
                        for ct in range(NCT):
                            t.matmul(ps_o[:, bk * 512:bk * 512 + 256],
                                     comb[:, ct, tt * 128:(tt + 1) * 128],
                                     woutb[:, dch % 2,
                                           ct * 256:(ct + 1) * 256],
                                     start=ct == 0, stop=ct == NCT - 1)
                        inc(t.last, "mm", 1)
                    o_mark = C["mm"]

                    @SEC('vector')
                    def _(v, bk=bk, o_mark=o_mark, xr_mark=xr_mark, sxr=sxr,
                          spm=None):
                        v.wait_ge(S["mm"], o_mark)
                        v.wait_ge(S[sxr], xr_mark)
                        v.wait_ge(S["r1w%d" % bk], r1spill.get(bk, 0))
                        v.tensor_add(r1c[:, bk, :],
                                     ps_o[:, bk * 512:bk * 512 + 256],
                                     xres[:, bk, :])
                        inc(v.last, "v", 1)
                    evo_mark[bk] = C["v"]
                    r1c_mark = C["v"]

                    @SEC('gpsimd')
                    def _(g, dch=dch, tt=tt, bk=bk, r1c_mark=r1c_mark):
                        g.wait_ge(S["v"], r1c_mark)
                        inc(g.dma_start(
                            out=d_r1[tt * 128:(tt + 1) * 128,
                                     dch * 256:(dch + 1) * 256],
                            in_=r1c[:, bk, :]), "r1w%d" % bk, 16)
                    r1spill[bk] = C["r1w%d" % bk]
            r1w_done = (C["r1w0"], C["r1w1"])

            if db_r1 is not None:
                @SEC('gpsimd')
                def _(g):
                    g.wait_ge(S["r1w0"], r1w_done[0])
                    g.wait_ge(S["r1w1"], r1w_done[1])
                    inc(g.dma_start(out=db_r1, in_=d_r1), "sdbg", 16)
                    wall(g, ["sdbg"])

            x2cons = {0: 0, 1: 0}
            x2spill = {0: 0, 1: 0}
            for tt in range(NTT):
                bk = tt % 2
                r1r = "r1r%d" % bk

                @SEC('sync')
                def _(sy, tt=tt, bk=bk, r1r=r1r):
                    sy.wait_ge(S["r1w0"], r1w_done[0])
                    sy.wait_ge(S["r1w1"], r1w_done[1])
                    sy.wait_ge(S["v"], x2cons[bk])
                    inc(sy.dma_start(out=r1t[:, bk, :],
                                     in_=d_r1[tt * 128:(tt + 1) * 128, :]),
                        r1r, 16)
                r1t_mark = C[r1r]

                @SEC('scalar')
                def _(s, bk=bk, r1t_mark=r1t_mark, r1r=r1r):
                    s.wait_ge(S[r1r], r1t_mark)
                    s.wait_ge(S["v"], C["v"])
                    s.activation(sqs[:, :], r1t[:, bk, :], AF.Square,
                                 accum_out=cols[:, 0:1])
                    inc(s.last, "sc", 1)
                sq_mark = C["sc"]

                @SEC('vector')
                def _(v, bk=bk, sq_mark=sq_mark):
                    v.wait_ge(S["sc"], sq_mark)
                    v.reduce_sum(cols[:, 1:2], r1t[:, bk, :], AX)
                    vlink(v, "v")
                    v.tensor_scalar(cols[:, 2:3], cols[:, 1:2], 1.0 / D,
                                    None, OP.mult)
                    v.tensor_scalar(cols[:, 3:4], cols[:, 0:1], 1.0 / D,
                                    None, OP.mult)
                    vlink(v, "v")
                    v.tensor_mul(cols[:, 4:5], cols[:, 2:3], cols[:, 2:3])
                    vlink(v, "v")
                    v.tensor_sub(cols[:, 5:6], cols[:, 3:4], cols[:, 4:5])
                    vlink(v, "v")
                    v.tensor_scalar(cols[:, 5:6], cols[:, 5:6], 1.0, EPS,
                                    OP.mult, OP.add)
                    inc(v.last, "v", 1)
                var_mark = C["v"]

                @SEC('scalar')
                def _(s, var_mark=var_mark):
                    s.wait_ge(S["v"], var_mark)
                    s.activation(cols[:, 6:7], cols[:, 5:6], AF.Ln)
                    inc(s.last, "sc", 1)
                ln_mark = C["sc"]

                @SEC('vector')
                def _(v, ln_mark=ln_mark):
                    v.wait_ge(S["sc"], ln_mark)
                    v.tensor_scalar(cols[:, 6:7], cols[:, 6:7], -0.5, None,
                                    OP.mult)
                    inc(v.last, "v", 1)
                lh_mark = C["v"]

                @SEC('scalar')
                def _(s, lh_mark=lh_mark):
                    s.wait_ge(S["v"], lh_mark)
                    s.activation(cols[:, 6:7], cols[:, 6:7], AF.Exp)
                    inc(s.last, "sc", 1)
                rstd_mark = C["sc"]

                @SEC('vector')
                def _(v, rstd_mark=rstd_mark):
                    v.wait_ge(S["sc"], rstd_mark)
                    v.tensor_mul(cols[:, 7:8], cols[:, 2:3], cols[:, 6:7])
                    vlink(v, "v")
                    v.tensor_scalar(cols[:, 7:8], cols[:, 7:8], -1.0, None,
                                    OP.mult)
                    inc(v.last, "v", 1)
                nb_mark = C["v"]

                @SEC('vector')
                def _(v, bk=bk, nb_mark=nb_mark):
                    v.wait_ge(S["v"], nb_mark)
                    v.wait_ge(S["x2w%d" % bk], x2spill[bk])
                    v.tensor_scalar(x2t[:, bk, :], r1t[:, bk, :],
                                    cols[:, 6:7], cols[:, 7:8],
                                    OP.mult, OP.add)
                    inc(v.last, "v", 1)
                x2_mark = C["v"]

                @SEC('gpsimd')
                def _(g, tt=tt, bk=bk, x2_mark=x2_mark):
                    g.wait_ge(S["v"], x2_mark)
                    inc(g.dma_start(out=d_x2[tt * 128:(tt + 1) * 128, :],
                                    in_=x2t[:, bk, :]), "x2w%d" % bk, 16)
                    if db_cols is not None:
                        inc(g.dma_start(
                            out=db_cols[tt * 128:(tt + 1) * 128, :],
                            in_=cols[:, 0:16]), "sdbg", 16)
                        wall(g, ["sdbg"])
                x2spill[bk] = C["x2w%d" % bk]

                @SEC('vector')
                def _(v, bk=bk, x2_mark=x2_mark):
                    v.reduce_max(cols[:, 8:9], x2t[:, bk, :], AX,
                                 apply_absolute_value=True)
                    vlink(v, "v")
                    v.reciprocal(cols[:, 9:10], cols[:, 8:9])
                    vlink(v, "v")
                    v.tensor_scalar(cols[:, 10:11], cols[:, 9:10], 224.0,
                                    None, OP.mult)
                    vlink(v, "v")
                    v.tensor_scalar(cols[:, 11:12], cols[:, 8:9], 1.0 / 224.0,
                                    None, OP.mult)
                    vlink(v, "v")
                qs_mark = C["v"]

                @SEC('vector')
                def _(v, bk=bk, qs_mark=qs_mark):
                    v.tensor_scalar(qf8x[:, :], x2t[:, bk, :],
                                    cols[:, 10:11], None, OP.mult)
                    inc(v.last, "v", 1)
                q8_mark = C["v"]
                x2cons[bk] = C["v"]

                @SEC('vector')
                def _(v, q8_mark=q8_mark):
                    v.tensor_scalar(qxdq[:, :], qf8x[:, :], cols[:, 11:12],
                                    None, OP.mult)
                    inc(v.last, "v", 1)
                qdq_mark = C["v"]

                if db_qxdq is not None:
                    @SEC('gpsimd')
                    def _(g, tt=tt, qdq_mark=qdq_mark):
                        g.wait_ge(S["v"], qdq_mark)
                        inc(g.dma_start(
                            out=db_qxdq[tt * 128:(tt + 1) * 128, :],
                            in_=qxdq[:, :]), "sdbg", 16)
                        wall(g, ["sdbg"])

                for d4 in range(4):
                    @SEC('tensor')
                    def _(t, d4=d4, qdq_mark=qdq_mark):
                        t.wait_ge(S["v"], qdq_mark)
                        t.wait_ge(S["v"], C["v"])
                        last = None
                        for kk in range(4):
                            di = d4 * 4 + kk
                            last = t.transpose(
                                ps_qt[:, kk * 128:(kk + 1) * 128],
                                qxdq[:, di * 128:(di + 1) * 128],
                                p_idb[:, :])
                        inc(last, "mm", 1)
                    tr_mark = C["mm"]

                    @SEC('vector')
                    def _(v, tt=tt, d4=d4, tr_mark=tr_mark):
                        v.wait_ge(S["mm"], tr_mark)
                        v.tensor_copy(
                            qxT[:, d4 * 4:(d4 + 1) * 4,
                                tt * 128:(tt + 1) * 128],
                            ps_qt[:, 0:512].rearrange("p (a b) -> p a b",
                                                      a=4))
                        inc(v.last, "v", 1)

            emit_wgu(10 ** 9)

            if db_x2 is not None:
                @SEC('gpsimd')
                def _(g):
                    wall(g, ["x2w0", "x2w1"])
                    inc(g.dma_start(out=db_x2, in_=d_x2), "sdbg", 16)
                    wall(g, ["sdbg"])

            @SEC('tensor')
            def _(t):
                wall(t, ["v", "sc", "gcq"])
                inc(t.last, "mm", 1)
            barB = C["mm"]

            if upto == "mixer":
                with ExitStack() as pt:
                    ot = pt.enter_context(nc.sbuf_tensor("ot", [128, D], F32))

                    @SEC('gpsimd')
                    def _(g):
                        g.wait_ge(S["mm"], barB)
                        wall(g, ["x2w0", "x2w1"])
                        for tt in range(NTT):
                            inc(g.dma_start(
                                out=ot[:, :],
                                in_=d_x2[tt * 128:(tt + 1) * 128, :]),
                                "outw", 16)
                            g.wait_ge(S["outw"], C["outw"])
                            inc(g.dma_start(
                                out=d_out[tt * 128:(tt + 1) * 128, :],
                                in_=ot[:, :]), "outw", 16)
                        g.wait_ge(S["outw"], C["outw"])
                return nc
        scope_w.close()
        scope_c.close()

        ICH = [(0, 1024), (1024, 1024), (2048, 1024), (3072, 1024),
               (4096, 1024), (5120, 512)]
        pc = ExitStack()
        with pc:
            ec = pc.enter_context
            wmv = ec(nc.sbuf_tensor("wmv", [128, 2, 2, ND * 1024], BF16))
            gs_t = ec(nc.sbuf_tensor("gs_t", [128, 1024], F32))
            tpc = ec(nc.sbuf_tensor("tpc", [128, 1024], F32))
            wsld = ec(nc.sbuf_tensor("wsld", [128, 2, 2816], F32))
            wq8d = ec(nc.sbuf_tensor("wq8d", [128, 2, 2816], FP8))
            wdqd = ec(nc.sbuf_tensor("wdqd", [128, 2816], BF16))
            ps_g = ec(nc.psum_tensor("ps_g", [128, 2, 1024], F32))
            ps_u = ec(nc.psum_tensor("ps_u", [128, 2, 1024], F32))

            wd_state = {"sl": 0, "cons": 0, "dq": {0: 0, 1: 0}}

            def emit_wd(n):
                for _ in range(n):
                    if wd_state["sl"] >= ND:
                        return
                    sl = wd_state["sl"]
                    wd_state["sl"] += 1
                    adn = "ad%d" % (sl % 2)

                    @SEC('scalar')
                    def _(s, sl=sl, adn=adn):
                        s.wait_ge(S["mm"], barB)
                        for h in range(2):
                            inc(s.dma_start(
                                out=wsld[:, h, :],
                                in_=d_wd[sl * 128:(sl + 1) * 128,
                                         h * 2816:(h + 1) * 2816]), adn, 16)
                    slm = C[adn]

                    cbd = 28 + 4 * (sl % 2)

                    @SEC('vector')
                    def _(v, slm=slm, adn=adn, cbd=cbd):
                        v.wait_ge(S[adn], slm)
                        v.reduce_max(cols[:, cbd:cbd + 1], wsld[:, 0, :], AX,
                                     apply_absolute_value=True)
                        v.reduce_max(cols[:, cbd + 1:cbd + 2], wsld[:, 1, :],
                                     AX, apply_absolute_value=True)
                        vlink(v, "v")
                        v.tensor_max(cols[:, cbd:cbd + 1],
                                     cols[:, cbd:cbd + 1],
                                     cols[:, cbd + 1:cbd + 2])
                        vlink(v, "v")
                        v.reciprocal(cols[:, cbd + 2:cbd + 3],
                                     cols[:, cbd:cbd + 1])
                        vlink(v, "v")
                        v.tensor_scalar(cols[:, cbd + 3:cbd + 4],
                                        cols[:, cbd + 2:cbd + 3],
                                        224.0, None, OP.mult)
                        vlink(v, "v")
                        v.tensor_scalar(cols[:, cbd + 1:cbd + 2],
                                        cols[:, cbd:cbd + 1],
                                        1.0 / 224.0, None, OP.mult)
                        vlink(v, "qscl")
                    qsm = C["qscl"]

                    for h in range(2):
                        @SEC('vector')
                        def _(v, h=h, cbd=cbd):
                            v.wait_ge(S["gcq"], wd_state["dq"][h])
                            v.tensor_scalar(wq8d[:, h, :], wsld[:, h, :],
                                            cols[:, cbd + 3:cbd + 4],
                                            None, OP.mult)
                            inc(v.last, "qc8", 1)
                        c8m = C["qc8"]
                        if h == 1:
                            wd_state["cons"] = C["qc8"]

                        @SEC('scalar')
                        def _(s, h=h, c8m=c8m, cbd=cbd):
                            s.wait_ge(S["qc8"], c8m)
                            s.wait_ge(S["wdqw"], C["wdqw"])
                            s.activation(wdqd[:, :], wq8d[:, h, :],
                                         AF.Identity,
                                         scale=cols[:, cbd + 1:cbd + 2])
                            inc(s.last, "gcq", 1)
                        dqm = C["gcq"]
                        wd_state["dq"][h] = C["gcq"]

                        @SEC('gpsimd')
                        def _(g, sl=sl, h=h, dqm=dqm):
                            g.wait_ge(S["gcq"], dqm)
                            inc(g.dma_start(
                                out=d_wdq[sl * 128:(sl + 1) * 128,
                                          h * 2816:(h + 1) * 2816],
                                in_=wdqd[:, :]), "wdqw", 16)

            @SEC('vector')
            def _(v):
                v.wait_ge(S["mm"], barB)
                v.memset(rmac[:, :], 0.0)
                inc(v.last, "v", 1)

            def wmv_load(sy, ic, i0, iw):
                swm = "swm%d" % (ic % 2)
                for wi, dd in ((0, d_wgq), (1, d_wuq)):
                    for di in range(ND):
                        inc(sy.dma_start(
                            out=wmv[:, ic % 2, wi,
                                    di * 1024:di * 1024 + iw],
                            in_=dd[i0:i0 + iw, di * 128:(di + 1) * 128],
                            transpose=True), swm, 16)

            @SEC('sync')
            def _(sy):
                sy.wait_ge(S["mm"], barB)
                wall(sy, ["wgqw", "wuqw"])
                wmv_load(sy, 0, ICH[0][0], ICH[0][1])
            wm_mark = {0: C["swm0"]}

            ev_mark2 = {0: [0, 0], 1: [0, 0]}
            gscons = [0]
            for ic, (i0, iw) in enumerate(ICH):
                emit_wd(3)
                if ic + 1 < len(ICH):
                    @SEC('sync')
                    def _(sy, ic=ic):
                        sy.wait_ge(S["mm"], C["mm"])
                        wmv_load(sy, ic + 1, ICH[ic + 1][0], ICH[ic + 1][1])
                    wm_mark[ic + 1] = C["swm%d" % ((ic + 1) % 2)]

                for tt in range(NTT):
                    pp = (ic * NTT + tt) % 2

                    @SEC('tensor')
                    def _(t, ic=ic, tt=tt, iw=iw, pp=pp):
                        t.wait_ge(S["swm%d" % (ic % 2)], wm_mark[ic])
                        t.wait_ge(S["v"], ev_mark2[pp][0])
                        t.wait_ge(S["sc"], ev_mark2[pp][1])
                        for half in range(2 if iw == 1024 else 1):
                            for di in range(ND):
                                qx = qxT[:, di, tt * 128:(tt + 1) * 128]
                                base = di * 1024 + half * 512
                                t.matmul(ps_g[:, pp,
                                              half * 512:half * 512 + 512],
                                         qx, wmv[:, ic % 2, 0,
                                                 base:base + 512],
                                         start=di == 0, stop=di == ND - 1)
                                t.matmul(ps_u[:, pp,
                                              half * 512:half * 512 + 512],
                                         qx, wmv[:, ic % 2, 1,
                                                 base:base + 512],
                                         start=di == 0, stop=di == ND - 1)
                        inc(t.last, "mm", 1)
                    gu_mark = C["mm"]

                    @SEC('scalar')
                    def _(s, iw=iw, gu_mark=gu_mark, pp=pp):
                        s.wait_ge(S["mm"], gu_mark)
                        s.wait_ge(S["v"], gscons[0])
                        s.activation(gs_t[:, 0:iw], ps_g[:, pp, 0:iw],
                                     AF.Silu)
                        inc(s.last, "sc", 1)
                    ev_mark2[pp][1] = C["sc"]
                    gs_mark = C["sc"]

                    @SEC('vector')
                    def _(v, tt=tt, iw=iw, gs_mark=gs_mark, pp=pp):
                        v.wait_ge(S["sc"], gs_mark)
                        v.wait_ge(S["tw"], C["tw"])
                        v.tensor_mul(tpc[:, 0:iw], gs_t[:, 0:iw],
                                     ps_u[:, pp, 0:iw])
                        v.reduce_max(cols[:, 36:37], tpc[:, 0:iw], AX,
                                     apply_absolute_value=True)
                        vlink(v, "v")
                        v.tensor_max(rmac[:, tt:tt + 1], rmac[:, tt:tt + 1],
                                     cols[:, 36:37])
                        inc(v.last, "v", 1)
                    ev_mark2[pp][0] = C["v"]
                    gscons[0] = C["v"]
                    tp_mark = C["v"]

                    @SEC('gpsimd')
                    def _(g, tt=tt, i0=i0, iw=iw, tp_mark=tp_mark):
                        g.wait_ge(S["v"], tp_mark)
                        inc(g.dma_start(out=d_t[tt * 128:(tt + 1) * 128,
                                                i0:i0 + iw],
                                        in_=tpc[:, 0:iw]), "tw", 16)
            emit_wd(10 ** 9)
            if db_wgq is not None:
                @SEC('gpsimd')
                def _(g):
                    wall(g, ["wgqw"])
                    inc(g.dma_start(out=db_wgq, in_=d_wgq), "sdbg", 16)
                    wall(g, ["sdbg"])
            tw_done = C["tw"]

            if db_t is not None:
                @SEC('gpsimd')
                def _(g):
                    g.wait_ge(S["tw"], tw_done)
                    inc(g.dma_start(out=db_t, in_=d_t), "sdbg", 16)
                    wall(g, ["sdbg"])

        @SEC('tensor')
        def _(t):
            wall(t, ["v", "sc"])
            inc(t.last, "mm", 1)
        barC = C["mm"]
        scope_q.close()

        pc2 = ExitStack()
        with pc2:
            ec2 = pc2.enter_context
            tbuf = ec2(nc.sbuf_tensor("tbuf", [128, 2, IF], F32))
            qf8t = ec2(nc.sbuf_tensor("qf8t", [128, IF], FP8))
            qtdq = ec2(nc.sbuf_tensor("qtdq", [128, IF], BF16))

            tcons = {0: 0, 1: 0}
            for tt in range(NTT):
                bk = tt % 2
                trs = "tr%d" % bk

                @SEC('sync')
                def _(sy, tt=tt, bk=bk, trs=trs):
                    if tt == 0:
                        sy.wait_ge(S["mm"], barC)
                    sy.wait_ge(S["tw"], tw_done)
                    sy.wait_ge(S["v"], tcons[bk])
                    inc(sy.dma_start(out=tbuf[:, bk, :],
                                     in_=d_t[tt * 128:(tt + 1) * 128, :]),
                        trs, 16)
                tb_mark = C[trs]

                @SEC('vector')
                def _(v, tt=tt, tb_mark=tb_mark, trs=trs):
                    v.wait_ge(S["mm"], barC)
                    v.wait_ge(S[trs], tb_mark)
                    v.reciprocal(cols[:, 50:51], rmac[:, tt:tt + 1])
                    vlink(v, "v")
                    v.tensor_scalar(cols[:, 51:52], cols[:, 50:51], 224.0,
                                    None, OP.mult)
                    vlink(v, "v")
                    v.tensor_scalar(cols[:, 52:53], rmac[:, tt:tt + 1],
                                    1.0 / 224.0, None, OP.mult)
                    vlink(v, "v")
                qs2_mark = C["v"]

                @SEC('vector')
                def _(v, bk=bk, qs2_mark=qs2_mark):
                    v.tensor_scalar(qf8t[:, :], tbuf[:, bk, :],
                                    cols[:, 51:52], None, OP.mult)
                    inc(v.last, "v", 1)
                q8t_mark = C["v"]
                tcons[bk] = C["v"]

                @SEC('vector')
                def _(v, q8t_mark=q8t_mark):
                    v.wait_ge(S["qdqw"], C["qdqw"])
                    v.tensor_scalar(qtdq[:, :], qf8t[:, :], cols[:, 52:53],
                                    None, OP.mult)
                    inc(v.last, "v", 1)
                qdqt_mark = C["v"]

                @SEC('gpsimd')
                def _(g, tt=tt, qdqt_mark=qdqt_mark):
                    g.wait_ge(S["v"], qdqt_mark)
                    inc(g.dma_start(out=d_qdq[tt * 128:(tt + 1) * 128, :],
                                    in_=qtdq[:, :]), "qdqw", 16)
                    if db_qtdq is not None:
                        g.wait_ge(S["qdqw"], C["qdqw"])
                        inc(g.dma_start(
                            out=db_qtdq[tt * 128:(tt + 1) * 128, :],
                            in_=qtdq[:, :]), "sdbg", 16)
                        wall(g, ["sdbg"])
            qdq_done = C["qdqw"]

        @SEC('tensor')
        def _(t):
            wall(t, ["v", "sc"])
            inc(t.last, "mm", 1)
        barC2 = C["mm"]

        pd = ExitStack()
        with pd:
            ed = pd.enter_context
            qdT = ed(nc.sbuf_tensor("qdT", [128, NIT, T], BF16))
            wdm = ed(nc.sbuf_tensor("wdm", [128, 2, NIT, 512], BF16))
            x2c = ed(nc.sbuf_tensor("x2c", [128, 2, 512], F32))
            r2c = ed(nc.sbuf_tensor("r2c", [128, 2, 512], F32))
            r2t = ed(nc.sbuf_tensor("r2t", [128, 1, D], F32))
            otb = ed(nc.sbuf_tensor("otb", [128, 1, D], F32))
            ps_d = ed(nc.psum_tensor("ps_d", [128, NTT * 512], F32))

            @SEC('sync')
            def _(sy):
                sy.wait_ge(S["mm"], barC2)
                sy.wait_ge(S["qdqw"], qdq_done)
                for i in range(NIT):
                    inc(sy.dma_start(out=qdT[:, i, :],
                                     in_=d_qdq[:, i * 128:(i + 1) * 128],
                                     transpose=True), "qdT", 16)
            qdT_mark = C["qdT"]

            def wdm_load(sy, dch):
                sem = "wdm%d" % (dch % 2)
                for i in range(NIT):
                    inc(sy.dma_start(
                        out=wdm[:, dch % 2, i, :],
                        in_=d_wdq[dch * 512:(dch + 1) * 512,
                                  i * 128:(i + 1) * 128],
                        transpose=True), sem, 16)
                return C[sem]

            @SEC('sync')
            def _(sy):
                wall(sy, ["wdqw"])
                wdm_load(sy, 0)
                wdm_load(sy, 1)
            wdm_marks = {0: C["wdm0"], 1: C["wdm1"]}

            dmm_marks = {}
            r2spill = {0: 0, 1: 0}
            for dch in range(4):
                @SEC('tensor')
                def _(t, dch=dch):
                    t.wait_ge(S["qdT"], qdT_mark)
                    t.wait_ge(S["wdm%d" % (dch % 2)], wdm_marks[dch])
                    t.wait_ge(S["v"], C["v"])
                    for i in range(NIT):
                        for tt in range(NTT):
                            t.matmul(ps_d[:, tt * 512:(tt + 1) * 512],
                                     qdT[:, i, tt * 128:(tt + 1) * 128],
                                     wdm[:, dch % 2, i, :],
                                     start=i == 0, stop=i == NIT - 1)
                    inc(t.last, "mm", 1)
                d_mark = C["mm"]
                dmm_marks[dch] = C["mm"]
                if dch + 2 < 4:
                    @SEC('sync')
                    def _(sy, dch=dch):
                        sy.wait_ge(S["mm"], dmm_marks[dch])
                        wdm_marks[dch + 2] = wdm_load(sy, dch + 2)

                for tt in range(NTT):
                    bk = tt % 2
                    sxc = "sx2c%d" % bk

                    @SEC('sync')
                    def _(sy, dch=dch, tt=tt, bk=bk, sxc=sxc):
                        sy.wait_ge(S["v"], C["v"])
                        wall(sy, ["x2w0", "x2w1"])
                        inc(sy.dma_start(
                            out=x2c[:, bk, :],
                            in_=d_x2[tt * 128:(tt + 1) * 128,
                                     dch * 512:(dch + 1) * 512]), sxc, 16)
                    x2c_mark = C[sxc]

                    @SEC('vector')
                    def _(v, tt=tt, bk=bk, d_mark=d_mark, x2c_mark=x2c_mark,
                          sxc=sxc):
                        v.wait_ge(S["mm"], d_mark)
                        v.wait_ge(S[sxc], x2c_mark)
                        v.wait_ge(S["r2w%d" % bk], r2spill[bk])
                        v.tensor_add(r2c[:, bk, :],
                                     ps_d[:, tt * 512:(tt + 1) * 512],
                                     x2c[:, bk, :])
                        inc(v.last, "v", 1)
                    r2c_mark = C["v"]

                    @SEC('gpsimd')
                    def _(g, dch=dch, tt=tt, bk=bk, r2c_mark=r2c_mark):
                        g.wait_ge(S["v"], r2c_mark)
                        inc(g.dma_start(
                            out=d_r2[tt * 128:(tt + 1) * 128,
                                     dch * 512:(dch + 1) * 512],
                            in_=r2c[:, bk, :]), "r2w%d" % bk, 16)
                    r2spill[bk] = C["r2w%d" % bk]
            r2w_done = (C["r2w0"], C["r2w1"])

            ocons = {0: 0, 1: 0}
            ospill = {0: 0, 1: 0}
            for tt in range(NTT):
                bk = tt % 2
                r2r = "r2r%d" % bk

                @SEC('sync')
                def _(sy, tt=tt, bk=bk, r2r=r2r):
                    sy.wait_ge(S["r2w0"], r2w_done[0])
                    sy.wait_ge(S["r2w1"], r2w_done[1])
                    sy.wait_ge(S["v"], ocons[0])
                    inc(sy.dma_start(out=r2t[:, 0, :],
                                     in_=d_r2[tt * 128:(tt + 1) * 128, :]),
                        r2r, 16)
                r2t_mark = C[r2r]

                @SEC('scalar')
                def _(s, bk=bk, r2t_mark=r2t_mark, r2r=r2r):
                    s.wait_ge(S[r2r], r2t_mark)
                    s.wait_ge(S["v"], C["v"])
                    s.wait_ge(S["outw"], ospill[0])
                    s.activation(otb[:, 0, :], r2t[:, 0, :], AF.Square,
                                 accum_out=cols[:, 40:41])
                    inc(s.last, "sc", 1)
                sq_mark = C["sc"]

                @SEC('vector')
                def _(v, bk=bk, sq_mark=sq_mark):
                    v.wait_ge(S["sc"], sq_mark)
                    v.reduce_sum(cols[:, 41:42], r2t[:, 0, :], AX)
                    vlink(v, "v")
                    v.tensor_scalar(cols[:, 42:43], cols[:, 41:42], 1.0 / D,
                                    None, OP.mult)
                    v.tensor_scalar(cols[:, 43:44], cols[:, 40:41], 1.0 / D,
                                    None, OP.mult)
                    vlink(v, "v")
                    v.tensor_mul(cols[:, 44:45], cols[:, 42:43],
                                 cols[:, 42:43])
                    vlink(v, "v")
                    v.tensor_sub(cols[:, 45:46], cols[:, 43:44],
                                 cols[:, 44:45])
                    vlink(v, "v")
                    v.tensor_scalar(cols[:, 45:46], cols[:, 45:46], 1.0, EPS,
                                    OP.mult, OP.add)
                    inc(v.last, "v", 1)
                var2_mark = C["v"]

                @SEC('scalar')
                def _(s, var2_mark=var2_mark):
                    s.wait_ge(S["v"], var2_mark)
                    s.activation(cols[:, 46:47], cols[:, 45:46], AF.Ln)
                    inc(s.last, "sc", 1)
                ln2_mark = C["sc"]

                @SEC('vector')
                def _(v, ln2_mark=ln2_mark):
                    v.wait_ge(S["sc"], ln2_mark)
                    v.tensor_scalar(cols[:, 46:47], cols[:, 46:47], -0.5,
                                    None, OP.mult)
                    inc(v.last, "v", 1)
                lh2_mark = C["v"]

                @SEC('scalar')
                def _(s, lh2_mark=lh2_mark):
                    s.wait_ge(S["v"], lh2_mark)
                    s.activation(cols[:, 46:47], cols[:, 46:47], AF.Exp)
                    inc(s.last, "sc", 1)
                rstd2_mark = C["sc"]

                @SEC('vector')
                def _(v, rstd2_mark=rstd2_mark):
                    v.wait_ge(S["sc"], rstd2_mark)
                    v.tensor_mul(cols[:, 47:48], cols[:, 42:43],
                                 cols[:, 46:47])
                    vlink(v, "v")
                    v.tensor_scalar(cols[:, 47:48], cols[:, 47:48], -1.0,
                                    None, OP.mult)
                    inc(v.last, "v", 1)
                nb2_mark = C["v"]

                @SEC('vector')
                def _(v, bk=bk, nb2_mark=nb2_mark):
                    v.wait_ge(S["v"], nb2_mark)
                    v.wait_ge(S["outw"], ospill[0])
                    v.tensor_scalar(otb[:, 0, :], r2t[:, 0, :],
                                    cols[:, 46:47], cols[:, 47:48],
                                    OP.mult, OP.add)
                    inc(v.last, "v", 1)
                ot_mark = C["v"]
                ocons[0] = C["v"]

                @SEC('gpsimd')
                def _(g, tt=tt, bk=bk, ot_mark=ot_mark):
                    g.wait_ge(S["v"], ot_mark)
                    inc(g.dma_start(out=d_out[tt * 128:(tt + 1) * 128, :],
                                    in_=otb[:, 0, :]), "outw", 16)
                ospill[0] = C["outw"]

            @SEC('gpsimd')
            def _(g):
                g.wait_ge(S["outw"], C["outw"])
                for dbp, srcp in ((db_er1, d_r1), (db_ex2, d_x2),
                                  (db_er2, d_r2)):
                    if dbp is not None:
                        inc(g.dma_start(out=dbp, in_=srcp), "sdbg", 16)
                if C["sdbg"]:
                    wall(g, ["sdbg"])

    return nc


def kernel(**inputs):
    if "main" not in _CACHE:
        _CACHE["main"] = build()
    nc = _CACHE["main"]
    s = _stage_shared(inputs)
    in_maps = [_stage_core(s, c) for c in range(8)]
    res = run_bass_kernel_spmd(nc, in_maps, core_ids=list(range(8)))
    out = np.zeros((2, 4096, 2048), np.float32)
    for c in range(8):
        out[c // 4, 1024 * (c % 4):1024 * (c % 4 + 1)] = res.results[c]["out"]
    return out
